# version 16
# baseline (speedup 1.0000x reference)
import numpy as np
import concourse.bass as bass
import concourse.mybir as mybir
from concourse.alu_op_type import AluOpType
from concourse.bass_utils import run_bass_kernel_spmd

F32 = mybir.dt.float32
BF16 = mybir.dt.bfloat16
I32 = mybir.dt.int32
ACT = mybir.ActivationFunctionType
AX = mybir.AxisListType.X

D = 1024
T = 2048
NSEQ = 2
NTOK = NSEQ * T
NH = 16
HD = 64
DFF = 2816
DMLP = 4096
NCHUNK = NTOK // 128
NIC = DFF // 128
QMAX = 127.0
NCORES = 8

ENGINES = ("sync", "tensor", "vector", "scalar", "gpsimd")



def _perm_q():
    perm = np.zeros(D, dtype=np.int64)
    for half in range(2):
        for j in range(4):
            for g in range(4):
                for i in range(32):
                    perm[half * 512 + j * 128 + g * 32 + i] = (4 * g + j) * 64 + half * 32 + i
    return perm


def _perm_k():
    perm = np.zeros(256, dtype=np.int64)
    for half in range(2):
        for g in range(4):
            for i in range(32):
                perm[half * 128 + g * 32 + i] = g * 64 + half * 32 + i
    return perm


def host_prep(inputs):
    import ml_dtypes
    d = {}
    wq = np.asarray(inputs["L0_attn_Wq"], np.float32)
    wk = np.asarray(inputs["L0_attn_Wk"], np.float32)
    wv = np.asarray(inputs["L0_attn_Wv"], np.float32)
    d["wqT"] = np.ascontiguousarray(wq[_perm_q(), :].T)
    d["wkT"] = np.ascontiguousarray(wk[_perm_k(), :].T)
    d["wvT"] = np.ascontiguousarray(wv.T)
    d["w1T"] = np.ascontiguousarray(np.asarray(inputs["L0_attn_W1"], np.float32).T)
    d["w2T"] = np.ascontiguousarray(np.asarray(inputs["L0_attn_W2"], np.float32).T)
    d["wg_n"] = np.asarray(inputs["L0_ffn_w_gate"], np.float32)
    d["wu_n"] = np.asarray(inputs["L0_ffn_w_up"], np.float32)
    d["wd_n"] = np.asarray(inputs["L0_ffn_w_down"], np.float32)
    d["wgT"] = np.ascontiguousarray(d["wg_n"].T)
    d["wuT"] = np.ascontiguousarray(d["wu_n"].T)
    d["wdT"] = np.ascontiguousarray(d["wd_n"].T)
    d["clip_x"] = np.asarray(inputs["L0_ffn_clip_x"], np.float32).reshape(1, 1)
    d["clip_down"] = np.asarray(inputs["L0_ffn_clip_down"], np.float32).reshape(1, 1)

    inv = 1.0 / 10000.0 ** (np.arange(0, HD, 2, dtype=np.float64) / HD)
    fr = np.outer(inv, np.arange(T, dtype=np.float64))
    d["cos_t"] = np.tile(np.cos(fr), (4, 1)).astype(np.float32)
    d["sin_t"] = np.tile(np.sin(fr), (4, 1)).astype(np.float32)

    q = np.arange(128)[:, None]
    kk = np.arange(256)[None, :]
    NEG = np.float32(-1e30)
    allow_main = np.where(kk < 128, kk >= q + 1, (kk - 128) <= q)
    allow_first = np.where(kk < 128, False, (kk - 128) <= q)
    d["mask_main"] = np.where(allow_main, np.float32(0), NEG).astype(ml_dtypes.bfloat16)
    d["mask_first"] = np.where(allow_first, np.float32(0), NEG).astype(ml_dtypes.bfloat16)
    d["ident_bf"] = np.eye(128, dtype=ml_dtypes.bfloat16)
    d["ident_f32"] = np.eye(128, dtype=np.float32)
    return d



class Prog:
    def __init__(self, nc):
        self.nc = nc
        self.items = {e: [] for e in ENGINES}
        self.sems = {}
        self.cnt = {}

    def sem(self, name):
        h = self.nc.semaphore(name).__enter__()
        self.sems[name] = h
        self.cnt[name] = 0

    def emit(self, engine, fn, inc=None, inc_amt=1):
        if inc is not None:
            sem = self.sems[inc]
            self.cnt[inc] += inc_amt

            def wrapped(eng, fn=fn, sem=sem, amt=inc_amt):
                fn(eng).then_inc(sem, amt)
            self.items[engine].append(wrapped)
        else:
            self.items[engine].append(fn)
        return self.cnt[inc] if inc is not None else None

    def wait(self, engine, sem_name, value=None):
        if value is None:
            value = self.cnt[sem_name]
        sem = self.sems[sem_name]
        self.items[engine].append(lambda eng, sem=sem, v=value: eng.wait_ge(sem, v))

    def vnop(self):
        self.items["vector"].append(lambda eng: eng.nop())

    HSEM = {"sync": "hSs", "scalar": "hSa", "gpsimd": "hSg"}

    def dma_handoff(self, engine, out, in_, **kw):
        c = self.dma(engine, out, in_, **kw)
        qsem = {"sync": "dS", "scalar": "dA", "gpsimd": "dG"}[engine]
        hs = self.HSEM[engine]
        self.wait(engine, qsem, c)
        self.emit(engine, lambda e, hs=hs: e.sem_inc(self.sems[hs], 1), inc=None)
        return (hs, self._bump(hs))

    def hwait(self, engine, h):
        self.wait(engine, h[0], h[1])

    def queue_handshake(self, engine):
        qsem = {"sync": "dS", "scalar": "dA", "gpsimd": "dG"}[engine]
        hs = self.HSEM[engine]
        self.wait(engine, qsem, self.cnt[qsem])
        self.emit(engine, lambda e, hs=hs: e.sem_inc(self.sems[hs], 1), inc=None)
        return (hs, self._bump(hs))

    def _bump(self, name):
        self.cnt[name] += 1
        return self.cnt[name]

    def drain(self, engine, inc=None):
        if inc is not None:
            return self.emit(engine, lambda e: e.drain(), inc=inc)
        self.items[engine].append(lambda eng: eng.drain())
        return None

    def dma(self, engine, out, in_, sem=None, **kw):
        if sem is None:
            sem = {"sync": "dS", "scalar": "dA", "gpsimd": "dG"}[engine]
        return self.emit(engine, lambda e: e.dma_start(out=out, in_=in_, **kw),
                         inc=sem, inc_amt=16)

    def replay(self, block):
        @block.sync
        def _(eng):
            for fn in self.items["sync"]:
                fn(eng)

        @block.tensor
        def _(eng):
            for fn in self.items["tensor"]:
                fn(eng)

        @block.vector
        def _(eng):
            for fn in self.items["vector"]:
                fn(eng)

        @block.scalar
        def _(eng):
            for fn in self.items["scalar"]:
                fn(eng)

        @block.gpsimd
        def _(eng):
            for fn in self.items["gpsimd"]:
                fn(eng)



def build_nc(phases=6, debug=()):
    nc = bass.Bass("TRN2", target_bir_lowering=False)

    x = nc.dram_tensor("x", [NTOK, D], F32, kind="ExternalInput").ap()
    y = nc.dram_tensor("y", [NTOK, D], F32, kind="ExternalOutput").ap()
    din = {}
    for name, shape in [
        ("wqT", [D, D]), ("wkT", [D, 256]), ("wvT", [D, 256]),
        ("w1T", [D, DMLP]), ("w2T", [DMLP, D]),
        ("wg_n", [DFF, D]), ("wu_n", [DFF, D]), ("wd_n", [D, DFF]),
        ("wgT", [D, DFF]), ("wuT", [D, DFF]), ("wdT", [DFF, D]),
        ("clip_x", [1, 1]), ("clip_down", [1, 1]),
        ("cos_t", [128, T]), ("sin_t", [128, T]),
    ]:
        din[name] = nc.dram_tensor(name, shape, F32, kind="ExternalInput").ap()
    for name, shape in [("mask_main", [128, 256]), ("mask_first", [128, 256]),
                        ("ident_bf", [128, 128])]:
        din[name] = nc.dram_tensor(name, shape, BF16, kind="ExternalInput").ap()
    din["ident_f32"] = nc.dram_tensor("ident_f32", [128, 128], F32, kind="ExternalInput").ap()

    def scratch(name, shape, dt):
        kind = "ExternalOutput" if name in debug else "Internal"
        return nc.dram_tensor(name, shape, dt, kind=kind).ap()

    wg_park = scratch("wg_park", [D, DFF], BF16)
    wu_park = scratch("wu_park", [D, DFF], BF16)
    wd_park = scratch("wd_park", [DFF, D], BF16)
    g_spill = scratch("g_spill", [DFF, NTOK], F32)
    o_park = scratch("o_park", [D, NTOK], BF16)
    abuf = scratch("abuf", [D, NTOK], F32)
    scl_pack = nc.dram_tensor("scl_pack", [64, 128], F32).ap()
    cc1_in = nc.dram_tensor("cc1_in", [128, 128], F32).ap()
    cc1_out = nc.dram_tensor("cc1_out", [128, 128], F32).ap()
    cc2_in = nc.dram_tensor("cc2_in", [128, 128], F32).ap()
    cc2_out = nc.dram_tensor("cc2_out", [128, 128], F32).ap()
    dbg = {}
    for nm, shape, dt in [("smA", [128, 888], F32), ("smB", [128, 888], F32),
                          ("smC", [128, 888], F32),
                          ("cc1i", [128, 128], F32), ("cc1o", [128, 128], F32),
                          ("cc2i", [128, 128], F32), ("cc2o", [128, 128], F32),
                          ("recips", [128, DFF], F32), ("xnT", [128, 8 * NTOK], BF16), ("xq", [128, 8 * NTOK], BF16),
                          ("q", [128, 8 * T], BF16), ("k", [128, 2 * T], BF16),
                          ("v", [128, 16 * 256], BF16), ("smalls", [128, 888], F32)]:
        if nm in debug:
            dbg[nm] = nc.dram_tensor("dbg_" + nm, shape, dt, kind="ExternalOutput").ap()

    SB_BASE = 16512
    def sb(name, off, shape, dt):
        return nc.alloc_sbuf_tensor_at(name, shape, dt, offset=SB_BASE + off).ap()

    xnT = sb("xnT", 0, [128, 8 * NTOK], BF16)
    w1T_sb = sb("w1T_sb", 0, [128, 8 * DMLP], BF16)
    w1T_a = sb("w1T_a", 98304, [128, 4 * DMLP], BF16)
    wdT_sb = sb("wdT_sb", 0, [128, NIC * D], BF16)
    h_int = sb("h_int", 45056, [128, NIC * 512], BF16)
    h_int2 = sb("h_int2", 112640, [128, NIC * 512], BF16)
    qint6 = sb("qint6", 135168, [128, 512], I32)
    g_slab = [sb(f"g_slab{i}", 67584 + i * 2048, [128, 512], F32) for i in range(2)]
    ab_slab = [sb(f"ab_slab{i}", 71680 + i * 2048, [128, 512], F32) for i in range(2)]
    sum_sb = [sb(f"sum_sb{i}", 75776 + i * 2048, [128, 512], F32) for i in range(2)]
    out_sb = [sb(f"out_sb{i}", 79872 + i * 4096, [128, D], F32) for i in range(4)]
    x_sb = [sb(f"x_sb{i}", 96256 + i * 4096, [128, D], F32) for i in range(4)]

    w2T_sb = sb("w2T_sb", 65536, [128, 32 * D], BF16)
    wnat = [sb(f"wnat{i}", 65536 + i * 11264, [128, DFF], F32) for i in range(2)]
    wTb = [sb(f"wTb{i}", 88064 + i * 11264, [128, DFF], F32) for i in range(2)]
    wint32 = sb("wint32", 110592, [128, DFF], I32)
    wintbf = sb("wintbf", 121856, [128, DFF], BF16)
    recip_gb = sb("recip_gb", 127488, [128, DFF], F32)
    recip_ub = sb("recip_ub", 138752, [128, DFF], F32)
    recip_db = sb("recip_db", 150016, [128, D], F32)
    xbuf = [sb(f"xbuf{i}", 154112 + i * 4096, [128, D], F32) for i in range(2)]
    xn_tm = [sb(f"xn_tm{i}", 162304 + i * 2048, [128, D], BF16) for i in range(2)]
    q_rope = sb("q_rope", 65536, [128, 8 * T], BF16)
    k_rope = sb("k_rope", 98304, [128, 2 * T], BF16)
    v_tm = sb("v_tm", 106496, [128, 16 * 256], BF16)
    qraw = sb("qraw", 114688, [128, 10 * 512], F32)
    rope_t = [sb(f"rope_t{i}", 135168 + i * 2048, [128, 512], F32) for i in range(2)]
    wvT_sb = sb("wvT_sb", 150016, [128, 8 * 256], BF16)
    wqT_sb = sb("wqT_sb", 166400, [128, 8 * 1024], BF16)
    wkT_sb = sb("wkT_sb", 182784, [128, 8 * 256], BF16)
    sq_scr = sb("sq_scr", 186880, [128, D], F32)
    xq_int = sb("xq_int", 114688, [128, 4096], I32)
    probs_tm = sb("probs_tm", 114688, [128, NH * 256], BF16)
    probs_T = sb("probs_T", 122880, [128, NH * 256], BF16)
    o_sb = [sb(f"o_sb{i}", 131072 + i * 2048, [128, D], BF16) for i in range(2)]
    wgi_sb = [sb(f"wgi_sb{i}", 131072 + i * 2048, [128, 8 * 128], BF16) for i in range(2)]
    wui_sb = [sb(f"wui_sb{i}", 135168 + i * 2048, [128, 8 * 128], BF16) for i in range(2)]
    gate_f32 = sb("gate_f32", 139264, [128, 4096], F32)
    gup_f32 = sb("gup_f32", 155648, [128, 4096], F32)
    h1_sb = sb("h1_sb", 131072, [128, 32 * 512], BF16)
    o_sb2 = [sb(f"o_sb2{i}", 163840 + i * 8192, [128, 8 * 512], BF16) for i in range(2)]
    ab_evac = [sb(f"ab_evac{i}", 180224 + i * 2048, [128, 512], F32) for i in range(2)]

    PT = 191104
    cos_sb = sb("cos_sb", PT, [128, T], F32)
    sin_sb = sb("sin_sb", PT + 8192, [128, T], F32)
    mask_main = sb("mask_main_sb", PT + 16384, [128, 256], BF16)
    mask_first = sb("mask_first_sb", PT + 16896, [128, 256], BF16)
    ident_bf = sb("ident_bf_sb", PT + 17408, [128, 128], BF16)
    ident_f32 = sb("ident_f32_sb", PT + 17664, [128, 128], F32)
    S = sb("smalls", PT + 18176, [128, 888], F32)

    (C_GMAX1, C_MEAN, C_MSQ, C_VAR, C_RSTD, C_MX, C_MN, C_T0, C_T1, C_SUM,
     C_SQS, C_S1, C_RS1, C_SIGX, C_SIGD, C_GMAX2, C_S2, C_RS2, C_CLX,
     C_CLD) = range(20)
    C_MPACK, C_DPACK, C_RDPACK = 32, 48, 64
    C_SUMP, C_SQSP, C_MXP, C_MNP = 600, 632, 664, 696
    C_MEANP, C_MSQP, C_VARP, C_RSTDP, C_AMP = 728, 760, 792, 824, 856
    C_RPG, C_RPU, C_RPD = 96, 118, 140
    C_CBG, C_CBU, C_CBD = 192, 224, 256
    C_BCR = 384
    C_SWG, C_SWU, C_SWD = 512, 544, 576

    def Sc(col, n=1):
        return S[:, col:col + n]

    ps_a = nc.alloc_psum_tensor("ps_a", [128, 2048], F32).ap()
    ps_b = nc.alloc_psum_tensor("ps_b", [128, 2048], F32).ap()
    pst = ps_a.bitcast(BF16)[:, 0:1024]

    prog = Prog(nc)
    for s_ in ("dS", "dA", "dG", "vS", "aS", "tS", "gS", "hSs", "hSa", "hSg"):
        prog.sem(s_)
    E, W, DMA = prog.emit, prog.wait, prog.dma

    DMA("sync", cos_sb[:], din["cos_t"])
    DMA("sync", sin_sb[:], din["sin_t"])
    DMA("sync", mask_main[:], din["mask_main"])
    DMA("sync", mask_first[:], din["mask_first"])
    DMA("sync", ident_bf[:], din["ident_bf"])
    DMA("sync", ident_f32[:], din["ident_f32"])
    DMA("sync", Sc(C_CLX), bass.AP(din["clip_x"].tensor, 0, [[0, 128], [1, 1]]))
    idf_loaded = prog.cnt["dS"] + 16
    c0 = prog.dma_handoff("sync", Sc(C_CLD), bass.AP(din["clip_down"].tensor, 0, [[0, 128], [1, 1]]))
    prog.hwait("vector", c0)
    E("vector", lambda e: e.memset(Sc(C_GMAX1), 0.0))
    E("vector", lambda e: e.memset(Sc(C_GMAX2), 0.0), inc="vS")
    prog.hwait("scalar", c0)
    E("scalar", lambda e: e.activation(out=Sc(C_SIGX), in_=Sc(C_CLX), func=ACT.Sigmoid))
    sig_done = E("scalar", lambda e: e.activation(out=Sc(C_SIGD), in_=Sc(C_CLD), func=ACT.Sigmoid), inc="aS")

    for ifc in range(8):
        DMA("gpsimd", wqT_sb[:, ifc * 1024:(ifc + 1) * 1024], din["wqT"][ifc * 128:(ifc + 1) * 128, :])
    for ifc in range(7):
        DMA("gpsimd", wkT_sb[:, ifc * 256:(ifc + 1) * 256], din["wkT"][ifc * 128:(ifc + 1) * 128, :])
    wqk_loaded = prog.dma_handoff("gpsimd", wkT_sb[:, 7 * 256:8 * 256], din["wkT"][7 * 128:8 * 128, :])

    C_SWGN, C_SWUN, C_SWDN = 302, 324, 346
    wq_jobs = []
    for src, cS, cN, nchunk, width in ((din["wg_n"], C_SWG, C_SWGN, NIC, D),
                                       (din["wu_n"], C_SWU, C_SWUN, NIC, D),
                                       (din["wd_n"], C_SWD, C_SWDN, 8, DFF)):
        for c in range(nchunk):
            wq_jobs.append((src, cS + c, cN + c, c, width))
    nat_done = []
    for idx, (src, cc, ccn, c, width) in enumerate(wq_jobs):
        slot = wnat[idx % 2]
        if idx >= 2:
            W("sync", "vS", nat_done[idx - 2])
        cd = prog.dma_handoff("sync", slot[:, 0:width], src[c * 128:(c + 1) * 128, :])
        prog.hwait("vector", cd)
        nd = E("vector", lambda e, s_=slot, w=width, cc=cc: e.tensor_reduce(
            out=S[:, cc:cc + 1], in_=s_[:, 0:w], axis=AX, op=AluOpType.max,
            apply_absolute_value=True), inc="vS")
        nat_done.append(nd)
    prog.drain("vector")
    for base, n in ((C_SWG, NIC), (C_SWU, NIC), (C_SWD, 8)):
        E("vector", lambda e, b=base, n_=n: e.tensor_scalar(
            out=S[:, b:b + n_], in0=S[:, b:b + n_], scalar1=1.0 / QMAX,
            scalar2=None, op0=AluOpType.mult))
    prog.drain("vector")
    for dst, srcb, n in ((C_RPG, C_SWG, NIC), (C_RPU, C_SWU, NIC), (C_RPD, C_SWD, 8)):
        E("vector", lambda e, d=dst, s=srcb, n_=n: e.reciprocal(
            out=S[:, d:d + n_], in_=S[:, s:s + n_]))
    rp_done = prog.drain("vector", inc="vS")
    W("tensor", "vS", rp_done)
    W("tensor", "dS", idf_loaded)
    tp = E("tensor", lambda e: e.transpose(
        out=ps_b[0:64, 0:128], in_=S[:, C_RPG:C_RPG + 64], identity=ident_f32[:]), inc="tS")
    W("scalar", "tS", tp)
    E("scalar", lambda e: e.copy(
        out=S[0:64, C_BCR:C_BCR + 128], in_=ps_b[0:64, 0:128]))
    tpe = prog.drain("scalar", inc="aS")
    W("gpsimd", "aS", tpe)
    pk = DMA("gpsimd", scl_pack[0:64, :], S[0:64, C_BCR:C_BCR + 128])
    W("gpsimd", "dG", pk)
    DMA("gpsimd", recip_gb[:], bass.AP(scl_pack.tensor, 0, [[0, 128], [1, DFF]]))
    DMA("gpsimd", recip_ub[:], bass.AP(scl_pack.tensor, 22 * 128, [[0, 128], [1, DFF]]))
    bc_done = prog.dma_handoff("gpsimd", recip_db[:], bass.AP(scl_pack.tensor, 44 * 128, [[0, 128], [1, D]]))

    if "recips" in dbg:
        W("gpsimd", "dG", bc_done)
        DMA("gpsimd", dbg["recips"], recip_gb[:])
    wt_jobs = []
    for src, park, rb, nchunk, width in ((din["wgT"], wg_park, recip_gb, 8, DFF),
                                         (din["wuT"], wu_park, recip_ub, 8, DFF),
                                         (din["wdT"], wd_park, recip_db, NIC, D)):
        for c in range(nchunk):
            wt_jobs.append((src, park, rb, c, width))
    prog.hwait("sync", bc_done)
    t_done, park_done = [], []
    for idx, (src, park, rb, c, width) in enumerate(wt_jobs):
        slot = wTb[idx % 2]
        if idx >= 2:
            W("sync", "vS", t_done[idx - 2])
        cd = prog.dma_handoff("sync", slot[:, 0:width], src[c * 128:(c + 1) * 128, :])
        prog.hwait("vector", cd)
        if idx >= 1:
            W("vector", "dG", park_done[idx - 1])
        E("vector", lambda e, s=slot, r=rb, w=width: e.tensor_tensor(
            out=wint32[:, 0:w], in0=s[:, 0:w], in1=r[:, 0:w], op=AluOpType.mult))
        E("vector", lambda e, w=width: e.tensor_copy(
            out=wintbf[:, 0:w], in_=wint32[:, 0:w]))
        cv = prog.drain("vector", inc="vS")
        t_done.append(cv)
        W("gpsimd", "vS", cv)
        pd = DMA("gpsimd", park[c * 128:(c + 1) * 128, :], wintbf[:, 0:width])
        park_done.append(pd)

    pa_done = []
    pa_gs = []
    for kch in range(NCHUNK):
        slot = xbuf[kch % 2]
        if kch >= 2:
            W("sync", "vS", pa_done[kch - 2])
        cd = prog.dma_handoff("sync", slot[:], x[kch * 128:(kch + 1) * 128, :])
        prog.hwait("vector", cd)
        E("vector", lambda e, s=slot, k=kch: e.tensor_reduce(
            out=S[:, C_SUMP + k:C_SUMP + k + 1], in_=s[:], axis=AX, op=AluOpType.add))
        E("vector", lambda e, s=slot: e.tensor_tensor(
            out=sq_scr[:], in0=s[:], in1=s[:], op=AluOpType.mult))
        E("vector", lambda e, k=kch: e.tensor_reduce(
            out=S[:, C_SQSP + k:C_SQSP + k + 1], in_=sq_scr[:], axis=AX, op=AluOpType.add))
        E("vector", lambda e, s=slot, k=kch: e.tensor_reduce(
            out=S[:, C_MXP + k:C_MXP + k + 1], in_=s[:], axis=AX, op=AluOpType.max))
        nv = E("vector", lambda e, s=slot, k=kch: e.tensor_reduce(
            out=S[:, C_MNP + k:C_MNP + k + 1], in_=s[:], axis=AX, op=AluOpType.min), inc="vS")
        pa_done.append(nv)
        pa_gs.append(nv)
    DR = prog.drain
    E("vector", lambda e: e.tensor_scalar(
        out=S[:, C_MEANP:C_MEANP + 32], in0=S[:, C_SUMP:C_SUMP + 32],
        scalar1=1.0 / D, scalar2=None, op0=AluOpType.mult))
    E("vector", lambda e: e.tensor_scalar(
        out=S[:, C_MSQP:C_MSQP + 32], in0=S[:, C_SQSP:C_SQSP + 32],
        scalar1=1.0 / D, scalar2=None, op0=AluOpType.mult))
    DR("vector")
    E("vector", lambda e: e.tensor_tensor(
        out=S[:, C_VARP:C_VARP + 32], in0=S[:, C_MEANP:C_MEANP + 32],
        in1=S[:, C_MEANP:C_MEANP + 32], op=AluOpType.mult))
    DR("vector")
    E("vector", lambda e: e.tensor_tensor(
        out=S[:, C_VARP:C_VARP + 32], in0=S[:, C_MSQP:C_MSQP + 32],
        in1=S[:, C_VARP:C_VARP + 32], op=AluOpType.subtract))
    DR("vector")
    vv = DR("vector", inc="vS")
    E("vector", lambda e: e.tensor_scalar(
        out=S[:, C_VARP:C_VARP + 32], in0=S[:, C_VARP:C_VARP + 32],
        scalar1=1e-5, scalar2=None, op0=AluOpType.add))
    vv = DR("vector", inc="vS")
    W("scalar", "vS", vv)
    E("scalar", lambda e: e.activation(
        out=S[:, C_VARP:C_VARP + 32], in_=S[:, C_VARP:C_VARP + 32], func=ACT.Sqrt))
    sv = DR("scalar", inc="aS")
    W("vector", "aS", sv)
    E("vector", lambda e: e.reciprocal(
        out=S[:, C_RSTDP:C_RSTDP + 32], in_=S[:, C_VARP:C_VARP + 32]))
    DR("vector")
    E("vector", lambda e: e.tensor_tensor(
        out=S[:, C_MXP:C_MXP + 32], in0=S[:, C_MXP:C_MXP + 32],
        in1=S[:, C_MEANP:C_MEANP + 32], op=AluOpType.subtract))
    E("vector", lambda e: e.tensor_tensor(
        out=S[:, C_MNP:C_MNP + 32], in0=S[:, C_MEANP:C_MEANP + 32],
        in1=S[:, C_MNP:C_MNP + 32], op=AluOpType.subtract))
    DR("vector")
    E("vector", lambda e: e.tensor_tensor(
        out=S[:, C_AMP:C_AMP + 32], in0=S[:, C_MXP:C_MXP + 32],
        in1=S[:, C_MNP:C_MNP + 32], op=AluOpType.max))
    DR("vector")
    E("vector", lambda e: e.tensor_tensor(
        out=S[:, C_AMP:C_AMP + 32], in0=S[:, C_AMP:C_AMP + 32],
        in1=S[:, C_RSTDP:C_RSTDP + 32], op=AluOpType.mult))
    DR("vector")
    E("vector", lambda e: e.tensor_reduce(
        out=Sc(C_GMAX1), in_=S[:, C_AMP:C_AMP + 32], axis=AX, op=AluOpType.max))
    stats_done = DR("vector", inc="vS")

    W("sync", "vS", stats_done)
    pb_done = []
    tp_done = []
    ev_done = 0
    for kch in range(NCHUNK):
        slot = xbuf[kch % 2]
        nslot = xn_tm[kch % 2]
        if kch >= 2:
            W("sync", "vS", pb_done[kch - 2])
        cd = prog.dma_handoff("sync", slot[:], x[kch * 128:(kch + 1) * 128, :])
        prog.hwait("vector", cd)
        if kch == 0:
            W("vector", "vS", stats_done)
        if kch >= 2:
            W("vector", "tS", tp_done[kch - 2])
        nv = E("vector", lambda e, s=slot, n=nslot, k=kch: e.tensor_scalar(
            out=n[:], in0=s[:], scalar1=S[:, C_MEANP + k:C_MEANP + k + 1],
            scalar2=S[:, C_RSTDP + k:C_RSTDP + k + 1],
            op0=AluOpType.subtract, op1=AluOpType.mult), inc="vS")
        pb_done.append(nv)
        W("tensor", "vS", nv)
        if kch >= 1:
            W("tensor", "aS", ev_done)
        for t8 in range(8):
            E("tensor", lambda e, n=nslot, t=t8: e.transpose(
                out=pst[:, t * 128:(t + 1) * 128], in_=n[:, t * 128:(t + 1) * 128],
                identity=ident_bf[:]), inc="tS")
        tp_done.append(prog.cnt["tS"])
        W("scalar", "tS", prog.cnt["tS"])
        for t8 in range(8):
            E("scalar", lambda e, t=t8, k=kch: e.copy(
                out=xnT[:, t * NTOK + k * 128: t * NTOK + (k + 1) * 128],
                in_=pst[:, t * 128:(t + 1) * 128]), inc="aS")
        ev_done = prog.cnt["aS"]

    prog.drain("scalar", inc="aS")
    p1_aS = prog.cnt["aS"]
    p1_vS = prog.cnt["vS"]

    if "xnT" in dbg:
        W("gpsimd", "aS", p1_aS)
        DMA("gpsimd", dbg["xnT"], xnT[:])

    W("gpsimd", "vS", p1_vS)
    c1 = DMA("gpsimd", cc1_in[0:1, 0:128], Sc(C_GMAX1))
    W("gpsimd", "dG", c1)
    ccd = E("gpsimd", lambda e: e.collective_compute(
        "AllReduce", AluOpType.max, replica_groups=[list(range(NCORES))],
        ins=[cc1_in[0:1, 0:128].opt()], outs=[cc1_out[0:1, 0:128].opt()]), inc="gS")
    W("gpsimd", "gS", ccd)
    if "cc1i" in dbg:
        DMA("gpsimd", dbg["cc1i"], cc1_in)
        DMA("gpsimd", dbg["cc1o"], cc1_out)
    c2 = prog.dma_handoff("gpsimd", S[:, C_BCR:C_BCR + 128],
                          bass.AP(cc1_out.tensor, 0, [[0, 128], [1, 128]]))
    prog.hwait("vector", c2)
    E("vector", lambda e: e.tensor_reduce(
        out=Sc(C_S1), in_=S[:, C_BCR:C_BCR + 128], axis=AX, op=AluOpType.max))
    W("vector", "aS", sig_done)
    prog.drain("vector")
    E("vector", lambda e: e.tensor_tensor(
        out=Sc(C_S1), in0=Sc(C_S1), in1=Sc(C_SIGX), op=AluOpType.mult))
    prog.drain("vector")
    E("vector", lambda e: e.tensor_scalar(
        out=Sc(C_S1), in0=Sc(C_S1), scalar1=1.0 / QMAX, scalar2=None, op0=AluOpType.mult))
    prog.drain("vector")
    E("vector", lambda e: e.reciprocal(out=Sc(C_RS1), in_=Sc(C_S1)))
    s1_done = prog.drain("vector", inc="vS")
    E("vector", lambda e: e.tensor_scalar(
        out=S[:, C_CBG:C_CBG + NIC], in0=S[:, C_SWG:C_SWG + NIC],
        scalar1=Sc(C_S1), scalar2=None, op0=AluOpType.mult))
    E("vector", lambda e: e.tensor_scalar(
        out=S[:, C_CBU:C_CBU + NIC], in0=S[:, C_SWU:C_SWU + NIC],
        scalar1=Sc(C_S1), scalar2=None, op0=AluOpType.mult))
    combo_done = prog.drain("vector", inc="vS")

    def snap(tag):
        if tag in dbg:
            W("gpsimd", "vS", prog.cnt["vS"])
            W("gpsimd", "aS", prog.cnt["aS"])
            DMA("gpsimd", dbg[tag], S[:])

    def finish():
        if "smalls" in dbg:
            W("gpsimd", "vS", prog.cnt["vS"])
            W("gpsimd", "aS", prog.cnt["aS"])
            DMA("gpsimd", dbg["smalls"], S[:])
        if phases < 6:
            for k in range(NCHUNK):
                DMA("gpsimd", y[k * 128:(k + 1) * 128, :], x[k * 128:(k + 1) * 128, :])
        W("gpsimd", "dG", prog.cnt["dG"])
        W("gpsimd", "dS", prog.cnt["dS"])
        W("gpsimd", "dA", prog.cnt["dA"])
        with nc.Block() as block:
            prog.replay(block)

    snap("smA")
    if phases < 2:
        finish()
        return nc

    W("gpsimd", "vS", t_done[-1])
    for ifc in range(7):
        DMA("gpsimd", wvT_sb[:, ifc * 256:(ifc + 1) * 256], din["wvT"][ifc * 128:(ifc + 1) * 128, :])
    wv_loaded = prog.dma_handoff("gpsimd", wvT_sb[:, 7 * 256:8 * 256], din["wvT"][7 * 128:8 * 128, :])

    prog.hwait("tensor", max(wqk_loaded, wv_loaded, key=lambda t: t[1]))
    W("tensor", "aS", p1_aS)
    ps_q = [ps_a[:, i * 512:(i + 1) * 512] for i in range(4)]
    qev_cnt = []
    vev_cnt = []
    attn_st = {"o_dma": [0, 0], "stats": [], "red": [], "exp": [], "norm": [],
               "tp": [], "ev": [], "av": [], "oev": [], "last_oev": 0,
               "last_stats": 0}

    rope_tt_done = 0
    for s in range(NSEQ):
        if s > 0:
            W("vector", "tS", prog.cnt["tS"])
            W("scalar", "tS", prog.cnt["tS"])
            W("tensor", "aS", prog.cnt["aS"])
        for tt in range(4):
            for oc in range(10):
                pslot = ps_q[len(qev_cnt) % 4]
                if len(qev_cnt) >= 4:
                    W("tensor", "aS", qev_cnt[-4])
                for ifc in range(8):
                    if oc < 8:
                        lhs = wqT_sb[:, ifc * 1024 + oc * 128: ifc * 1024 + (oc + 1) * 128]
                    else:
                        lhs = wkT_sb[:, ifc * 256 + (oc - 8) * 128: ifc * 256 + (oc - 7) * 128]
                    E("tensor", lambda e, l=lhs, sl=pslot, i=ifc, s_=s, t_=tt: e.matmul(
                        out=sl[:], lhsT=l,
                        rhs=xnT[:, i * NTOK + s_ * T + t_ * 512: i * NTOK + s_ * T + (t_ + 1) * 512],
                        start=(i == 0), stop=(i == 7)), inc="tS" if ifc == 7 else None)
                W("scalar", "tS", prog.cnt["tS"])
                if oc == 0:
                    W("scalar", "vS", rope_tt_done)
                ec = E("scalar", lambda e, sl=pslot, o=oc: e.copy(
                    out=qraw[:, o * 512:(o + 1) * 512], in_=sl[:]), inc="aS")
                qev_cnt.append(ec)
            prog.drain("scalar", inc="aS")
            W("vector", "aS", prog.cnt["aS"])
            cosT = cos_sb[:, tt * 512:(tt + 1) * 512]
            sinT = sin_sb[:, tt * 512:(tt + 1) * 512]
            for j in range(5):
                if j < 4:
                    x1 = qraw[:, j * 512:(j + 1) * 512]
                    x2 = qraw[:, (4 + j) * 512:(5 + j) * 512]
                    d1 = q_rope[:, j * T + tt * 512: j * T + (tt + 1) * 512]
                    d2 = q_rope[:, (4 + j) * T + tt * 512: (4 + j) * T + (tt + 1) * 512]
                else:
                    x1 = qraw[:, 8 * 512:9 * 512]
                    x2 = qraw[:, 9 * 512:10 * 512]
                    d1 = k_rope[:, tt * 512:(tt + 1) * 512]
                    d2 = k_rope[:, T + tt * 512: T + (tt + 1) * 512]
                sc = 0.125 if j < 4 else 1.0
                E("vector", lambda e, a=x1, c=cosT, sc=sc: e.scalar_tensor_tensor(
                    out=rope_t[0][:], in0=a, scalar=sc, in1=c,
                    op0=AluOpType.mult, op1=AluOpType.mult))
                E("vector", lambda e, b=x2, sn=sinT, sc=sc: e.scalar_tensor_tensor(
                    out=rope_t[1][:], in0=b, scalar=sc, in1=sn,
                    op0=AluOpType.mult, op1=AluOpType.mult))
                prog.drain("vector")
                E("vector", lambda e, d=d1: e.tensor_tensor(
                    out=d, in0=rope_t[0][:], in1=rope_t[1][:], op=AluOpType.subtract))
                E("vector", lambda e, a=x1, sn=sinT, sc=sc: e.scalar_tensor_tensor(
                    out=rope_t[0][:], in0=a, scalar=sc, in1=sn,
                    op0=AluOpType.mult, op1=AluOpType.mult))
                E("vector", lambda e, b=x2, c=cosT, sc=sc: e.scalar_tensor_tensor(
                    out=rope_t[1][:], in0=b, scalar=sc, in1=c,
                    op0=AluOpType.mult, op1=AluOpType.mult))
                prog.drain("vector")
                E("vector", lambda e, d=d2: e.tensor_tensor(
                    out=d, in0=rope_t[0][:], in1=rope_t[1][:], op=AluOpType.add),
                  inc="vS" if j == 4 else None)
            rope_tt_done = prog.cnt["vS"]
        rope_done = prog.cnt["vS"]
        for tc in range(16):
            vslot = ps_b[:, (tc % 3) * 512:(tc % 3) * 512 + 256]
            if tc >= 3:
                W("tensor", "aS", vev_cnt[-3])
            for ifc in range(8):
                E("tensor", lambda e, sl=vslot, i=ifc, s_=s, t_=tc: e.matmul(
                    out=sl[:],
                    lhsT=xnT[:, i * NTOK + s_ * T + t_ * 128: i * NTOK + s_ * T + t_ * 128 + 128],
                    rhs=wvT_sb[:, i * 256:(i + 1) * 256],
                    start=(i == 0), stop=(i == 7)), inc="tS" if ifc == 7 else None)
            W("scalar", "tS", prog.cnt["tS"])
            ec = E("scalar", lambda e, sl=vslot, t_=tc: e.copy(
                out=v_tm[:, t_ * 256:(t_ + 1) * 256], in_=sl[:]), inc="aS")
            vev_cnt.append(ec)
        v_done = prog.drain("scalar", inc="aS")

        if phases >= 3:
            _emit_attention((prog, s, q_rope, k_rope, v_tm, probs_tm, probs_T,
                             o_sb, o_park, mask_main, mask_first, ident_bf,
                             ps_a, ps_b, S, C_MPACK, C_DPACK, C_RDPACK,
                             rope_done, v_done, attn_st))

    p23_tS = prog.cnt["tS"]
    p23_vS = prog.cnt["vS"]
    p23_aS = prog.cnt["aS"]
    if "q" in dbg:
        W("gpsimd", "vS", p23_vS)
        DMA("gpsimd", dbg["q"], q_rope[:])
        DMA("gpsimd", dbg["k"], k_rope[:])
        W("gpsimd", "aS", p23_aS)
        DMA("gpsimd", dbg["v"], v_tm[:])

    snap("smB")
    W("vector", "vS", s1_done)
    W("vector", "tS", p23_tS)
    W("sync", "tS", p23_tS)
    q_ev = 0
    q_tp = []
    q_nv = []
    for kch in range(NCHUNK):
        slot = xbuf[kch % 2]
        nslot = xn_tm[kch % 2]
        if kch >= 2:
            W("sync", "vS", q_nv[kch - 2])
        cd = prog.dma_handoff("sync", slot[:], x[kch * 128:(kch + 1) * 128, :])
        prog.hwait("vector", cd)
        if kch >= 2:
            W("vector", "tS", q_tp[kch - 2])
        E("vector", lambda e, s_=slot, k=kch: e.tensor_scalar(
            out=s_[:], in0=s_[:], scalar1=S[:, C_MEANP + k:C_MEANP + k + 1],
            scalar2=S[:, C_RSTDP + k:C_RSTDP + k + 1],
            op0=AluOpType.subtract, op1=AluOpType.mult))
        E("vector", lambda e, s_=slot: e.tensor_scalar(
            out=xq_int[:, 0:D], in0=s_[:], scalar1=Sc(C_RS1), scalar2=None,
            op0=AluOpType.mult))
        E("vector", lambda e: e.tensor_scalar(
            out=xq_int[:, 0:D], in0=xq_int[:, 0:D], scalar1=-128.0, scalar2=127.0,
            op0=AluOpType.max, op1=AluOpType.min))
        nv = E("vector", lambda e, n=nslot: e.tensor_copy(
            out=n[:], in_=xq_int[:, 0:D]), inc="vS")
        q_nv.append(nv)
        W("tensor", "vS", nv)
        if kch >= 1:
            W("tensor", "aS", q_ev)
        for t8 in range(8):
            E("tensor", lambda e, n=nslot, t=t8: e.transpose(
                out=pst[:, t * 128:(t + 1) * 128], in_=n[:, t * 128:(t + 1) * 128],
                identity=ident_bf[:]), inc="tS")
        q_tp.append(prog.cnt["tS"])
        W("scalar", "tS", prog.cnt["tS"])
        for t8 in range(8):
            E("scalar", lambda e, t=t8, k=kch: e.copy(
                out=xnT[:, t * NTOK + k * 128: t * NTOK + (k + 1) * 128],
                in_=pst[:, t * 128:(t + 1) * 128]), inc="aS")
        q_ev = prog.cnt["aS"]
    xq_done = prog.drain("vector", inc="vS")
    xq_evac_done = prog.drain("scalar", inc="aS")
    if "xq" in dbg:
        W("gpsimd", "vS", xq_done)
        DMA("gpsimd", dbg["xq"], xnT[:])

    if phases < 5:
        finish()
        return nc

    C_GM2P = 280
    p3_tS = prog.cnt["tS"]
    p3_aS = prog.cnt["aS"]
    wg_slab_done = []
    spill_done = [0, 0]
    stt_u = []
    o_dma_last = max(attn_st["o_dma"])
    for ic in range(NIC):
        sl = ic % 2
        if ic == 0:
            W("gpsimd", "dG", o_dma_last)
        if ic >= 2:
            W("gpsimd", "tS", wg_slab_done[ic - 2])
        lg = DMA("gpsimd", wgi_sb[sl][:],
                 bass.AP(wg_park.tensor, ic * 128, [[DFF, 128], [128 * DFF, 8], [1, 128]]))
        lu = prog.dma_handoff("gpsimd", wui_sb[sl][:],
                 bass.AP(wu_park.tensor, ic * 128, [[DFF, 128], [128 * DFF, 8], [1, 128]]))
        prog.hwait("tensor", lu)
        if ic == 0:
            W("tensor", "vS", xq_done)
            W("tensor", "aS", xq_evac_done)
        for tt in range(8):
            pg = ps_a[:, (tt % 2) * 512:(tt % 2) * 512 + 512]
            pu = ps_a[:, 1024 + (tt % 2) * 512:1024 + (tt % 2) * 512 + 512]
            if len(stt_u) >= 2:
                W("tensor", "vS", stt_u[-2])
            for ifc in range(8):
                E("tensor", lambda e, p=pg, w=wgi_sb[sl], i=ifc, t=tt: e.matmul(
                    out=p[:], lhsT=w[:, i * 128:(i + 1) * 128],
                    rhs=xnT[:, i * NTOK + t * 512:i * NTOK + (t + 1) * 512],
                    start=(i == 0), stop=(i == 7)), inc="tS" if ifc == 7 else None)
            gm = prog.cnt["tS"]
            for ifc in range(8):
                E("tensor", lambda e, p=pu, w=wui_sb[sl], i=ifc, t=tt: e.matmul(
                    out=p[:], lhsT=w[:, i * 128:(i + 1) * 128],
                    rhs=xnT[:, i * NTOK + t * 512:i * NTOK + (t + 1) * 512],
                    start=(i == 0), stop=(i == 7)), inc="tS" if ifc == 7 else None)
            um = prog.cnt["tS"]
            W("vector", "tS", gm)
            if ic >= 1 and tt == 0:
                W("vector", "dG", spill_done[(ic + 1) % 2])
            E("vector", lambda e, p=pg, t=tt, ic=ic: e.tensor_scalar(
                out=gate_f32[:, t * 512:(t + 1) * 512], in0=p[:],
                scalar1=S[:, C_CBG + ic:C_CBG + ic + 1], scalar2=None,
                op0=AluOpType.mult))
            ev = prog.drain("vector", inc="vS")
            W("scalar", "vS", ev)
            E("scalar", lambda e, t=tt: e.activation(
                out=gup_f32[:, t * 512:(t + 1) * 512],
                in_=gate_f32[:, t * 512:(t + 1) * 512], func=ACT.Silu))
            sv2 = prog.drain("scalar", inc="aS")
            W("vector", "aS", sv2)
            W("vector", "tS", um)
            sg = E("vector", lambda e, p=pu, t=tt, ic=ic: e.scalar_tensor_tensor(
                out=gate_f32[:, t * 512:(t + 1) * 512], in0=p[:],
                scalar=S[:, C_CBU + ic:C_CBU + ic + 1],
                in1=gup_f32[:, t * 512:(t + 1) * 512],
                op0=AluOpType.mult, op1=AluOpType.mult), inc="vS")
            stt_u.append(sg)
        wg_slab_done.append(prog.cnt["tS"])
        prog.drain("vector")
        am = E("vector", lambda e, ic=ic: e.tensor_reduce(
            out=S[:, C_GM2P + ic:C_GM2P + ic + 1], in_=gate_f32[:], axis=AX,
            op=AluOpType.max, apply_absolute_value=True), inc="vS")
        W("gpsimd", "vS", am)
        if ic == NIC - 1:
            spill_h = prog.dma_handoff("gpsimd", g_spill[ic * 128:(ic + 1) * 128, :], gate_f32[:])
            spill_done[ic % 2] = prog.cnt["dG"]
        else:
            spill_done[ic % 2] = DMA("gpsimd", g_spill[ic * 128:(ic + 1) * 128, :], gate_f32[:])

    snap("smC")
    p5_vflush = prog.drain("vector", inc="vS")
    prog.drain("vector")
    E("vector", lambda e: e.tensor_reduce(
        out=Sc(C_GMAX2), in_=S[:, C_GM2P:C_GM2P + NIC], axis=AX, op=AluOpType.max))
    g2 = prog.drain("vector", inc="vS")
    W("gpsimd", "vS", g2)
    cg1 = DMA("gpsimd", cc2_in[0:1, 0:128], Sc(C_GMAX2))
    W("gpsimd", "dG", cg1)
    ccd2 = E("gpsimd", lambda e: e.collective_compute(
        "AllReduce", AluOpType.max, replica_groups=[list(range(NCORES))],
        ins=[cc2_in[0:1, 0:128].opt()], outs=[cc2_out[0:1, 0:128].opt()]), inc="gS")
    W("gpsimd", "gS", ccd2)
    cg2 = prog.dma_handoff("gpsimd", S[:, C_BCR:C_BCR + 128],
                           bass.AP(cc2_out.tensor, 0, [[0, 128], [1, 128]]))
    if "cc2i" in dbg:
        DMA("gpsimd", dbg["cc2i"], cc2_in)
        DMA("gpsimd", dbg["cc2o"], cc2_out)
    prog.hwait("vector", cg2)
    E("vector", lambda e: e.tensor_reduce(
        out=Sc(C_S2), in_=S[:, C_BCR:C_BCR + 128], axis=AX, op=AluOpType.max))
    prog.drain("vector")
    E("vector", lambda e: e.tensor_tensor(
        out=Sc(C_S2), in0=Sc(C_S2), in1=Sc(C_SIGD), op=AluOpType.mult))
    prog.drain("vector")
    E("vector", lambda e: e.tensor_scalar(
        out=Sc(C_S2), in0=Sc(C_S2), scalar1=1.0 / QMAX, scalar2=None, op0=AluOpType.mult))
    prog.drain("vector")
    E("vector", lambda e: e.reciprocal(out=Sc(C_RS2), in_=Sc(C_S2)))
    prog.drain("vector")
    E("vector", lambda e: e.tensor_scalar(
        out=S[:, C_CBD:C_CBD + 8], in0=S[:, C_SWD:C_SWD + 8],
        scalar1=Sc(C_S2), scalar2=None, op0=AluOpType.mult))
    s2_done = prog.drain("vector", inc="vS")

    W("gpsimd", "tS", p3_tS)
    W("gpsimd", "vS", xq_done)
    for c in range(32):
        DMA("gpsimd", w2T_sb[:, c * 1024:(c + 1) * 1024], din["w2T"][c * 128:(c + 1) * 128, :])
    w2_loaded = prog.dma_handoff("gpsimd", w2T_sb[:, 0:16], din["w2T"][0:128, 0:16])
    _USE_SPLIT = False
    W("gpsimd", "tS", wg_slab_done[-1])
    for c in range(0, 7):
        DMA("gpsimd", w1T_sb[:, c * 4096:(c + 1) * 4096], din["w1T"][c * 128:(c + 1) * 128, :])
    w1_loaded = prog.dma_handoff("gpsimd", w1T_sb[:, 7 * 4096:8 * 4096], din["w1T"][7 * 128:8 * 128, :])

    o_ld = [0, 0]
    ab_st = [0, 0]
    mlp_of2_done = 0
    h1_done = 0
    W("sync", "dG", o_dma_last)
    W("sync", "vS", p5_vflush)
    prog.hwait("sync", spill_h)
    for t8 in range(8):
        sl = t8 % 2
        if t8 >= 2:
            W("sync", "tS", o_ld[sl])
        od = prog.dma_handoff("sync", o_sb2[sl][:],
                 bass.AP(o_park.tensor, t8 * 512, [[NTOK, 128], [128 * NTOK, 8], [1, 512]]))
        prog.hwait("tensor", od)
        if t8 == 0:
            prog.hwait("tensor", w1_loaded)
            W("tensor", "vS", p5_vflush)
            prog.hwait("tensor", spill_h)
        if t8 >= 1:
            W("tensor", "aS", mlp_of2_done)
        silus = []
        for of1 in range(32):
            p = ps_a[:, (of1 % 4) * 512:(of1 % 4) * 512 + 512]
            if of1 >= 4:
                W("tensor", "aS", silus[of1 - 4])
            for ifc in range(8):
                w1src = w1T_sb
                E("tensor", lambda e, p=p, c=of1, i=ifc, sl=sl, w=w1src: e.matmul(
                    out=p[:], lhsT=w[:, i * 4096 + c * 128:i * 4096 + (c + 1) * 128],
                    rhs=o_sb2[sl][:, i * 512:(i + 1) * 512],
                    start=(i == 0), stop=(i == 7)), inc="tS" if ifc == 7 else None)
            W("scalar", "tS", prog.cnt["tS"])
            sv2 = E("scalar", lambda e, p=p, c=of1: e.activation(
                out=h1_sb[:, c * 512:(c + 1) * 512], in_=p[:], func=ACT.Silu), inc="aS")
            silus.append(sv2)
        o_ld[sl] = prog.cnt["tS"]
        h1_land = prog.drain("scalar", inc="aS")
        W("tensor", "aS", h1_land)
        if t8 == 0:
            prog.hwait("tensor", w2_loaded)
        ab_ev = []
        for of2 in range(8):
            p = ps_b[:, (of2 % 3) * 512:(of2 % 3) * 512 + 512]
            if of2 >= 3:
                W("tensor", "aS", ab_ev[of2 - 3])
            for c in range(32):
                E("tensor", lambda e, p=p, c=c, o=of2: e.matmul(
                    out=p[:], lhsT=w2T_sb[:, c * 1024 + o * 128:c * 1024 + (o + 1) * 128],
                    rhs=h1_sb[:, c * 512:(c + 1) * 512],
                    start=(c == 0), stop=(c == 31)), inc="tS" if c == 31 else None)
            W("scalar", "tS", prog.cnt["tS"])
            asl = of2 % 2
            if ab_st[asl]:
                W("scalar", "dA", ab_st[asl])
            E("scalar", lambda e, p=p, asl=asl: e.copy(
                out=ab_evac[asl][:], in_=p[:]))
            ae = prog.drain("scalar", inc="aS")
            ab_ev.append(ae)
            ab_st[asl] = prog.dma("scalar",
                bass.AP(abuf.tensor, of2 * 128 * NTOK + t8 * 512, [[NTOK, 128], [1, 512]]),
                ab_evac[asl][:])
        mlp_of2_done = prog.cnt["aS"]
        h1_done = prog.cnt["tS"]

    if phases < 6:
        finish()
        return nc

    W("sync", "dG", prog.cnt["dG"])
    W("sync", "tS", h1_done)
    for c in range(NIC - 1):
        DMA("sync", wdT_sb[:, c * 1024:(c + 1) * 1024], wd_park[c * 128:(c + 1) * 128, :])
    wd_loaded = prog.dma_handoff("sync", wdT_sb[:, (NIC - 1) * 1024:NIC * 1024],
                                 wd_park[(NIC - 1) * 128:NIC * 128, :])
    W("vector", "vS", s2_done)

    q_done = [[0] * NIC, [0] * NIC]
    x_ld = [0] * 4
    ab_ld = [0, 0]
    merge_done = [0, 0]
    out_dma = [0, 0, 0, 0]
    dps_ev = []
    dps_done = [0, 0]
    hbufs = [h_int, h_int2]

    def emit_hquant(t8, ic0, ic1):
        hb = hbufs[t8 % 2]
        for ic in range(ic0, min(ic1, NIC)):
            sl = ic % 2
            prev = q_done[t8 % 2][ic - 2] if ic >= 2 else q_done[(t8 - 1) % 2][NIC - 2 + ic]
            if prev:
                W("sync", "vS", prev)
            gl = prog.dma_handoff("sync", g_slab[sl][:],
                     bass.AP(g_spill.tensor, ic * 128 * NTOK + t8 * 512, [[NTOK, 128], [1, 512]]))
            prog.hwait("vector", gl)
            if ic == 0 and dps_done[t8 % 2]:
                W("vector", "tS", dps_done[t8 % 2])
            E("vector", lambda e, sl=sl: e.tensor_scalar(
                out=qint6[:], in0=g_slab[sl][:], scalar1=Sc(C_RS2),
                scalar2=None, op0=AluOpType.mult))
            E("vector", lambda e: e.tensor_scalar(
                out=qint6[:], in0=qint6[:], scalar1=-128.0,
                scalar2=127.0, op0=AluOpType.max, op1=AluOpType.min))
            E("vector", lambda e, ic=ic, hb=hb: e.tensor_copy(
                out=hb[:, ic * 512:(ic + 1) * 512], in_=qint6[:]))
            q_done[t8 % 2][ic] = prog.drain("vector", inc="vS")

    emit_hquant(0, 0, NIC)
    for t8 in range(8):
        hb = hbufs[t8 % 2]
        hq_all = q_done[t8 % 2][NIC - 1]
        for tc in range(4):
            if out_dma[tc]:
                W("sync", "dG", out_dma[tc])
            x_ld[tc] = prog.dma_handoff("sync", x_sb[tc][:],
                           x[(t8 * 4 + tc) * 128:(t8 * 4 + tc + 1) * 128, :])
        W("tensor", "vS", hq_all)
        if t8 == 0:
            prog.hwait("tensor", wd_loaded)
        tadd = [0] * 8
        for of in range(8):
            if t8 < 7:
                emit_hquant(t8 + 1, of * 3, of * 3 + 3)
            dps = ps_a[:, (of % 2) * 512:(of % 2) * 512 + 512]
            if len(dps_ev) >= 2:
                W("tensor", "vS", dps_ev[-2])
            for ic in range(NIC):
                E("tensor", lambda e, p=dps, ic=ic, o=of, hb=hb: e.matmul(
                    out=p[:], lhsT=wdT_sb[:, ic * 1024 + o * 128:ic * 1024 + (o + 1) * 128],
                    rhs=hb[:, ic * 512:(ic + 1) * 512],
                    start=(ic == 0), stop=(ic == NIC - 1)), inc="tS" if ic == NIC - 1 else None)
            dmm = prog.cnt["tS"]
            asl = of % 2
            if ab_ld[asl]:
                W("scalar", "vS", merge_done[asl])
            ab_ld[asl] = prog.dma_handoff("scalar", ab_slab[asl][:],
                bass.AP(abuf.tensor, of * 128 * NTOK + t8 * 512, [[NTOK, 128], [1, 512]]))
            W("vector", "tS", dmm)
            prog.hwait("vector", ab_ld[asl])
            E("vector", lambda e, p=dps, o=of, asl=asl: e.scalar_tensor_tensor(
                out=sum_sb[asl][:], in0=p[:], scalar=S[:, C_CBD + o:C_CBD + o + 1],
                in1=ab_slab[asl][:], op0=AluOpType.mult, op1=AluOpType.add))
            mg = prog.drain("vector", inc="vS")
            dps_ev.append(mg)
            merge_done[asl] = mg
            W("tensor", "vS", mg)
            tbank = ps_b[:, (of % 2) * 512:(of % 2) * 512 + 512]
            if of >= 2:
                W("tensor", "vS", tadd[of - 2])
            for tc in range(4):
                E("tensor", lambda e, tb=tbank, tc=tc, asl=asl: e.transpose(
                    out=tb[:, tc * 128:(tc + 1) * 128],
                    in_=sum_sb[asl][:, tc * 128:(tc + 1) * 128],
                    identity=ident_f32[:]), inc="tS" if tc == 3 else None)
            ttp = prog.cnt["tS"]
            W("vector", "tS", ttp)
            for tc in range(4):
                prog.hwait("vector", x_ld[tc])
                E("vector", lambda e, tb=tbank, tc=tc, o=of: e.tensor_tensor(
                    out=out_sb[tc][:, o * 128:(o + 1) * 128],
                    in0=tb[:, tc * 128:(tc + 1) * 128],
                    in1=x_sb[tc][:, o * 128:(o + 1) * 128], op=AluOpType.add),
                    inc="vS" if tc == 3 else None)
            tadd[of] = prog.cnt["vS"]
        dps_done[t8 % 2] = prog.cnt["tS"]
        all_adds = prog.drain("vector", inc="vS")
        W("gpsimd", "vS", all_adds)
        for tc in range(4):
            out_dma[tc] = DMA("gpsimd", y[(t8 * 4 + tc) * 128:(t8 * 4 + tc + 1) * 128, :],
                              out_sb[tc][:])

    finish()
    return nc


def _emit_attention(ctx):
    (prog, s, q_rope, k_rope, v_tm, probs_tm, probs_T, o_sb, o_park,
     mask_main, mask_first, ident_bf, ps_a, ps_b, S,
     C_MPACK, C_DPACK, C_RDPACK, rope_done, v_done, st) = ctx
    E, W, DMA = prog.emit, prog.wait, prog.dma
    PS = [ps_a, ps_b]
    PSB = [ps_a.bitcast(mybir.dt.bfloat16), ps_b.bitcast(mybir.dt.bfloat16)]
    T_ = T
    NK = 32

    def half_info(k):
        return k // 2, (k % 2) * 8

    for k in range(NK + 1):
        if k < NK:
            qc, h0 = half_info(k)
            ps = PS[k % 2]
            W("tensor", "vS", rope_done)
            if k >= 2:
                W("tensor", "aS", st["oev"][k - 2])
            else:
                W("tensor", "aS", v_done)
            for ih in range(8):
                h = h0 + ih
                j, g = h % 4, h // 4
                tile = ps[:, ih * 256:(ih + 1) * 256]
                qx1 = q_rope[32 * g:32 * g + 32, j * T_ + qc * 128: j * T_ + (qc + 1) * 128]
                qx2 = q_rope[32 * g:32 * g + 32, (4 + j) * T_ + qc * 128: (4 + j) * T_ + (qc + 1) * 128]
                if qc == 0:
                    kx1 = k_rope[32 * g:32 * g + 32, 0:128]
                    kx2 = k_rope[32 * g:32 * g + 32, T_:T_ + 128]
                    out = tile[:, 128:256]
                    mask = mask_first
                else:
                    kx1 = k_rope[32 * g:32 * g + 32, (qc - 1) * 128:(qc + 1) * 128]
                    kx2 = k_rope[32 * g:32 * g + 32, T_ + (qc - 1) * 128:T_ + (qc + 1) * 128]
                    out = tile
                    mask = mask_main
                E("tensor", lambda e, o=out, q=qx1, kk=kx1, g=g: e.matmul(
                    out=o, lhsT=q, rhs=kk, start=True, stop=False, tile_position=(32 * g, 0)))
                E("tensor", lambda e, o=out, q=qx2, kk=kx2, g=g: e.matmul(
                    out=o, lhsT=q, rhs=kk, start=False, stop=False, tile_position=(32 * g, 0)))
                E("tensor", lambda e, t=tile, m=mask: e.matmul(
                    out=t, lhsT=ident_bf[:], rhs=m[:], start=False, stop=True,
                    tile_position=(0, 0)), inc="tS" if ih == 7 else None)
            st["stats"].append(prog.cnt["tS"])
            W("vector", "tS", st["stats"][k])
            mp = C_MPACK + 8 * (k % 2)
            dp = C_DPACK + 8 * (k % 2)
            rp = C_RDPACK + 8 * (k % 2)
            for ih in range(8):
                E("vector", lambda e, t=ps[:, ih * 256:(ih + 1) * 256], c=mp + ih: e.tensor_reduce(
                    out=S[:, c:c + 1], in_=t, axis=AX, op=AluOpType.max, negate=True))
            st["red"].append(prog.drain("vector", inc="vS"))
            W("scalar", "vS", st["red"][k])
            if k >= 2:
                prog.hwait("scalar", st["tp"][k - 2][0])
                prog.hwait("scalar", st["tp"][k - 2][1])
            for ih in range(8):
                E("scalar", lambda e, t=ps[:, ih * 256:(ih + 1) * 256], ih=ih, c=mp + ih,
                  d=dp + ih, k=k: e.activation(
                    out=probs_tm[:, (k % 2) * 2048 + ih * 256:(k % 2) * 2048 + (ih + 1) * 256],
                    in_=t, func=ACT.Exp, bias=S[:, c:c + 1], accum_out=S[:, d:d + 1]))
            st["exp"].append(prog.drain("scalar", inc="aS"))
            W("vector", "aS", st["exp"][k])
            E("vector", lambda e, rp=rp, dp=dp: e.reciprocal(
                out=S[:, rp:rp + 8], in_=S[:, dp:dp + 8]))
            prog.drain("vector")
            lo = 128 if qc == 0 else 0
            for ih in range(8):
                E("vector", lambda e, ih=ih, k=k, lo=lo, rp=rp: e.tensor_scalar(
                    out=probs_tm[:, (k % 2) * 2048 + ih * 256 + lo:(k % 2) * 2048 + (ih + 1) * 256],
                    in0=probs_tm[:, (k % 2) * 2048 + ih * 256 + lo:(k % 2) * 2048 + (ih + 1) * 256],
                    scalar1=S[:, rp + ih:rp + ih + 1], scalar2=None, op0=AluOpType.mult))
            st["norm"].append(prog.drain("vector", inc="vS"))

        if k >= 1:
            kb = k - 1
            qc, h0 = half_info(kb)
            psf = PS[kb % 2]
            psb = PSB[kb % 2]
            sides = (1,) if qc == 0 else (0, 1)
            for eng in ("sync", "scalar"):
                W(eng, "vS", st["norm"][kb])
                if kb >= 2:
                    W(eng, "tS", st["av"][kb - 2])
            i = 0
            for ih in range(8):
                for side in sides:
                    ti = 2 * ih + side
                    eng = "sync" if i % 2 == 0 else "scalar"
                    DMA(eng,
                        probs_T[:, (kb % 2) * 2048 + ti * 128:(kb % 2) * 2048 + (ti + 1) * 128],
                        probs_tm[:, (kb % 2) * 2048 + ih * 256 + side * 128:
                                 (kb % 2) * 2048 + ih * 256 + side * 128 + 128],
                        transpose=True)
                    i += 1
            hs1 = prog.queue_handshake("sync")
            hs2 = prog.queue_handshake("scalar")
            st["tp"].append((hs1, hs2))
            st["ev"].append((hs1, hs2))
            prog.hwait("tensor", hs1)
            prog.hwait("tensor", hs2)
            W("tensor", "aS", st["exp"][kb])
            W("tensor", "aS", v_done)
            for ih in range(8):
                h = h0 + ih
                g = h // 4
                p = ih // 2
                cg = 64 * (h % 2)
                out = psf[64 * (h % 2):64 * (h % 2) + 64, 1024 + p * 128:1024 + (p + 1) * 128]
                if qc == 0:
                    E("tensor", lambda e, o=out, g=g, cg=cg, ih=ih, kb=kb: e.matmul(
                        out=o, lhsT=v_tm[:, 64 * g: 64 * g + 64],
                        rhs=probs_T[:, (kb % 2) * 2048 + (2 * ih + 1) * 128:(kb % 2) * 2048 + (2 * ih + 2) * 128],
                        start=True, stop=True, tile_position=(0, cg)), inc="tS" if ih == 7 else None)
                else:
                    E("tensor", lambda e, o=out, g=g, cg=cg, ih=ih, kb=kb, qc=qc: e.matmul(
                        out=o, lhsT=v_tm[:, (qc - 1) * 256 + 64 * g: (qc - 1) * 256 + 64 * g + 64],
                        rhs=probs_T[:, (kb % 2) * 2048 + 2 * ih * 128:(kb % 2) * 2048 + (2 * ih + 1) * 128],
                        start=True, stop=False, tile_position=(0, cg)))
                    E("tensor", lambda e, o=out, g=g, cg=cg, ih=ih, kb=kb, qc=qc: e.matmul(
                        out=o, lhsT=v_tm[:, qc * 256 + 64 * g: qc * 256 + 64 * g + 64],
                        rhs=probs_T[:, (kb % 2) * 2048 + (2 * ih + 1) * 128:(kb % 2) * 2048 + (2 * ih + 2) * 128],
                        start=False, stop=True, tile_position=(0, cg)), inc="tS" if ih == 7 else None)
            st["av"].append(prog.cnt["tS"])
            import os as _os
            _cut = _os.environ.get("ATTN_CUT", "0")
            if _cut == "5":
                st["oev"].append(prog.drain("scalar", inc="aS"))
                continue
            W("scalar", "tS", st["av"][kb])
            slot = o_sb[qc % 2]
            if kb % 2 == 0 and st["o_dma"][qc % 2]:
                W("scalar", "dG", st["o_dma"][qc % 2])
            for p in range(4):
                pa = 4 * (kb % 2) + p
                E("scalar", lambda e, p=p, pa=pa, sl=slot, psf=psf: e.copy(
                    out=sl[:, pa * 128:(pa + 1) * 128],
                    in_=psf[:, 1024 + p * 128:1024 + (p + 1) * 128]))
            st["oev"].append(prog.drain("scalar", inc="aS"))
            if kb % 2 == 1 and _cut != "4":
                W("gpsimd", "aS", st["oev"][kb])
                st["o_dma"][qc % 2] = DMA(
                    "gpsimd",
                    bass.AP(o_park.tensor, s * T_ + qc * 128,
                            [[NTOK, 128], [128 * NTOK, 8], [1, 128]]),
                    slot[:])
    st["last_oev"] = st["oev"][-1]
    st["last_stats"] = st["stats"][-1]
    for key in ("stats", "red", "exp", "norm", "tp", "ev", "av", "oev"):
        st[key] = []



def kernel(**inputs):
    xfull = np.asarray(inputs["L0_attn_query"], np.float32)
    prep = host_prep(inputs)
    nc = build_nc(phases=6)
    in_maps = []
    for c in range(NCORES):
        m = {"x": np.ascontiguousarray(xfull[2 * c:2 * c + 2].reshape(NTOK, D))}
        m.update(prep)
        in_maps.append(m)
    res = run_bass_kernel_spmd(nc, in_maps, core_ids=list(range(NCORES)))
    return np.concatenate([r["y"].reshape(2, T, D) for r in res.results], axis=0)


# version 17
# speedup vs baseline: 1.0961x; 1.0961x over previous
import numpy as np
import concourse.bass as bass
import concourse.mybir as mybir
from concourse.alu_op_type import AluOpType
from concourse.bass_utils import run_bass_kernel_spmd

F32 = mybir.dt.float32
BF16 = mybir.dt.bfloat16
I32 = mybir.dt.int32
ACT = mybir.ActivationFunctionType
AX = mybir.AxisListType.X

D = 1024
T = 2048
NSEQ = 2
NTOK = NSEQ * T
NH = 16
HD = 64
DFF = 2816
DMLP = 4096
NCHUNK = NTOK // 128
NIC = DFF // 128
QMAX = 127.0
NCORES = 8

ENGINES = ("sync", "tensor", "vector", "scalar", "gpsimd")



def _perm_q():
    perm = np.zeros(D, dtype=np.int64)
    for half in range(2):
        for j in range(4):
            for g in range(4):
                for i in range(32):
                    perm[half * 512 + j * 128 + g * 32 + i] = (4 * g + j) * 64 + half * 32 + i
    return perm


def _perm_k():
    perm = np.zeros(256, dtype=np.int64)
    for half in range(2):
        for g in range(4):
            for i in range(32):
                perm[half * 128 + g * 32 + i] = g * 64 + half * 32 + i
    return perm


def host_prep(inputs):
    import ml_dtypes
    d = {}
    wq = np.asarray(inputs["L0_attn_Wq"], np.float32)
    wk = np.asarray(inputs["L0_attn_Wk"], np.float32)
    wv = np.asarray(inputs["L0_attn_Wv"], np.float32)
    d["wqT"] = np.ascontiguousarray(wq[_perm_q(), :].T)
    d["wkT"] = np.ascontiguousarray(wk[_perm_k(), :].T)
    d["wvT"] = np.ascontiguousarray(wv.T)
    d["w1T"] = np.ascontiguousarray(np.asarray(inputs["L0_attn_W1"], np.float32).T)
    d["w2T"] = np.ascontiguousarray(np.asarray(inputs["L0_attn_W2"], np.float32).T)
    d["wg_n"] = np.asarray(inputs["L0_ffn_w_gate"], np.float32)
    d["wu_n"] = np.asarray(inputs["L0_ffn_w_up"], np.float32)
    d["wd_n"] = np.asarray(inputs["L0_ffn_w_down"], np.float32)
    d["wgT"] = np.ascontiguousarray(d["wg_n"].T)
    d["wuT"] = np.ascontiguousarray(d["wu_n"].T)
    d["wdT"] = np.ascontiguousarray(d["wd_n"].T)
    d["clip_x"] = np.asarray(inputs["L0_ffn_clip_x"], np.float32).reshape(1, 1)
    d["clip_down"] = np.asarray(inputs["L0_ffn_clip_down"], np.float32).reshape(1, 1)

    inv = 1.0 / 10000.0 ** (np.arange(0, HD, 2, dtype=np.float64) / HD)
    fr = np.outer(inv, np.arange(T, dtype=np.float64))
    d["cos_t"] = np.tile(np.cos(fr), (4, 1)).astype(np.float32)
    d["sin_t"] = np.tile(np.sin(fr), (4, 1)).astype(np.float32)

    q = np.arange(128)[:, None]
    kk = np.arange(256)[None, :]
    NEG = np.float32(-1e30)
    allow_main = np.where(kk < 128, kk >= q + 1, (kk - 128) <= q)
    allow_first = np.where(kk < 128, False, (kk - 128) <= q)
    d["mask_main"] = np.where(allow_main, np.float32(0), NEG).astype(ml_dtypes.bfloat16)
    d["mask_first"] = np.where(allow_first, np.float32(0), NEG).astype(ml_dtypes.bfloat16)
    d["ident_bf"] = np.eye(128, dtype=ml_dtypes.bfloat16)
    d["ident_f32"] = np.eye(128, dtype=np.float32)
    return d



class Prog:
    def __init__(self, nc):
        self.nc = nc
        self.items = {e: [] for e in ENGINES}
        self.sems = {}
        self.cnt = {}

    def sem(self, name):
        h = self.nc.semaphore(name).__enter__()
        self.sems[name] = h
        self.cnt[name] = 0

    def emit(self, engine, fn, inc=None, inc_amt=1):
        if inc is not None:
            sem = self.sems[inc]
            self.cnt[inc] += inc_amt

            def wrapped(eng, fn=fn, sem=sem, amt=inc_amt):
                fn(eng).then_inc(sem, amt)
            self.items[engine].append(wrapped)
        else:
            self.items[engine].append(fn)
        return self.cnt[inc] if inc is not None else None

    def wait(self, engine, sem_name, value=None):
        if value is None:
            value = self.cnt[sem_name]
        sem = self.sems[sem_name]
        self.items[engine].append(lambda eng, sem=sem, v=value: eng.wait_ge(sem, v))

    def vnop(self):
        self.items["vector"].append(lambda eng: eng.nop())

    HSEM = {"sync": "hSs", "scalar": "hSa", "gpsimd": "hSg"}

    def dma_handoff(self, engine, out, in_, **kw):
        c = self.dma(engine, out, in_, **kw)
        qsem = {"sync": "dS", "scalar": "dA", "gpsimd": "dG"}[engine]
        hs = self.HSEM[engine]
        self.wait(engine, qsem, c)
        self.emit(engine, lambda e, hs=hs: e.sem_inc(self.sems[hs], 1), inc=None)
        return (hs, self._bump(hs))

    def hwait(self, engine, h):
        self.wait(engine, h[0], h[1])

    def queue_handshake(self, engine):
        qsem = {"sync": "dS", "scalar": "dA", "gpsimd": "dG"}[engine]
        hs = self.HSEM[engine]
        self.wait(engine, qsem, self.cnt[qsem])
        self.emit(engine, lambda e, hs=hs: e.sem_inc(self.sems[hs], 1), inc=None)
        return (hs, self._bump(hs))

    def _bump(self, name):
        self.cnt[name] += 1
        return self.cnt[name]

    def drain(self, engine, inc=None):
        if inc is not None:
            return self.emit(engine, lambda e: e.drain(), inc=inc)
        self.items[engine].append(lambda eng: eng.drain())
        return None

    def dma(self, engine, out, in_, sem=None, **kw):
        if sem is None:
            sem = {"sync": "dS", "scalar": "dA", "gpsimd": "dG"}[engine]
        return self.emit(engine, lambda e: e.dma_start(out=out, in_=in_, **kw),
                         inc=sem, inc_amt=16)

    def replay(self, block):
        @block.sync
        def _(eng):
            for fn in self.items["sync"]:
                fn(eng)

        @block.tensor
        def _(eng):
            for fn in self.items["tensor"]:
                fn(eng)

        @block.vector
        def _(eng):
            for fn in self.items["vector"]:
                fn(eng)

        @block.scalar
        def _(eng):
            for fn in self.items["scalar"]:
                fn(eng)

        @block.gpsimd
        def _(eng):
            for fn in self.items["gpsimd"]:
                fn(eng)



def build_nc(phases=6, debug=()):
    nc = bass.Bass("TRN2", target_bir_lowering=False)

    x = nc.dram_tensor("x", [NTOK, D], F32, kind="ExternalInput").ap()
    y = nc.dram_tensor("y", [NTOK, D], F32, kind="ExternalOutput").ap()
    din = {}
    for name, shape in [
        ("wqT", [D, D]), ("wkT", [D, 256]), ("wvT", [D, 256]),
        ("w1T", [D, DMLP]), ("w2T", [DMLP, D]),
        ("wg_n", [DFF, D]), ("wu_n", [DFF, D]), ("wd_n", [D, DFF]),
        ("wgT", [D, DFF]), ("wuT", [D, DFF]), ("wdT", [DFF, D]),
        ("clip_x", [1, 1]), ("clip_down", [1, 1]),
        ("cos_t", [128, T]), ("sin_t", [128, T]),
    ]:
        din[name] = nc.dram_tensor(name, shape, F32, kind="ExternalInput").ap()
    for name, shape in [("mask_main", [128, 256]), ("mask_first", [128, 256]),
                        ("ident_bf", [128, 128])]:
        din[name] = nc.dram_tensor(name, shape, BF16, kind="ExternalInput").ap()
    din["ident_f32"] = nc.dram_tensor("ident_f32", [128, 128], F32, kind="ExternalInput").ap()

    def scratch(name, shape, dt):
        kind = "ExternalOutput" if name in debug else "Internal"
        return nc.dram_tensor(name, shape, dt, kind=kind).ap()

    wg_park = scratch("wg_park", [D, DFF], BF16)
    wu_park = scratch("wu_park", [D, DFF], BF16)
    wd_park = scratch("wd_park", [DFF, D], BF16)
    g_spill = scratch("g_spill", [DFF, NTOK], F32)
    o_park = scratch("o_park", [D, NTOK], BF16)
    abuf = scratch("abuf", [D, NTOK], F32)
    scl_pack = nc.dram_tensor("scl_pack", [64, 128], F32).ap()
    cc1_in = nc.dram_tensor("cc1_in", [128, 128], F32).ap()
    cc1_out = nc.dram_tensor("cc1_out", [128, 128], F32).ap()
    cc2_in = nc.dram_tensor("cc2_in", [128, 128], F32).ap()
    cc2_out = nc.dram_tensor("cc2_out", [128, 128], F32).ap()
    dbg = {}
    for nm, shape, dt in [("smA", [128, 888], F32), ("smB", [128, 888], F32),
                          ("smC", [128, 888], F32),
                          ("cc1i", [128, 128], F32), ("cc1o", [128, 128], F32),
                          ("cc2i", [128, 128], F32), ("cc2o", [128, 128], F32),
                          ("recips", [128, DFF], F32), ("xnT", [128, 8 * NTOK], BF16), ("xq", [128, 8 * NTOK], BF16),
                          ("q", [128, 8 * T], BF16), ("k", [128, 2 * T], BF16),
                          ("v", [128, 16 * 256], BF16), ("smalls", [128, 888], F32)]:
        if nm in debug:
            dbg[nm] = nc.dram_tensor("dbg_" + nm, shape, dt, kind="ExternalOutput").ap()

    SB_BASE = 16512
    def sb(name, off, shape, dt):
        return nc.alloc_sbuf_tensor_at(name, shape, dt, offset=SB_BASE + off).ap()

    xnT = sb("xnT", 0, [128, 8 * NTOK], BF16)
    w1T_sb = sb("w1T_sb", 0, [128, 8 * DMLP], BF16)
    w1T_a = sb("w1T_a", 98304, [128, 4 * DMLP], BF16)
    wdT_sb = sb("wdT_sb", 0, [128, NIC * D], BF16)
    h_int = sb("h_int", 45056, [128, NIC * 512], BF16)
    h_int2 = sb("h_int2", 112640, [128, NIC * 512], BF16)
    qint6 = sb("qint6", 135168, [128, 512], I32)
    g_slab = [sb(f"g_slab{i}", 67584 + i * 2048, [128, 512], F32) for i in range(2)]
    ab_slab = [sb(f"ab_slab{i}", 71680 + i * 2048, [128, 512], F32) for i in range(2)]
    sum_sb = [sb(f"sum_sb{i}", 75776 + i * 2048, [128, 512], F32) for i in range(2)]
    out_sb = [sb(f"out_sb{i}", 79872 + i * 4096, [128, D], F32) for i in range(4)]
    x_sb = [sb(f"x_sb{i}", 96256 + i * 4096, [128, D], F32) for i in range(4)]

    w2T_sb = sb("w2T_sb", 65536, [128, 32 * D], BF16)
    wnat = [sb(f"wnat{i}", 65536 + i * 11264, [128, DFF], F32) for i in range(2)]
    wTb = [sb(f"wTb{i}", 88064 + i * 11264, [128, DFF], F32) for i in range(2)]
    wint32 = sb("wint32", 110592, [128, DFF], I32)
    wintbf = sb("wintbf", 121856, [128, DFF], BF16)
    recip_gb = sb("recip_gb", 127488, [128, DFF], F32)
    recip_ub = sb("recip_ub", 138752, [128, DFF], F32)
    recip_db = sb("recip_db", 150016, [128, D], F32)
    xbuf = [sb(f"xbuf{i}", 154112 + i * 4096, [128, D], F32) for i in range(2)]
    xn_tm = [sb(f"xn_tm{i}", 162304 + i * 2048, [128, D], BF16) for i in range(2)]
    q_rope = sb("q_rope", 65536, [128, 8 * T], BF16)
    k_rope = sb("k_rope", 98304, [128, 2 * T], BF16)
    v_tm = sb("v_tm", 106496, [128, 16 * 256], BF16)
    qraw = sb("qraw", 114688, [128, 10 * 512], F32)
    rope_t = [sb(f"rope_t{i}", 135168 + i * 2048, [128, 512], F32) for i in range(2)]
    wvT_sb = sb("wvT_sb", 150016, [128, 8 * 256], BF16)
    wqT_sb = sb("wqT_sb", 166400, [128, 8 * 1024], BF16)
    wkT_sb = sb("wkT_sb", 182784, [128, 8 * 256], BF16)
    sq_scr = sb("sq_scr", 186880, [128, D], F32)
    xq_int = sb("xq_int", 114688, [128, 4096], I32)
    probs_tm = sb("probs_tm", 114688, [128, NH * 256], BF16)
    probs_T = sb("probs_T", 122880, [128, NH * 256], BF16)
    o_sb = [sb(f"o_sb{i}", 131072 + i * 2048, [128, D], BF16) for i in range(2)]
    wgi_sb = [sb(f"wgi_sb{i}", 131072 + i * 2048, [128, 8 * 128], BF16) for i in range(2)]
    wui_sb = [sb(f"wui_sb{i}", 135168 + i * 2048, [128, 8 * 128], BF16) for i in range(2)]
    gate_f32 = sb("gate_f32", 139264, [128, 4096], F32)
    gup_f32 = sb("gup_f32", 155648, [128, 4096], F32)
    h1_sb = sb("h1_sb", 131072, [128, 32 * 512], BF16)
    o_sb2 = [sb(f"o_sb2{i}", 163840 + i * 8192, [128, 8 * 512], BF16) for i in range(2)]
    ab_evac = [sb(f"ab_evac{i}", 180224 + i * 2048, [128, 512], F32) for i in range(2)]

    PT = 191104
    cos_sb = sb("cos_sb", PT, [128, T], F32)
    sin_sb = sb("sin_sb", PT + 8192, [128, T], F32)
    mask_main = sb("mask_main_sb", PT + 16384, [128, 256], BF16)
    mask_first = sb("mask_first_sb", PT + 16896, [128, 256], BF16)
    ident_bf = sb("ident_bf_sb", PT + 17408, [128, 128], BF16)
    ident_f32 = sb("ident_f32_sb", PT + 17664, [128, 128], F32)
    S = sb("smalls", PT + 18176, [128, 888], F32)

    (C_GMAX1, C_MEAN, C_MSQ, C_VAR, C_RSTD, C_MX, C_MN, C_T0, C_T1, C_SUM,
     C_SQS, C_S1, C_RS1, C_SIGX, C_SIGD, C_GMAX2, C_S2, C_RS2, C_CLX,
     C_CLD) = range(20)
    C_MPACK, C_DPACK, C_RDPACK = 32, 48, 64
    C_SUMP, C_SQSP, C_MXP, C_MNP = 600, 632, 664, 696
    C_MEANP, C_MSQP, C_VARP, C_RSTDP, C_AMP = 728, 760, 792, 824, 856
    C_RPG, C_RPU, C_RPD = 96, 118, 140
    C_CBG, C_CBU, C_CBD = 192, 224, 256
    C_BCR = 384
    C_SWG, C_SWU, C_SWD = 512, 544, 576

    def Sc(col, n=1):
        return S[:, col:col + n]

    ps_a = nc.alloc_psum_tensor("ps_a", [128, 2048], F32).ap()
    ps_b = nc.alloc_psum_tensor("ps_b", [128, 2048], F32).ap()
    pst = ps_a.bitcast(BF16)[:, 0:1024]

    prog = Prog(nc)
    for s_ in ("dS", "dA", "dG", "vS", "aS", "tS", "gS", "hSs", "hSa", "hSg"):
        prog.sem(s_)
    E, W, DMA = prog.emit, prog.wait, prog.dma

    DMA("sync", cos_sb[:], din["cos_t"])
    DMA("sync", sin_sb[:], din["sin_t"])
    DMA("sync", mask_main[:], din["mask_main"])
    DMA("sync", mask_first[:], din["mask_first"])
    DMA("sync", ident_bf[:], din["ident_bf"])
    DMA("sync", ident_f32[:], din["ident_f32"])
    DMA("sync", Sc(C_CLX), bass.AP(din["clip_x"].tensor, 0, [[0, 128], [1, 1]]))
    idf_loaded = prog.cnt["dS"] + 16
    c0 = prog.dma_handoff("sync", Sc(C_CLD), bass.AP(din["clip_down"].tensor, 0, [[0, 128], [1, 1]]))
    prog.hwait("vector", c0)
    E("vector", lambda e: e.memset(Sc(C_GMAX1), 0.0))
    E("vector", lambda e: e.memset(Sc(C_GMAX2), 0.0), inc="vS")
    prog.hwait("scalar", c0)
    E("scalar", lambda e: e.activation(out=Sc(C_SIGX), in_=Sc(C_CLX), func=ACT.Sigmoid))
    sig_done = E("scalar", lambda e: e.activation(out=Sc(C_SIGD), in_=Sc(C_CLD), func=ACT.Sigmoid), inc="aS")

    for ifc in range(8):
        DMA("gpsimd", wqT_sb[:, ifc * 1024:(ifc + 1) * 1024], din["wqT"][ifc * 128:(ifc + 1) * 128, :])
    for ifc in range(7):
        DMA("gpsimd", wkT_sb[:, ifc * 256:(ifc + 1) * 256], din["wkT"][ifc * 128:(ifc + 1) * 128, :])
    wqk_loaded = prog.dma_handoff("gpsimd", wkT_sb[:, 7 * 256:8 * 256], din["wkT"][7 * 128:8 * 128, :])

    C_SWGN, C_SWUN, C_SWDN = 302, 324, 346
    wq_jobs = []
    for src, cS, cN, nchunk, width in ((din["wg_n"], C_SWG, C_SWGN, NIC, D),
                                       (din["wu_n"], C_SWU, C_SWUN, NIC, D),
                                       (din["wd_n"], C_SWD, C_SWDN, 8, DFF)):
        for c in range(nchunk):
            wq_jobs.append((src, cS + c, cN + c, c, width))
    nat_done = []
    for idx, (src, cc, ccn, c, width) in enumerate(wq_jobs):
        slot = wnat[idx % 2]
        if idx >= 2:
            W("sync", "vS", nat_done[idx - 2])
        cd = prog.dma_handoff("sync", slot[:, 0:width], src[c * 128:(c + 1) * 128, :])
        prog.hwait("vector", cd)
        nd = E("vector", lambda e, s_=slot, w=width, cc=cc: e.tensor_reduce(
            out=S[:, cc:cc + 1], in_=s_[:, 0:w], axis=AX, op=AluOpType.max,
            apply_absolute_value=True), inc="vS")
        nat_done.append(nd)
    prog.drain("vector")
    for base, n in ((C_SWG, NIC), (C_SWU, NIC), (C_SWD, 8)):
        E("vector", lambda e, b=base, n_=n: e.tensor_scalar(
            out=S[:, b:b + n_], in0=S[:, b:b + n_], scalar1=1.0 / QMAX,
            scalar2=None, op0=AluOpType.mult))
    prog.drain("vector")
    for dst, srcb, n in ((C_RPG, C_SWG, NIC), (C_RPU, C_SWU, NIC), (C_RPD, C_SWD, 8)):
        E("vector", lambda e, d=dst, s=srcb, n_=n: e.reciprocal(
            out=S[:, d:d + n_], in_=S[:, s:s + n_]))
    rp_done = prog.drain("vector", inc="vS")
    W("tensor", "vS", rp_done)
    W("tensor", "dS", idf_loaded)
    tp = E("tensor", lambda e: e.transpose(
        out=ps_b[0:64, 0:128], in_=S[:, C_RPG:C_RPG + 64], identity=ident_f32[:]), inc="tS")
    W("scalar", "tS", tp)
    E("scalar", lambda e: e.copy(
        out=S[0:64, C_BCR:C_BCR + 128], in_=ps_b[0:64, 0:128]))
    tpe = prog.drain("scalar", inc="aS")
    W("gpsimd", "aS", tpe)
    pk = DMA("gpsimd", scl_pack[0:64, :], S[0:64, C_BCR:C_BCR + 128])
    W("gpsimd", "dG", pk)
    DMA("gpsimd", recip_gb[:], bass.AP(scl_pack.tensor, 0, [[0, 128], [1, DFF]]))
    DMA("gpsimd", recip_ub[:], bass.AP(scl_pack.tensor, 22 * 128, [[0, 128], [1, DFF]]))
    bc_done = prog.dma_handoff("gpsimd", recip_db[:], bass.AP(scl_pack.tensor, 44 * 128, [[0, 128], [1, D]]))

    if "recips" in dbg:
        W("gpsimd", "dG", bc_done)
        DMA("gpsimd", dbg["recips"], recip_gb[:])
    wt_jobs = []
    for src, park, rb, nchunk, width in ((din["wgT"], wg_park, recip_gb, 8, DFF),
                                         (din["wuT"], wu_park, recip_ub, 8, DFF),
                                         (din["wdT"], wd_park, recip_db, NIC, D)):
        for c in range(nchunk):
            wt_jobs.append((src, park, rb, c, width))
    prog.hwait("sync", bc_done)
    t_done, park_done = [], []
    for idx, (src, park, rb, c, width) in enumerate(wt_jobs):
        slot = wTb[idx % 2]
        if idx >= 2:
            W("sync", "vS", t_done[idx - 2])
        cd = prog.dma_handoff("sync", slot[:, 0:width], src[c * 128:(c + 1) * 128, :])
        prog.hwait("vector", cd)
        if idx >= 1:
            W("vector", "dG", park_done[idx - 1])
        E("vector", lambda e, s=slot, r=rb, w=width: e.tensor_tensor(
            out=wint32[:, 0:w], in0=s[:, 0:w], in1=r[:, 0:w], op=AluOpType.mult))
        E("vector", lambda e, w=width: e.tensor_copy(
            out=wintbf[:, 0:w], in_=wint32[:, 0:w]))
        cv = prog.drain("vector", inc="vS")
        t_done.append(cv)
        W("gpsimd", "vS", cv)
        pd = DMA("gpsimd", park[c * 128:(c + 1) * 128, :], wintbf[:, 0:width])
        park_done.append(pd)

    pa_done = []
    pa_gs = []
    for kch in range(NCHUNK):
        slot = xbuf[kch % 2]
        if kch >= 2:
            W("sync", "vS", pa_done[kch - 2])
        cd = prog.dma_handoff("sync", slot[:], x[kch * 128:(kch + 1) * 128, :])
        prog.hwait("vector", cd)
        E("vector", lambda e, s=slot, k=kch: e.tensor_reduce(
            out=S[:, C_SUMP + k:C_SUMP + k + 1], in_=s[:], axis=AX, op=AluOpType.add))
        E("vector", lambda e, s=slot: e.tensor_tensor(
            out=sq_scr[:], in0=s[:], in1=s[:], op=AluOpType.mult))
        E("vector", lambda e, k=kch: e.tensor_reduce(
            out=S[:, C_SQSP + k:C_SQSP + k + 1], in_=sq_scr[:], axis=AX, op=AluOpType.add))
        E("vector", lambda e, s=slot, k=kch: e.tensor_reduce(
            out=S[:, C_MXP + k:C_MXP + k + 1], in_=s[:], axis=AX, op=AluOpType.max))
        nv = E("vector", lambda e, s=slot, k=kch: e.tensor_reduce(
            out=S[:, C_MNP + k:C_MNP + k + 1], in_=s[:], axis=AX, op=AluOpType.min), inc="vS")
        pa_done.append(nv)
        pa_gs.append(nv)
    DR = prog.drain
    E("vector", lambda e: e.tensor_scalar(
        out=S[:, C_MEANP:C_MEANP + 32], in0=S[:, C_SUMP:C_SUMP + 32],
        scalar1=1.0 / D, scalar2=None, op0=AluOpType.mult))
    E("vector", lambda e: e.tensor_scalar(
        out=S[:, C_MSQP:C_MSQP + 32], in0=S[:, C_SQSP:C_SQSP + 32],
        scalar1=1.0 / D, scalar2=None, op0=AluOpType.mult))
    DR("vector")
    E("vector", lambda e: e.tensor_tensor(
        out=S[:, C_VARP:C_VARP + 32], in0=S[:, C_MEANP:C_MEANP + 32],
        in1=S[:, C_MEANP:C_MEANP + 32], op=AluOpType.mult))
    DR("vector")
    E("vector", lambda e: e.tensor_tensor(
        out=S[:, C_VARP:C_VARP + 32], in0=S[:, C_MSQP:C_MSQP + 32],
        in1=S[:, C_VARP:C_VARP + 32], op=AluOpType.subtract))
    DR("vector")
    vv = DR("vector", inc="vS")
    E("vector", lambda e: e.tensor_scalar(
        out=S[:, C_VARP:C_VARP + 32], in0=S[:, C_VARP:C_VARP + 32],
        scalar1=1e-5, scalar2=None, op0=AluOpType.add))
    vv = DR("vector", inc="vS")
    W("scalar", "vS", vv)
    E("scalar", lambda e: e.activation(
        out=S[:, C_VARP:C_VARP + 32], in_=S[:, C_VARP:C_VARP + 32], func=ACT.Sqrt))
    sv = DR("scalar", inc="aS")
    W("vector", "aS", sv)
    E("vector", lambda e: e.reciprocal(
        out=S[:, C_RSTDP:C_RSTDP + 32], in_=S[:, C_VARP:C_VARP + 32]))
    DR("vector")
    E("vector", lambda e: e.tensor_tensor(
        out=S[:, C_MXP:C_MXP + 32], in0=S[:, C_MXP:C_MXP + 32],
        in1=S[:, C_MEANP:C_MEANP + 32], op=AluOpType.subtract))
    E("vector", lambda e: e.tensor_tensor(
        out=S[:, C_MNP:C_MNP + 32], in0=S[:, C_MEANP:C_MEANP + 32],
        in1=S[:, C_MNP:C_MNP + 32], op=AluOpType.subtract))
    DR("vector")
    E("vector", lambda e: e.tensor_tensor(
        out=S[:, C_AMP:C_AMP + 32], in0=S[:, C_MXP:C_MXP + 32],
        in1=S[:, C_MNP:C_MNP + 32], op=AluOpType.max))
    DR("vector")
    E("vector", lambda e: e.tensor_tensor(
        out=S[:, C_AMP:C_AMP + 32], in0=S[:, C_AMP:C_AMP + 32],
        in1=S[:, C_RSTDP:C_RSTDP + 32], op=AluOpType.mult))
    DR("vector")
    E("vector", lambda e: e.tensor_reduce(
        out=Sc(C_GMAX1), in_=S[:, C_AMP:C_AMP + 32], axis=AX, op=AluOpType.max))
    stats_done = DR("vector", inc="vS")

    W("sync", "vS", stats_done)
    pb_done = []
    tp_done = []
    ev_done = 0
    for kch in range(NCHUNK):
        slot = xbuf[kch % 2]
        nslot = xn_tm[kch % 2]
        if kch >= 2:
            W("sync", "vS", pb_done[kch - 2])
        cd = prog.dma_handoff("sync", slot[:], x[kch * 128:(kch + 1) * 128, :])
        prog.hwait("vector", cd)
        if kch == 0:
            W("vector", "vS", stats_done)
        if kch >= 2:
            W("vector", "tS", tp_done[kch - 2])
        nv = E("vector", lambda e, s=slot, n=nslot, k=kch: e.tensor_scalar(
            out=n[:], in0=s[:], scalar1=S[:, C_MEANP + k:C_MEANP + k + 1],
            scalar2=S[:, C_RSTDP + k:C_RSTDP + k + 1],
            op0=AluOpType.subtract, op1=AluOpType.mult), inc="vS")
        pb_done.append(nv)
        W("tensor", "vS", nv)
        if kch >= 1:
            W("tensor", "aS", ev_done)
        for t8 in range(8):
            E("tensor", lambda e, n=nslot, t=t8: e.transpose(
                out=pst[:, t * 128:(t + 1) * 128], in_=n[:, t * 128:(t + 1) * 128],
                identity=ident_bf[:]), inc="tS")
        tp_done.append(prog.cnt["tS"])
        W("scalar", "tS", prog.cnt["tS"])
        for t8 in range(8):
            E("scalar", lambda e, t=t8, k=kch: e.copy(
                out=xnT[:, t * NTOK + k * 128: t * NTOK + (k + 1) * 128],
                in_=pst[:, t * 128:(t + 1) * 128]), inc="aS")
        ev_done = prog.cnt["aS"]

    prog.drain("scalar", inc="aS")
    p1_aS = prog.cnt["aS"]
    p1_vS = prog.cnt["vS"]

    if "xnT" in dbg:
        W("gpsimd", "aS", p1_aS)
        DMA("gpsimd", dbg["xnT"], xnT[:])

    W("gpsimd", "vS", p1_vS)
    c1 = DMA("gpsimd", cc1_in[0:1, 0:128], Sc(C_GMAX1))
    W("gpsimd", "dG", c1)
    ccd = E("gpsimd", lambda e: e.collective_compute(
        "AllReduce", AluOpType.max, replica_groups=[list(range(NCORES))],
        ins=[cc1_in[0:1, 0:128].opt()], outs=[cc1_out[0:1, 0:128].opt()]), inc="gS")
    W("gpsimd", "gS", ccd)
    if "cc1i" in dbg:
        DMA("gpsimd", dbg["cc1i"], cc1_in)
        DMA("gpsimd", dbg["cc1o"], cc1_out)
    c2 = prog.dma_handoff("gpsimd", S[:, C_BCR:C_BCR + 128],
                          bass.AP(cc1_out.tensor, 0, [[0, 128], [1, 128]]))
    prog.hwait("vector", c2)
    E("vector", lambda e: e.tensor_reduce(
        out=Sc(C_S1), in_=S[:, C_BCR:C_BCR + 128], axis=AX, op=AluOpType.max))
    W("vector", "aS", sig_done)
    prog.drain("vector")
    E("vector", lambda e: e.tensor_tensor(
        out=Sc(C_S1), in0=Sc(C_S1), in1=Sc(C_SIGX), op=AluOpType.mult))
    prog.drain("vector")
    E("vector", lambda e: e.tensor_scalar(
        out=Sc(C_S1), in0=Sc(C_S1), scalar1=1.0 / QMAX, scalar2=None, op0=AluOpType.mult))
    prog.drain("vector")
    E("vector", lambda e: e.reciprocal(out=Sc(C_RS1), in_=Sc(C_S1)))
    s1_done = prog.drain("vector", inc="vS")
    E("vector", lambda e: e.tensor_scalar(
        out=S[:, C_CBG:C_CBG + NIC], in0=S[:, C_SWG:C_SWG + NIC],
        scalar1=Sc(C_S1), scalar2=None, op0=AluOpType.mult))
    E("vector", lambda e: e.tensor_scalar(
        out=S[:, C_CBU:C_CBU + NIC], in0=S[:, C_SWU:C_SWU + NIC],
        scalar1=Sc(C_S1), scalar2=None, op0=AluOpType.mult))
    combo_done = prog.drain("vector", inc="vS")

    def snap(tag):
        if tag in dbg:
            W("gpsimd", "vS", prog.cnt["vS"])
            W("gpsimd", "aS", prog.cnt["aS"])
            DMA("gpsimd", dbg[tag], S[:])

    def finish():
        if "smalls" in dbg:
            W("gpsimd", "vS", prog.cnt["vS"])
            W("gpsimd", "aS", prog.cnt["aS"])
            DMA("gpsimd", dbg["smalls"], S[:])
        if phases < 6:
            for k in range(NCHUNK):
                DMA("gpsimd", y[k * 128:(k + 1) * 128, :], x[k * 128:(k + 1) * 128, :])
        W("gpsimd", "dG", prog.cnt["dG"])
        W("gpsimd", "dS", prog.cnt["dS"])
        W("gpsimd", "dA", prog.cnt["dA"])
        with nc.Block() as block:
            prog.replay(block)

    snap("smA")
    if phases < 2:
        finish()
        return nc

    W("gpsimd", "vS", t_done[-1])
    for ifc in range(7):
        DMA("gpsimd", wvT_sb[:, ifc * 256:(ifc + 1) * 256], din["wvT"][ifc * 128:(ifc + 1) * 128, :])
    wv_loaded = prog.dma_handoff("gpsimd", wvT_sb[:, 7 * 256:8 * 256], din["wvT"][7 * 128:8 * 128, :])

    prog.hwait("tensor", max(wqk_loaded, wv_loaded, key=lambda t: t[1]))
    W("tensor", "aS", p1_aS)
    ps_q = [ps_a[:, i * 512:(i + 1) * 512] for i in range(4)]
    qev_cnt = []
    vev_cnt = []
    attn_st = {"o_dma": [0, 0], "stats": [], "red": [], "exp": [], "norm": [],
               "tp": [], "ev": [], "av": [], "oev": [], "last_oev": 0,
               "last_stats": 0}

    rope_tt_done = 0
    for s in range(NSEQ):
        if s > 0:
            W("vector", "tS", prog.cnt["tS"])
            W("scalar", "tS", prog.cnt["tS"])
            W("tensor", "aS", prog.cnt["aS"])
        for tt in range(4):
            for oc in range(10):
                pslot = ps_q[len(qev_cnt) % 4]
                if len(qev_cnt) >= 4:
                    W("tensor", "aS", qev_cnt[-4])
                for ifc in range(8):
                    if oc < 8:
                        lhs = wqT_sb[:, ifc * 1024 + oc * 128: ifc * 1024 + (oc + 1) * 128]
                    else:
                        lhs = wkT_sb[:, ifc * 256 + (oc - 8) * 128: ifc * 256 + (oc - 7) * 128]
                    E("tensor", lambda e, l=lhs, sl=pslot, i=ifc, s_=s, t_=tt: e.matmul(
                        out=sl[:], lhsT=l,
                        rhs=xnT[:, i * NTOK + s_ * T + t_ * 512: i * NTOK + s_ * T + (t_ + 1) * 512],
                        start=(i == 0), stop=(i == 7)), inc="tS" if ifc == 7 else None)
                W("scalar", "tS", prog.cnt["tS"])
                if oc == 0:
                    W("scalar", "vS", rope_tt_done)
                ec = E("scalar", lambda e, sl=pslot, o=oc: e.copy(
                    out=qraw[:, o * 512:(o + 1) * 512], in_=sl[:]), inc="aS")
                qev_cnt.append(ec)
            prog.drain("scalar", inc="aS")
            W("vector", "aS", prog.cnt["aS"])
            cosT = cos_sb[:, tt * 512:(tt + 1) * 512]
            sinT = sin_sb[:, tt * 512:(tt + 1) * 512]
            for j in range(5):
                if j < 4:
                    x1 = qraw[:, j * 512:(j + 1) * 512]
                    x2 = qraw[:, (4 + j) * 512:(5 + j) * 512]
                    d1 = q_rope[:, j * T + tt * 512: j * T + (tt + 1) * 512]
                    d2 = q_rope[:, (4 + j) * T + tt * 512: (4 + j) * T + (tt + 1) * 512]
                else:
                    x1 = qraw[:, 8 * 512:9 * 512]
                    x2 = qraw[:, 9 * 512:10 * 512]
                    d1 = k_rope[:, tt * 512:(tt + 1) * 512]
                    d2 = k_rope[:, T + tt * 512: T + (tt + 1) * 512]
                sc = 0.125 if j < 4 else 1.0
                E("vector", lambda e, a=x1, c=cosT, sc=sc: e.scalar_tensor_tensor(
                    out=rope_t[0][:], in0=a, scalar=sc, in1=c,
                    op0=AluOpType.mult, op1=AluOpType.mult))
                E("vector", lambda e, b=x2, sn=sinT, sc=sc: e.scalar_tensor_tensor(
                    out=rope_t[1][:], in0=b, scalar=sc, in1=sn,
                    op0=AluOpType.mult, op1=AluOpType.mult))
                prog.drain("vector")
                E("vector", lambda e, d=d1: e.tensor_tensor(
                    out=d, in0=rope_t[0][:], in1=rope_t[1][:], op=AluOpType.subtract))
                E("vector", lambda e, a=x1, sn=sinT, sc=sc: e.scalar_tensor_tensor(
                    out=rope_t[0][:], in0=a, scalar=sc, in1=sn,
                    op0=AluOpType.mult, op1=AluOpType.mult))
                E("vector", lambda e, b=x2, c=cosT, sc=sc: e.scalar_tensor_tensor(
                    out=rope_t[1][:], in0=b, scalar=sc, in1=c,
                    op0=AluOpType.mult, op1=AluOpType.mult))
                prog.drain("vector")
                E("vector", lambda e, d=d2: e.tensor_tensor(
                    out=d, in0=rope_t[0][:], in1=rope_t[1][:], op=AluOpType.add),
                  inc="vS" if j == 4 else None)
            rope_tt_done = prog.cnt["vS"]
        rope_done = prog.cnt["vS"]
        for tc in range(16):
            vslot = ps_b[:, (tc % 3) * 512:(tc % 3) * 512 + 256]
            if tc >= 3:
                W("tensor", "aS", vev_cnt[-3])
            for ifc in range(8):
                E("tensor", lambda e, sl=vslot, i=ifc, s_=s, t_=tc: e.matmul(
                    out=sl[:],
                    lhsT=xnT[:, i * NTOK + s_ * T + t_ * 128: i * NTOK + s_ * T + t_ * 128 + 128],
                    rhs=wvT_sb[:, i * 256:(i + 1) * 256],
                    start=(i == 0), stop=(i == 7)), inc="tS" if ifc == 7 else None)
            W("scalar", "tS", prog.cnt["tS"])
            ec = E("scalar", lambda e, sl=vslot, t_=tc: e.copy(
                out=v_tm[:, t_ * 256:(t_ + 1) * 256], in_=sl[:]), inc="aS")
            vev_cnt.append(ec)
        v_done = prog.drain("scalar", inc="aS")

        if phases >= 3:
            _emit_attention((prog, s, q_rope, k_rope, v_tm, probs_tm, probs_T,
                             o_sb, o_park, mask_main, mask_first, ident_bf,
                             ps_a, ps_b, S, C_MPACK, C_DPACK, C_RDPACK,
                             rope_done, v_done, attn_st))

    p23_tS = prog.cnt["tS"]
    p23_vS = prog.cnt["vS"]
    p23_aS = prog.cnt["aS"]
    if "q" in dbg:
        W("gpsimd", "vS", p23_vS)
        DMA("gpsimd", dbg["q"], q_rope[:])
        DMA("gpsimd", dbg["k"], k_rope[:])
        W("gpsimd", "aS", p23_aS)
        DMA("gpsimd", dbg["v"], v_tm[:])

    snap("smB")
    W("vector", "vS", s1_done)
    W("vector", "tS", p23_tS)
    W("sync", "tS", p23_tS)
    q_ev = 0
    q_tp = []
    q_nv = []
    for kch in range(NCHUNK):
        slot = xbuf[kch % 2]
        nslot = xn_tm[kch % 2]
        if kch >= 2:
            W("sync", "vS", q_nv[kch - 2])
        cd = prog.dma_handoff("sync", slot[:], x[kch * 128:(kch + 1) * 128, :])
        prog.hwait("vector", cd)
        if kch >= 2:
            W("vector", "tS", q_tp[kch - 2])
        E("vector", lambda e, s_=slot, k=kch: e.tensor_scalar(
            out=s_[:], in0=s_[:], scalar1=S[:, C_MEANP + k:C_MEANP + k + 1],
            scalar2=S[:, C_RSTDP + k:C_RSTDP + k + 1],
            op0=AluOpType.subtract, op1=AluOpType.mult))
        E("vector", lambda e, s_=slot: e.tensor_scalar(
            out=xq_int[:, 0:D], in0=s_[:], scalar1=Sc(C_RS1), scalar2=None,
            op0=AluOpType.mult))
        E("vector", lambda e: e.tensor_scalar(
            out=xq_int[:, 0:D], in0=xq_int[:, 0:D], scalar1=-128.0, scalar2=127.0,
            op0=AluOpType.max, op1=AluOpType.min))
        nv = E("vector", lambda e, n=nslot: e.tensor_copy(
            out=n[:], in_=xq_int[:, 0:D]), inc="vS")
        q_nv.append(nv)
        W("tensor", "vS", nv)
        if kch >= 1:
            W("tensor", "aS", q_ev)
        for t8 in range(8):
            E("tensor", lambda e, n=nslot, t=t8: e.transpose(
                out=pst[:, t * 128:(t + 1) * 128], in_=n[:, t * 128:(t + 1) * 128],
                identity=ident_bf[:]), inc="tS")
        q_tp.append(prog.cnt["tS"])
        W("scalar", "tS", prog.cnt["tS"])
        for t8 in range(8):
            E("scalar", lambda e, t=t8, k=kch: e.copy(
                out=xnT[:, t * NTOK + k * 128: t * NTOK + (k + 1) * 128],
                in_=pst[:, t * 128:(t + 1) * 128]), inc="aS")
        q_ev = prog.cnt["aS"]
    xq_done = prog.drain("vector", inc="vS")
    xq_evac_done = prog.drain("scalar", inc="aS")
    if "xq" in dbg:
        W("gpsimd", "vS", xq_done)
        DMA("gpsimd", dbg["xq"], xnT[:])

    if phases < 5:
        finish()
        return nc

    C_GM2P = 280
    p3_tS = prog.cnt["tS"]
    p3_aS = prog.cnt["aS"]
    wg_slab_done = []
    spill_done = [0, 0]
    stt_u = []
    o_dma_last = max(attn_st["o_dma"])
    for ic in range(NIC):
        sl = ic % 2
        if ic == 0:
            W("gpsimd", "dG", o_dma_last)
        if ic >= 2:
            W("gpsimd", "tS", wg_slab_done[ic - 2])
        lg = DMA("gpsimd", wgi_sb[sl][:],
                 bass.AP(wg_park.tensor, ic * 128, [[DFF, 128], [128 * DFF, 8], [1, 128]]))
        lu = prog.dma_handoff("gpsimd", wui_sb[sl][:],
                 bass.AP(wu_park.tensor, ic * 128, [[DFF, 128], [128 * DFF, 8], [1, 128]]))
        prog.hwait("tensor", lu)
        if ic == 0:
            W("tensor", "vS", xq_done)
            W("tensor", "aS", xq_evac_done)
        for tt in range(8):
            pg = ps_a[:, (tt % 2) * 512:(tt % 2) * 512 + 512]
            pu = ps_a[:, 1024 + (tt % 2) * 512:1024 + (tt % 2) * 512 + 512]
            if len(stt_u) >= 2:
                W("tensor", "vS", stt_u[-2])
            for ifc in range(8):
                E("tensor", lambda e, p=pg, w=wgi_sb[sl], i=ifc, t=tt: e.matmul(
                    out=p[:], lhsT=w[:, i * 128:(i + 1) * 128],
                    rhs=xnT[:, i * NTOK + t * 512:i * NTOK + (t + 1) * 512],
                    start=(i == 0), stop=(i == 7)), inc="tS" if ifc == 7 else None)
            gm = prog.cnt["tS"]
            for ifc in range(8):
                E("tensor", lambda e, p=pu, w=wui_sb[sl], i=ifc, t=tt: e.matmul(
                    out=p[:], lhsT=w[:, i * 128:(i + 1) * 128],
                    rhs=xnT[:, i * NTOK + t * 512:i * NTOK + (t + 1) * 512],
                    start=(i == 0), stop=(i == 7)), inc="tS" if ifc == 7 else None)
            um = prog.cnt["tS"]
            W("vector", "tS", gm)
            if ic >= 1 and tt == 0:
                W("vector", "dG", spill_done[(ic + 1) % 2])
            E("vector", lambda e, p=pg, t=tt, ic=ic: e.tensor_scalar(
                out=gate_f32[:, t * 512:(t + 1) * 512], in0=p[:],
                scalar1=S[:, C_CBG + ic:C_CBG + ic + 1], scalar2=None,
                op0=AluOpType.mult))
            ev = prog.drain("vector", inc="vS")
            W("scalar", "vS", ev)
            E("scalar", lambda e, t=tt: e.activation(
                out=gup_f32[:, t * 512:(t + 1) * 512],
                in_=gate_f32[:, t * 512:(t + 1) * 512], func=ACT.Silu))
            sv2 = prog.drain("scalar", inc="aS")
            W("vector", "aS", sv2)
            W("vector", "tS", um)
            sg = E("vector", lambda e, p=pu, t=tt, ic=ic: e.scalar_tensor_tensor(
                out=gate_f32[:, t * 512:(t + 1) * 512], in0=p[:],
                scalar=S[:, C_CBU + ic:C_CBU + ic + 1],
                in1=gup_f32[:, t * 512:(t + 1) * 512],
                op0=AluOpType.mult, op1=AluOpType.mult), inc="vS")
            stt_u.append(sg)
        wg_slab_done.append(prog.cnt["tS"])
        prog.drain("vector")
        am = E("vector", lambda e, ic=ic: e.tensor_reduce(
            out=S[:, C_GM2P + ic:C_GM2P + ic + 1], in_=gate_f32[:], axis=AX,
            op=AluOpType.max, apply_absolute_value=True), inc="vS")
        W("gpsimd", "vS", am)
        if ic == NIC - 1:
            spill_h = prog.dma_handoff("gpsimd", g_spill[ic * 128:(ic + 1) * 128, :], gate_f32[:])
            spill_done[ic % 2] = prog.cnt["dG"]
        else:
            spill_done[ic % 2] = DMA("gpsimd", g_spill[ic * 128:(ic + 1) * 128, :], gate_f32[:])

    snap("smC")
    p5_vflush = prog.drain("vector", inc="vS")
    prog.drain("vector")
    E("vector", lambda e: e.tensor_reduce(
        out=Sc(C_GMAX2), in_=S[:, C_GM2P:C_GM2P + NIC], axis=AX, op=AluOpType.max))
    g2 = prog.drain("vector", inc="vS")
    W("gpsimd", "vS", g2)
    cg1 = DMA("gpsimd", cc2_in[0:1, 0:128], Sc(C_GMAX2))
    W("gpsimd", "dG", cg1)
    ccd2 = E("gpsimd", lambda e: e.collective_compute(
        "AllReduce", AluOpType.max, replica_groups=[list(range(NCORES))],
        ins=[cc2_in[0:1, 0:128].opt()], outs=[cc2_out[0:1, 0:128].opt()]), inc="gS")
    W("gpsimd", "gS", ccd2)
    cg2 = prog.dma_handoff("gpsimd", S[:, C_BCR:C_BCR + 128],
                           bass.AP(cc2_out.tensor, 0, [[0, 128], [1, 128]]))
    if "cc2i" in dbg:
        DMA("gpsimd", dbg["cc2i"], cc2_in)
        DMA("gpsimd", dbg["cc2o"], cc2_out)
    prog.hwait("vector", cg2)
    E("vector", lambda e: e.tensor_reduce(
        out=Sc(C_S2), in_=S[:, C_BCR:C_BCR + 128], axis=AX, op=AluOpType.max))
    prog.drain("vector")
    E("vector", lambda e: e.tensor_tensor(
        out=Sc(C_S2), in0=Sc(C_S2), in1=Sc(C_SIGD), op=AluOpType.mult))
    prog.drain("vector")
    E("vector", lambda e: e.tensor_scalar(
        out=Sc(C_S2), in0=Sc(C_S2), scalar1=1.0 / QMAX, scalar2=None, op0=AluOpType.mult))
    prog.drain("vector")
    E("vector", lambda e: e.reciprocal(out=Sc(C_RS2), in_=Sc(C_S2)))
    prog.drain("vector")
    E("vector", lambda e: e.tensor_scalar(
        out=S[:, C_CBD:C_CBD + 8], in0=S[:, C_SWD:C_SWD + 8],
        scalar1=Sc(C_S2), scalar2=None, op0=AluOpType.mult))
    s2_done = prog.drain("vector", inc="vS")

    W("gpsimd", "tS", p3_tS)
    W("gpsimd", "vS", xq_done)
    for c in range(32):
        DMA("gpsimd", w2T_sb[:, c * 1024:(c + 1) * 1024], din["w2T"][c * 128:(c + 1) * 128, :])
    w2_loaded = prog.dma_handoff("gpsimd", w2T_sb[:, 0:16], din["w2T"][0:128, 0:16])
    _USE_SPLIT = False
    W("gpsimd", "tS", wg_slab_done[-1])
    for c in range(0, 7):
        DMA("gpsimd", w1T_sb[:, c * 4096:(c + 1) * 4096], din["w1T"][c * 128:(c + 1) * 128, :])
    w1_loaded = prog.dma_handoff("gpsimd", w1T_sb[:, 7 * 4096:8 * 4096], din["w1T"][7 * 128:8 * 128, :])

    o_ld = [0, 0]
    ab_st = [0, 0]
    mlp_of2_done = 0
    h1_done = 0
    W("sync", "dG", o_dma_last)
    W("sync", "vS", p5_vflush)
    prog.hwait("sync", spill_h)
    for t8 in range(8):
        sl = t8 % 2
        if t8 >= 2:
            W("sync", "tS", o_ld[sl])
        od = prog.dma_handoff("sync", o_sb2[sl][:],
                 bass.AP(o_park.tensor, t8 * 512, [[NTOK, 128], [128 * NTOK, 8], [1, 512]]))
        prog.hwait("tensor", od)
        if t8 == 0:
            prog.hwait("tensor", w1_loaded)
            W("tensor", "vS", p5_vflush)
            prog.hwait("tensor", spill_h)
        if t8 >= 1:
            W("tensor", "aS", mlp_of2_done)
        silus = []
        for of1 in range(32):
            p = ps_a[:, (of1 % 4) * 512:(of1 % 4) * 512 + 512]
            if of1 >= 4:
                W("tensor", "aS", silus[of1 - 4])
            for ifc in range(8):
                w1src = w1T_sb
                E("tensor", lambda e, p=p, c=of1, i=ifc, sl=sl, w=w1src: e.matmul(
                    out=p[:], lhsT=w[:, i * 4096 + c * 128:i * 4096 + (c + 1) * 128],
                    rhs=o_sb2[sl][:, i * 512:(i + 1) * 512],
                    start=(i == 0), stop=(i == 7)), inc="tS" if ifc == 7 else None)
            W("scalar", "tS", prog.cnt["tS"])
            sv2 = E("scalar", lambda e, p=p, c=of1: e.activation(
                out=h1_sb[:, c * 512:(c + 1) * 512], in_=p[:], func=ACT.Silu), inc="aS")
            silus.append(sv2)
        o_ld[sl] = prog.cnt["tS"]
        h1_land = prog.drain("scalar", inc="aS")
        W("tensor", "aS", h1_land)
        if t8 == 0:
            prog.hwait("tensor", w2_loaded)
        ab_ev = []
        for of2 in range(8):
            p = ps_b[:, (of2 % 3) * 512:(of2 % 3) * 512 + 512]
            if of2 >= 3:
                W("tensor", "aS", ab_ev[of2 - 3])
            for c in range(32):
                E("tensor", lambda e, p=p, c=c, o=of2: e.matmul(
                    out=p[:], lhsT=w2T_sb[:, c * 1024 + o * 128:c * 1024 + (o + 1) * 128],
                    rhs=h1_sb[:, c * 512:(c + 1) * 512],
                    start=(c == 0), stop=(c == 31)), inc="tS" if c == 31 else None)
            W("scalar", "tS", prog.cnt["tS"])
            asl = of2 % 2
            if ab_st[asl]:
                W("scalar", "dA", ab_st[asl])
            E("scalar", lambda e, p=p, asl=asl: e.copy(
                out=ab_evac[asl][:], in_=p[:]))
            ae = prog.drain("scalar", inc="aS")
            ab_ev.append(ae)
            ab_st[asl] = prog.dma("scalar",
                bass.AP(abuf.tensor, of2 * 128 * NTOK + t8 * 512, [[NTOK, 128], [1, 512]]),
                ab_evac[asl][:])
        mlp_of2_done = prog.cnt["aS"]
        h1_done = prog.cnt["tS"]

    if phases < 6:
        finish()
        return nc

    W("sync", "dG", prog.cnt["dG"])
    W("sync", "tS", h1_done)
    for c in range(NIC - 1):
        DMA("sync", wdT_sb[:, c * 1024:(c + 1) * 1024], wd_park[c * 128:(c + 1) * 128, :])
    wd_loaded = prog.dma_handoff("sync", wdT_sb[:, (NIC - 1) * 1024:NIC * 1024],
                                 wd_park[(NIC - 1) * 128:NIC * 128, :])
    W("vector", "vS", s2_done)

    q_done = [[0] * NIC, [0] * NIC]
    x_ld = [0] * 4
    ab_ld = [0, 0]
    merge_done = [0, 0]
    out_dma = [0, 0, 0, 0]
    dps_ev = []
    dps_done = [0, 0]
    hbufs = [h_int, h_int2]

    def emit_hquant(t8, ic0, ic1):
        hb = hbufs[t8 % 2]
        for ic in range(ic0, min(ic1, NIC)):
            sl = ic % 2
            prev = q_done[t8 % 2][ic - 2] if ic >= 2 else q_done[(t8 - 1) % 2][NIC - 2 + ic]
            if prev:
                W("sync", "vS", prev)
            gl = prog.dma_handoff("sync", g_slab[sl][:],
                     bass.AP(g_spill.tensor, ic * 128 * NTOK + t8 * 512, [[NTOK, 128], [1, 512]]))
            prog.hwait("vector", gl)
            if ic == 0 and dps_done[t8 % 2]:
                W("vector", "tS", dps_done[t8 % 2])
            E("vector", lambda e, sl=sl: e.tensor_scalar(
                out=qint6[:], in0=g_slab[sl][:], scalar1=Sc(C_RS2),
                scalar2=None, op0=AluOpType.mult))
            E("vector", lambda e: e.tensor_scalar(
                out=qint6[:], in0=qint6[:], scalar1=-128.0,
                scalar2=127.0, op0=AluOpType.max, op1=AluOpType.min))
            E("vector", lambda e, ic=ic, hb=hb: e.tensor_copy(
                out=hb[:, ic * 512:(ic + 1) * 512], in_=qint6[:]))
            q_done[t8 % 2][ic] = prog.drain("vector", inc="vS")

    emit_hquant(0, 0, NIC)
    for t8 in range(8):
        hb = hbufs[t8 % 2]
        hq_all = q_done[t8 % 2][NIC - 1]
        for tc in range(4):
            if out_dma[tc]:
                W("sync", "dG", out_dma[tc])
            x_ld[tc] = prog.dma_handoff("sync", x_sb[tc][:],
                           x[(t8 * 4 + tc) * 128:(t8 * 4 + tc + 1) * 128, :])
        W("tensor", "vS", hq_all)
        if t8 == 0:
            prog.hwait("tensor", wd_loaded)
        tadd = [0] * 8
        for of in range(8):
            if t8 < 7:
                emit_hquant(t8 + 1, of * 3, of * 3 + 3)
            dps = ps_a[:, (of % 2) * 512:(of % 2) * 512 + 512]
            if len(dps_ev) >= 2:
                W("tensor", "vS", dps_ev[-2])
            for ic in range(NIC):
                E("tensor", lambda e, p=dps, ic=ic, o=of, hb=hb: e.matmul(
                    out=p[:], lhsT=wdT_sb[:, ic * 1024 + o * 128:ic * 1024 + (o + 1) * 128],
                    rhs=hb[:, ic * 512:(ic + 1) * 512],
                    start=(ic == 0), stop=(ic == NIC - 1)), inc="tS" if ic == NIC - 1 else None)
            dmm = prog.cnt["tS"]
            asl = of % 2
            if ab_ld[asl]:
                W("scalar", "vS", merge_done[asl])
            ab_ld[asl] = prog.dma_handoff("scalar", ab_slab[asl][:],
                bass.AP(abuf.tensor, of * 128 * NTOK + t8 * 512, [[NTOK, 128], [1, 512]]))
            W("vector", "tS", dmm)
            prog.hwait("vector", ab_ld[asl])
            E("vector", lambda e, p=dps, o=of, asl=asl: e.scalar_tensor_tensor(
                out=sum_sb[asl][:], in0=p[:], scalar=S[:, C_CBD + o:C_CBD + o + 1],
                in1=ab_slab[asl][:], op0=AluOpType.mult, op1=AluOpType.add))
            mg = prog.drain("vector", inc="vS")
            dps_ev.append(mg)
            merge_done[asl] = mg
            W("tensor", "vS", mg)
            tbank = ps_b[:, (of % 2) * 512:(of % 2) * 512 + 512]
            if of >= 2:
                W("tensor", "vS", tadd[of - 2])
            for tc in range(4):
                E("tensor", lambda e, tb=tbank, tc=tc, asl=asl: e.transpose(
                    out=tb[:, tc * 128:(tc + 1) * 128],
                    in_=sum_sb[asl][:, tc * 128:(tc + 1) * 128],
                    identity=ident_f32[:]), inc="tS" if tc == 3 else None)
            ttp = prog.cnt["tS"]
            W("vector", "tS", ttp)
            for tc in range(4):
                prog.hwait("vector", x_ld[tc])
                E("vector", lambda e, tb=tbank, tc=tc, o=of: e.tensor_tensor(
                    out=out_sb[tc][:, o * 128:(o + 1) * 128],
                    in0=tb[:, tc * 128:(tc + 1) * 128],
                    in1=x_sb[tc][:, o * 128:(o + 1) * 128], op=AluOpType.add),
                    inc="vS" if tc == 3 else None)
            tadd[of] = prog.cnt["vS"]
        dps_done[t8 % 2] = prog.cnt["tS"]
        all_adds = prog.drain("vector", inc="vS")
        W("gpsimd", "vS", all_adds)
        for tc in range(4):
            out_dma[tc] = DMA("gpsimd", y[(t8 * 4 + tc) * 128:(t8 * 4 + tc + 1) * 128, :],
                              out_sb[tc][:])

    finish()
    return nc


def _emit_attention(ctx):
    (prog, s, q_rope, k_rope, v_tm, probs_tm, probs_T, o_sb, o_park,
     mask_main, mask_first, ident_bf, ps_a, ps_b, S,
     C_MPACK, C_DPACK, C_RDPACK, rope_done, v_done, st) = ctx
    E, W, DMA = prog.emit, prog.wait, prog.dma
    PS = [ps_a, ps_b]
    PSB = [ps_a.bitcast(mybir.dt.bfloat16), ps_b.bitcast(mybir.dt.bfloat16)]
    T_ = T
    NK = 32

    def half_info(k):
        return k // 2, (k % 2) * 8

    for k in range(NK + 2):
        if k < NK:
            qc, h0 = half_info(k)
            ps = PS[k % 2]
            W("tensor", "vS", rope_done)
            if k >= 3:
                W("tensor", "aS", st["oev"][k - 3])
            if k >= 2:
                W("tensor", "aS", st["exp"][k - 2])
            else:
                W("tensor", "aS", v_done)
            for ih in range(8):
                h = h0 + ih
                j, g = h % 4, h // 4
                tile = ps[:, ih * 256:(ih + 1) * 256]
                qx1 = q_rope[32 * g:32 * g + 32, j * T_ + qc * 128: j * T_ + (qc + 1) * 128]
                qx2 = q_rope[32 * g:32 * g + 32, (4 + j) * T_ + qc * 128: (4 + j) * T_ + (qc + 1) * 128]
                if qc == 0:
                    kx1 = k_rope[32 * g:32 * g + 32, 0:128]
                    kx2 = k_rope[32 * g:32 * g + 32, T_:T_ + 128]
                    out = tile[:, 128:256]
                    mask = mask_first
                else:
                    kx1 = k_rope[32 * g:32 * g + 32, (qc - 1) * 128:(qc + 1) * 128]
                    kx2 = k_rope[32 * g:32 * g + 32, T_ + (qc - 1) * 128:T_ + (qc + 1) * 128]
                    out = tile
                    mask = mask_main
                E("tensor", lambda e, o=out, q=qx1, kk=kx1, g=g: e.matmul(
                    out=o, lhsT=q, rhs=kk, start=True, stop=False, tile_position=(32 * g, 0)))
                E("tensor", lambda e, o=out, q=qx2, kk=kx2, g=g: e.matmul(
                    out=o, lhsT=q, rhs=kk, start=False, stop=False, tile_position=(32 * g, 0)))
                E("tensor", lambda e, t=tile, m=mask: e.matmul(
                    out=t, lhsT=ident_bf[:], rhs=m[:], start=False, stop=True,
                    tile_position=(0, 0)), inc="tS" if ih == 7 else None)
            st["stats"].append(prog.cnt["tS"])
            W("vector", "tS", st["stats"][k])
            mp = C_MPACK + 8 * (k % 2)
            dp = C_DPACK + 8 * (k % 2)
            rp = C_RDPACK + 8 * (k % 2)
            for ih in range(8):
                E("vector", lambda e, t=ps[:, ih * 256:(ih + 1) * 256], c=mp + ih: e.tensor_reduce(
                    out=S[:, c:c + 1], in_=t, axis=AX, op=AluOpType.max, negate=True))
            st["red"].append(prog.drain("vector", inc="vS"))
            W("scalar", "vS", st["red"][k])
            if k >= 2:
                prog.hwait("scalar", st["tp"][k - 2][0])
                prog.hwait("scalar", st["tp"][k - 2][1])
            for ih in range(8):
                E("scalar", lambda e, t=ps[:, ih * 256:(ih + 1) * 256], ih=ih, c=mp + ih,
                  d=dp + ih, k=k: e.activation(
                    out=probs_tm[:, (k % 2) * 2048 + ih * 256:(k % 2) * 2048 + (ih + 1) * 256],
                    in_=t, func=ACT.Exp, bias=S[:, c:c + 1], accum_out=S[:, d:d + 1]))
            st["exp"].append(prog.drain("scalar", inc="aS"))
            W("vector", "aS", st["exp"][k])
            E("vector", lambda e, rp=rp, dp=dp: e.reciprocal(
                out=S[:, rp:rp + 8], in_=S[:, dp:dp + 8]))
            prog.drain("vector")
            lo = 128 if qc == 0 else 0
            for ih in range(8):
                E("vector", lambda e, ih=ih, k=k, lo=lo, rp=rp: e.tensor_scalar(
                    out=probs_tm[:, (k % 2) * 2048 + ih * 256 + lo:(k % 2) * 2048 + (ih + 1) * 256],
                    in0=probs_tm[:, (k % 2) * 2048 + ih * 256 + lo:(k % 2) * 2048 + (ih + 1) * 256],
                    scalar1=S[:, rp + ih:rp + ih + 1], scalar2=None, op0=AluOpType.mult))
            st["norm"].append(prog.drain("vector", inc="vS"))

        if k >= 2:
            j = k - 2
            qc, h0 = half_info(j)
            ava = PS[(j + 1) % 2]
            W("tensor", "aS", st["exp"][min(j + 1, NK - 1)])
            prog.hwait("tensor", st["tp"][j][0])
            prog.hwait("tensor", st["tp"][j][1])
            W("tensor", "aS", v_done)
            for ih in range(8):
                h = h0 + ih
                g = h // 4
                p = ih // 2
                cg = 64 * (h % 2)
                out = ava[cg:cg + 64, 1024 + p * 128:1024 + (p + 1) * 128]
                if qc == 0:
                    E("tensor", lambda e, o=out, g=g, cg=cg, ih=ih, j=j: e.matmul(
                        out=o, lhsT=v_tm[:, 64 * g: 64 * g + 64],
                        rhs=probs_T[:, (j % 2) * 2048 + (2 * ih + 1) * 128:(j % 2) * 2048 + (2 * ih + 2) * 128],
                        start=True, stop=True, tile_position=(0, cg)), inc="tS" if ih == 7 else None)
                else:
                    E("tensor", lambda e, o=out, g=g, cg=cg, ih=ih, j=j, qc=qc: e.matmul(
                        out=o, lhsT=v_tm[:, (qc - 1) * 256 + 64 * g: (qc - 1) * 256 + 64 * g + 64],
                        rhs=probs_T[:, (j % 2) * 2048 + 2 * ih * 128:(j % 2) * 2048 + (2 * ih + 1) * 128],
                        start=True, stop=False, tile_position=(0, cg)))
                    E("tensor", lambda e, o=out, g=g, cg=cg, ih=ih, j=j, qc=qc: e.matmul(
                        out=o, lhsT=v_tm[:, qc * 256 + 64 * g: qc * 256 + 64 * g + 64],
                        rhs=probs_T[:, (j % 2) * 2048 + (2 * ih + 1) * 128:(j % 2) * 2048 + (2 * ih + 2) * 128],
                        start=False, stop=True, tile_position=(0, cg)), inc="tS" if ih == 7 else None)
            st["av"].append(prog.cnt["tS"])
            W("scalar", "tS", st["av"][j])
            slot = o_sb[qc % 2]
            if j % 2 == 0 and st["o_dma"][qc % 2]:
                W("scalar", "dG", st["o_dma"][qc % 2])
            for p in range(4):
                pa = 4 * (j % 2) + p
                E("scalar", lambda e, pa=pa, p=p, sl=slot, ava=ava: e.copy(
                    out=sl[:, pa * 128:(pa + 1) * 128],
                    in_=ava[:, 1024 + p * 128:1024 + (p + 1) * 128]))
            st["oev"].append(prog.drain("scalar", inc="aS"))
            if j % 2 == 1:
                W("gpsimd", "aS", st["oev"][j])
                st["o_dma"][qc % 2] = DMA(
                    "gpsimd",
                    bass.AP(o_park.tensor, s * T_ + qc * 128,
                            [[NTOK, 128], [128 * NTOK, 8], [1, 128]]),
                    slot[:])
        if k >= 1 and k - 1 < NK:
            kb = k - 1
            qc, h0 = half_info(kb)
            sides = (1,) if qc == 0 else (0, 1)
            for eng in ("sync", "scalar"):
                W(eng, "vS", st["norm"][kb])
                if kb >= 2:
                    W(eng, "tS", st["av"][kb - 2])
            i = 0
            for ih in range(8):
                for side in sides:
                    ti = 2 * ih + side
                    eng = "sync" if i % 2 == 0 else "scalar"
                    DMA(eng,
                        probs_T[:, (kb % 2) * 2048 + ti * 128:(kb % 2) * 2048 + (ti + 1) * 128],
                        probs_tm[:, (kb % 2) * 2048 + ih * 256 + side * 128:
                                 (kb % 2) * 2048 + ih * 256 + side * 128 + 128],
                        transpose=True)
                    i += 1
            hs1 = prog.queue_handshake("sync")
            hs2 = prog.queue_handshake("scalar")
            st["tp"].append((hs1, hs2))
    st["last_oev"] = st["oev"][-1]
    st["last_stats"] = st["stats"][-1]
    for key in ("stats", "red", "exp", "norm", "tp", "ev", "av", "oev"):
        st[key] = []



def kernel(**inputs):
    xfull = np.asarray(inputs["L0_attn_query"], np.float32)
    prep = host_prep(inputs)
    nc = build_nc(phases=6)
    in_maps = []
    for c in range(NCORES):
        m = {"x": np.ascontiguousarray(xfull[2 * c:2 * c + 2].reshape(NTOK, D))}
        m.update(prep)
        in_maps.append(m)
    res = run_bass_kernel_spmd(nc, in_maps, core_ids=list(range(NCORES)))
    return np.concatenate([r["y"].reshape(2, T, D) for r in res.results], axis=0)


# version 18
# speedup vs baseline: 1.1025x; 1.0059x over previous
import numpy as np
import concourse.bass as bass
import concourse.mybir as mybir
from concourse.alu_op_type import AluOpType
from concourse.bass_utils import run_bass_kernel_spmd

F32 = mybir.dt.float32
BF16 = mybir.dt.bfloat16
I32 = mybir.dt.int32
ACT = mybir.ActivationFunctionType
AX = mybir.AxisListType.X

D = 1024
T = 2048
NSEQ = 2
NTOK = NSEQ * T
NH = 16
HD = 64
DFF = 2816
DMLP = 4096
NCHUNK = NTOK // 128
NIC = DFF // 128
QMAX = 127.0
NCORES = 8

ENGINES = ("sync", "tensor", "vector", "scalar", "gpsimd")



def _perm_q():
    perm = np.zeros(D, dtype=np.int64)
    for half in range(2):
        for j in range(4):
            for g in range(4):
                for i in range(32):
                    perm[half * 512 + j * 128 + g * 32 + i] = (4 * g + j) * 64 + half * 32 + i
    return perm


def _perm_k():
    perm = np.zeros(256, dtype=np.int64)
    for half in range(2):
        for g in range(4):
            for i in range(32):
                perm[half * 128 + g * 32 + i] = g * 64 + half * 32 + i
    return perm


def host_prep(inputs):
    import ml_dtypes
    d = {}
    wq = np.asarray(inputs["L0_attn_Wq"], np.float32)
    wk = np.asarray(inputs["L0_attn_Wk"], np.float32)
    wv = np.asarray(inputs["L0_attn_Wv"], np.float32)
    d["wqT"] = np.ascontiguousarray(wq[_perm_q(), :].T)
    d["wkT"] = np.ascontiguousarray(wk[_perm_k(), :].T)
    d["wvT"] = np.ascontiguousarray(wv.T)
    d["w1T"] = np.ascontiguousarray(np.asarray(inputs["L0_attn_W1"], np.float32).T)
    d["w2T"] = np.ascontiguousarray(np.asarray(inputs["L0_attn_W2"], np.float32).T)
    d["wg_n"] = np.asarray(inputs["L0_ffn_w_gate"], np.float32)
    d["wu_n"] = np.asarray(inputs["L0_ffn_w_up"], np.float32)
    d["wd_n"] = np.asarray(inputs["L0_ffn_w_down"], np.float32)
    d["wgT"] = np.ascontiguousarray(d["wg_n"].T)
    d["wuT"] = np.ascontiguousarray(d["wu_n"].T)
    d["wdT"] = np.ascontiguousarray(d["wd_n"].T)
    d["clip_x"] = np.asarray(inputs["L0_ffn_clip_x"], np.float32).reshape(1, 1)
    d["clip_down"] = np.asarray(inputs["L0_ffn_clip_down"], np.float32).reshape(1, 1)

    inv = 1.0 / 10000.0 ** (np.arange(0, HD, 2, dtype=np.float64) / HD)
    fr = np.outer(inv, np.arange(T, dtype=np.float64))
    d["cos_t"] = np.tile(np.cos(fr), (4, 1)).astype(np.float32)
    d["sin_t"] = np.tile(np.sin(fr), (4, 1)).astype(np.float32)

    q = np.arange(128)[:, None]
    kk = np.arange(256)[None, :]
    NEG = np.float32(-1e30)
    allow_main = np.where(kk < 128, kk >= q + 1, (kk - 128) <= q)
    allow_first = np.where(kk < 128, False, (kk - 128) <= q)
    d["mask_main"] = np.where(allow_main, np.float32(0), NEG).astype(ml_dtypes.bfloat16)
    d["mask_first"] = np.where(allow_first, np.float32(0), NEG).astype(ml_dtypes.bfloat16)
    d["ident_bf"] = np.eye(128, dtype=ml_dtypes.bfloat16)
    d["ident_f32"] = np.eye(128, dtype=np.float32)
    return d



class Prog:
    def __init__(self, nc):
        self.nc = nc
        self.items = {e: [] for e in ENGINES}
        self.sems = {}
        self.cnt = {}

    def sem(self, name):
        h = self.nc.semaphore(name).__enter__()
        self.sems[name] = h
        self.cnt[name] = 0

    def emit(self, engine, fn, inc=None, inc_amt=1):
        if inc is not None:
            sem = self.sems[inc]
            self.cnt[inc] += inc_amt

            def wrapped(eng, fn=fn, sem=sem, amt=inc_amt):
                fn(eng).then_inc(sem, amt)
            self.items[engine].append(wrapped)
        else:
            self.items[engine].append(fn)
        return self.cnt[inc] if inc is not None else None

    def wait(self, engine, sem_name, value=None):
        if value is None:
            value = self.cnt[sem_name]
        sem = self.sems[sem_name]
        self.items[engine].append(lambda eng, sem=sem, v=value: eng.wait_ge(sem, v))

    def vnop(self):
        self.items["vector"].append(lambda eng: eng.nop())

    HSEM = {"sync": "hSs", "scalar": "hSa", "gpsimd": "hSg"}

    def dma_handoff(self, engine, out, in_, **kw):
        c = self.dma(engine, out, in_, **kw)
        qsem = {"sync": "dS", "scalar": "dA", "gpsimd": "dG"}[engine]
        hs = self.HSEM[engine]
        self.wait(engine, qsem, c)
        self.emit(engine, lambda e, hs=hs: e.sem_inc(self.sems[hs], 1), inc=None)
        return (hs, self._bump(hs))

    def hwait(self, engine, h):
        self.wait(engine, h[0], h[1])

    def queue_handshake(self, engine):
        qsem = {"sync": "dS", "scalar": "dA", "gpsimd": "dG"}[engine]
        hs = self.HSEM[engine]
        self.wait(engine, qsem, self.cnt[qsem])
        self.emit(engine, lambda e, hs=hs: e.sem_inc(self.sems[hs], 1), inc=None)
        return (hs, self._bump(hs))

    def _bump(self, name):
        self.cnt[name] += 1
        return self.cnt[name]

    def drain(self, engine, inc=None):
        if inc is not None:
            return self.emit(engine, lambda e: e.drain(), inc=inc)
        self.items[engine].append(lambda eng: eng.drain())
        return None

    def dma(self, engine, out, in_, sem=None, **kw):
        if sem is None:
            sem = {"sync": "dS", "scalar": "dA", "gpsimd": "dG"}[engine]
        return self.emit(engine, lambda e: e.dma_start(out=out, in_=in_, **kw),
                         inc=sem, inc_amt=16)

    def replay(self, block):
        @block.sync
        def _(eng):
            for fn in self.items["sync"]:
                fn(eng)

        @block.tensor
        def _(eng):
            for fn in self.items["tensor"]:
                fn(eng)

        @block.vector
        def _(eng):
            for fn in self.items["vector"]:
                fn(eng)

        @block.scalar
        def _(eng):
            for fn in self.items["scalar"]:
                fn(eng)

        @block.gpsimd
        def _(eng):
            for fn in self.items["gpsimd"]:
                fn(eng)



def build_nc(phases=6, debug=()):
    nc = bass.Bass("TRN2", target_bir_lowering=False)

    x = nc.dram_tensor("x", [NTOK, D], F32, kind="ExternalInput").ap()
    y = nc.dram_tensor("y", [NTOK, D], F32, kind="ExternalOutput").ap()
    din = {}
    for name, shape in [
        ("wqT", [D, D]), ("wkT", [D, 256]), ("wvT", [D, 256]),
        ("w1T", [D, DMLP]), ("w2T", [DMLP, D]),
        ("wg_n", [DFF, D]), ("wu_n", [DFF, D]), ("wd_n", [D, DFF]),
        ("wgT", [D, DFF]), ("wuT", [D, DFF]), ("wdT", [DFF, D]),
        ("clip_x", [1, 1]), ("clip_down", [1, 1]),
        ("cos_t", [128, T]), ("sin_t", [128, T]),
    ]:
        din[name] = nc.dram_tensor(name, shape, F32, kind="ExternalInput").ap()
    for name, shape in [("mask_main", [128, 256]), ("mask_first", [128, 256]),
                        ("ident_bf", [128, 128])]:
        din[name] = nc.dram_tensor(name, shape, BF16, kind="ExternalInput").ap()
    din["ident_f32"] = nc.dram_tensor("ident_f32", [128, 128], F32, kind="ExternalInput").ap()

    def scratch(name, shape, dt):
        kind = "ExternalOutput" if name in debug else "Internal"
        return nc.dram_tensor(name, shape, dt, kind=kind).ap()

    wg_park = scratch("wg_park", [D, DFF], BF16)
    wu_park = scratch("wu_park", [D, DFF], BF16)
    wd_park = scratch("wd_park", [DFF, D], BF16)
    g_spill = scratch("g_spill", [DFF, NTOK], F32)
    o_park = scratch("o_park", [D, NTOK], BF16)
    abuf = scratch("abuf", [D, NTOK], F32)
    scl_pack = nc.dram_tensor("scl_pack", [64, 128], F32).ap()
    cc1_in = nc.dram_tensor("cc1_in", [128, 128], F32).ap()
    cc1_out = nc.dram_tensor("cc1_out", [128, 128], F32).ap()
    cc2_in = nc.dram_tensor("cc2_in", [128, 128], F32).ap()
    cc2_out = nc.dram_tensor("cc2_out", [128, 128], F32).ap()
    dbg = {}
    for nm, shape, dt in [("smA", [128, 888], F32), ("smB", [128, 888], F32),
                          ("smC", [128, 888], F32),
                          ("cc1i", [128, 128], F32), ("cc1o", [128, 128], F32),
                          ("cc2i", [128, 128], F32), ("cc2o", [128, 128], F32),
                          ("recips", [128, DFF], F32), ("xnT", [128, 8 * NTOK], BF16), ("xq", [128, 8 * NTOK], BF16),
                          ("q", [128, 8 * T], BF16), ("k", [128, 2 * T], BF16),
                          ("v", [128, 16 * 256], BF16), ("smalls", [128, 888], F32)]:
        if nm in debug:
            dbg[nm] = nc.dram_tensor("dbg_" + nm, shape, dt, kind="ExternalOutput").ap()

    SB_BASE = 16512
    def sb(name, off, shape, dt):
        return nc.alloc_sbuf_tensor_at(name, shape, dt, offset=SB_BASE + off).ap()

    xnT = sb("xnT", 0, [128, 8 * NTOK], BF16)
    w1T_sb = sb("w1T_sb", 0, [128, 8 * DMLP], BF16)
    w1T_a = sb("w1T_a", 98304, [128, 4 * DMLP], BF16)
    wdT_sb = sb("wdT_sb", 0, [128, NIC * D], BF16)
    h_int = sb("h_int", 45056, [128, NIC * 512], BF16)
    h_int2 = sb("h_int2", 112640, [128, NIC * 512], BF16)
    qint6 = sb("qint6", 135168, [128, 512], I32)
    g_slab = [sb(f"g_slab{i}", 67584 + i * 2048, [128, 512], F32) for i in range(2)]
    ab_slab = [sb(f"ab_slab{i}", 71680 + i * 2048, [128, 512], F32) for i in range(2)]
    sum_sb = [sb(f"sum_sb{i}", 75776 + i * 2048, [128, 512], F32) for i in range(2)]
    out_sb = [sb(f"out_sb{i}", 79872 + i * 4096, [128, D], F32) for i in range(4)]
    x_sb = [sb(f"x_sb{i}", 96256 + i * 4096, [128, D], F32) for i in range(4)]

    w2T_sb = sb("w2T_sb", 65536, [128, 32 * D], BF16)
    wnat = [sb(f"wnat{i}", 65536 + i * 11264, [128, DFF], F32) for i in range(2)]
    wTb = [sb(f"wTb{i}", 88064 + i * 11264, [128, DFF], F32) for i in range(2)]
    wint32 = sb("wint32", 110592, [128, DFF], I32)
    wintbf = sb("wintbf", 121856, [128, DFF], BF16)
    recip_gb = sb("recip_gb", 127488, [128, DFF], F32)
    recip_ub = sb("recip_ub", 138752, [128, DFF], F32)
    recip_db = sb("recip_db", 150016, [128, D], F32)
    xbuf = [sb(f"xbuf{i}", 154112 + i * 4096, [128, D], F32) for i in range(2)]
    xn_tm = [sb(f"xn_tm{i}", 162304 + i * 2048, [128, D], BF16) for i in range(2)]
    q_rope = sb("q_rope", 65536, [128, 8 * T], BF16)
    k_rope = sb("k_rope", 98304, [128, 2 * T], BF16)
    v_tm = sb("v_tm", 106496, [128, 16 * 256], BF16)
    qraw = sb("qraw", 114688, [128, 10 * 512], F32)
    rope_t = [sb(f"rope_t{i}", 135168 + i * 2048, [128, 512], F32) for i in range(2)]
    wvT_sb = sb("wvT_sb", 150016, [128, 8 * 256], BF16)
    wqT_sb = sb("wqT_sb", 166400, [128, 8 * 1024], BF16)
    wkT_sb = sb("wkT_sb", 182784, [128, 8 * 256], BF16)
    sq_scr = sb("sq_scr", 186880, [128, D], F32)
    xq_int = sb("xq_int", 114688, [128, 4096], I32)
    probs_tm = sb("probs_tm", 114688, [128, NH * 256], BF16)
    probs_T = sb("probs_T", 122880, [128, NH * 256], BF16)
    o_sb = [sb(f"o_sb{i}", 131072 + i * 2048, [128, D], BF16) for i in range(2)]
    wgi_sb = [sb(f"wgi_sb{i}", 131072 + i * 2048, [128, 8 * 128], BF16) for i in range(2)]
    wui_sb = [sb(f"wui_sb{i}", 135168 + i * 2048, [128, 8 * 128], BF16) for i in range(2)]
    gate_f32 = sb("gate_f32", 139264, [128, 4096], F32)
    gup_f32 = sb("gup_f32", 155648, [128, 4096], F32)
    h1_sb = sb("h1_sb", 131072, [128, 32 * 512], BF16)
    o_sb2 = [sb(f"o_sb2{i}", 163840 + i * 8192, [128, 8 * 512], BF16) for i in range(2)]
    ab_evac = [sb(f"ab_evac{i}", 180224 + i * 2048, [128, 512], F32) for i in range(2)]

    PT = 191104
    cos_sb = sb("cos_sb", PT, [128, T], F32)
    sin_sb = sb("sin_sb", PT + 8192, [128, T], F32)
    mask_main = sb("mask_main_sb", PT + 16384, [128, 256], BF16)
    mask_first = sb("mask_first_sb", PT + 16896, [128, 256], BF16)
    ident_bf = sb("ident_bf_sb", PT + 17408, [128, 128], BF16)
    ident_f32 = sb("ident_f32_sb", PT + 17664, [128, 128], F32)
    S = sb("smalls", PT + 18176, [128, 888], F32)

    (C_GMAX1, C_MEAN, C_MSQ, C_VAR, C_RSTD, C_MX, C_MN, C_T0, C_T1, C_SUM,
     C_SQS, C_S1, C_RS1, C_SIGX, C_SIGD, C_GMAX2, C_S2, C_RS2, C_CLX,
     C_CLD) = range(20)
    C_MPACK, C_DPACK, C_RDPACK = 32, 48, 64
    C_SUMP, C_SQSP, C_MXP, C_MNP = 600, 632, 664, 696
    C_MEANP, C_MSQP, C_VARP, C_RSTDP, C_AMP = 728, 760, 792, 824, 856
    C_RPG, C_RPU, C_RPD = 96, 118, 140
    C_CBG, C_CBU, C_CBD = 192, 224, 256
    C_BCR = 384
    C_SWG, C_SWU, C_SWD = 512, 544, 576

    def Sc(col, n=1):
        return S[:, col:col + n]

    ps_a = nc.alloc_psum_tensor("ps_a", [128, 2048], F32).ap()
    ps_b = nc.alloc_psum_tensor("ps_b", [128, 2048], F32).ap()
    pst = ps_a.bitcast(BF16)[:, 0:1024]

    prog = Prog(nc)
    for s_ in ("dS", "dA", "dG", "vS", "aS", "tS", "gS", "hSs", "hSa", "hSg"):
        prog.sem(s_)
    E, W, DMA = prog.emit, prog.wait, prog.dma

    DMA("sync", cos_sb[:], din["cos_t"])
    DMA("sync", sin_sb[:], din["sin_t"])
    DMA("sync", mask_main[:], din["mask_main"])
    DMA("sync", mask_first[:], din["mask_first"])
    DMA("sync", ident_bf[:], din["ident_bf"])
    DMA("sync", ident_f32[:], din["ident_f32"])
    DMA("sync", Sc(C_CLX), bass.AP(din["clip_x"].tensor, 0, [[0, 128], [1, 1]]))
    idf_loaded = prog.cnt["dS"] + 16
    c0 = prog.dma_handoff("sync", Sc(C_CLD), bass.AP(din["clip_down"].tensor, 0, [[0, 128], [1, 1]]))
    prog.hwait("vector", c0)
    E("vector", lambda e: e.memset(Sc(C_GMAX1), 0.0))
    E("vector", lambda e: e.memset(Sc(C_GMAX2), 0.0), inc="vS")
    prog.hwait("scalar", c0)
    E("scalar", lambda e: e.activation(out=Sc(C_SIGX), in_=Sc(C_CLX), func=ACT.Sigmoid))
    sig_done = E("scalar", lambda e: e.activation(out=Sc(C_SIGD), in_=Sc(C_CLD), func=ACT.Sigmoid), inc="aS")

    for ifc in range(8):
        DMA("gpsimd", wqT_sb[:, ifc * 1024:(ifc + 1) * 1024], din["wqT"][ifc * 128:(ifc + 1) * 128, :])
    for ifc in range(7):
        DMA("gpsimd", wkT_sb[:, ifc * 256:(ifc + 1) * 256], din["wkT"][ifc * 128:(ifc + 1) * 128, :])
    wqk_loaded = prog.dma_handoff("gpsimd", wkT_sb[:, 7 * 256:8 * 256], din["wkT"][7 * 128:8 * 128, :])

    C_SWGN, C_SWUN, C_SWDN = 302, 324, 346
    wq_jobs = []
    for src, cS, cN, nchunk, width in ((din["wg_n"], C_SWG, C_SWGN, NIC, D),
                                       (din["wu_n"], C_SWU, C_SWUN, NIC, D),
                                       (din["wd_n"], C_SWD, C_SWDN, 8, DFF)):
        for c in range(nchunk):
            wq_jobs.append((src, cS + c, cN + c, c, width))
    nat_done = []
    for idx, (src, cc, ccn, c, width) in enumerate(wq_jobs):
        slot = wnat[idx % 2]
        if idx >= 2:
            W("sync", "vS", nat_done[idx - 2])
        cd = prog.dma_handoff("sync", slot[:, 0:width], src[c * 128:(c + 1) * 128, :])
        prog.hwait("vector", cd)
        nd = E("vector", lambda e, s_=slot, w=width, cc=cc: e.tensor_reduce(
            out=S[:, cc:cc + 1], in_=s_[:, 0:w], axis=AX, op=AluOpType.max,
            apply_absolute_value=True), inc="vS")
        nat_done.append(nd)
    prog.drain("vector")
    for base, n in ((C_SWG, NIC), (C_SWU, NIC), (C_SWD, 8)):
        E("vector", lambda e, b=base, n_=n: e.tensor_scalar(
            out=S[:, b:b + n_], in0=S[:, b:b + n_], scalar1=1.0 / QMAX,
            scalar2=None, op0=AluOpType.mult))
    prog.drain("vector")
    for dst, srcb, n in ((C_RPG, C_SWG, NIC), (C_RPU, C_SWU, NIC), (C_RPD, C_SWD, 8)):
        E("vector", lambda e, d=dst, s=srcb, n_=n: e.reciprocal(
            out=S[:, d:d + n_], in_=S[:, s:s + n_]))
    rp_done = prog.drain("vector", inc="vS")
    W("tensor", "vS", rp_done)
    W("tensor", "dS", idf_loaded)
    tp = E("tensor", lambda e: e.transpose(
        out=ps_b[0:64, 0:128], in_=S[:, C_RPG:C_RPG + 64], identity=ident_f32[:]), inc="tS")
    W("scalar", "tS", tp)
    E("scalar", lambda e: e.copy(
        out=S[0:64, C_BCR:C_BCR + 128], in_=ps_b[0:64, 0:128]))
    tpe = prog.drain("scalar", inc="aS")
    W("gpsimd", "aS", tpe)
    pk = DMA("gpsimd", scl_pack[0:64, :], S[0:64, C_BCR:C_BCR + 128])
    W("gpsimd", "dG", pk)
    DMA("gpsimd", recip_gb[:], bass.AP(scl_pack.tensor, 0, [[0, 128], [1, DFF]]))
    DMA("gpsimd", recip_ub[:], bass.AP(scl_pack.tensor, 22 * 128, [[0, 128], [1, DFF]]))
    bc_done = prog.dma_handoff("gpsimd", recip_db[:], bass.AP(scl_pack.tensor, 44 * 128, [[0, 128], [1, D]]))

    if "recips" in dbg:
        W("gpsimd", "dG", bc_done)
        DMA("gpsimd", dbg["recips"], recip_gb[:])
    wt_jobs = []
    for src, park, rb, nchunk, width in ((din["wgT"], wg_park, recip_gb, 8, DFF),
                                         (din["wuT"], wu_park, recip_ub, 8, DFF),
                                         (din["wdT"], wd_park, recip_db, NIC, D)):
        for c in range(nchunk):
            wt_jobs.append((src, park, rb, c, width))
    prog.hwait("sync", bc_done)
    t_done, park_done = [], []
    for idx, (src, park, rb, c, width) in enumerate(wt_jobs):
        slot = wTb[idx % 2]
        if idx >= 2:
            W("sync", "vS", t_done[idx - 2])
        cd = prog.dma_handoff("sync", slot[:, 0:width], src[c * 128:(c + 1) * 128, :])
        prog.hwait("vector", cd)
        if idx >= 1:
            W("vector", "dG", park_done[idx - 1])
        E("vector", lambda e, s=slot, r=rb, w=width: e.tensor_tensor(
            out=wint32[:, 0:w], in0=s[:, 0:w], in1=r[:, 0:w], op=AluOpType.mult))
        E("vector", lambda e, w=width: e.tensor_copy(
            out=wintbf[:, 0:w], in_=wint32[:, 0:w]))
        cv = prog.drain("vector", inc="vS")
        t_done.append(cv)
        W("gpsimd", "vS", cv)
        pd = DMA("gpsimd", park[c * 128:(c + 1) * 128, :], wintbf[:, 0:width])
        park_done.append(pd)

    pa_done = []
    pa_gs = []
    for kch in range(NCHUNK):
        slot = xbuf[kch % 2]
        if kch >= 2:
            W("sync", "vS", pa_done[kch - 2])
        cd = prog.dma_handoff("sync", slot[:], x[kch * 128:(kch + 1) * 128, :])
        prog.hwait("vector", cd)
        E("vector", lambda e, s=slot, k=kch: e.tensor_reduce(
            out=S[:, C_SUMP + k:C_SUMP + k + 1], in_=s[:], axis=AX, op=AluOpType.add))
        E("vector", lambda e, s=slot: e.tensor_tensor(
            out=sq_scr[:], in0=s[:], in1=s[:], op=AluOpType.mult))
        E("vector", lambda e, k=kch: e.tensor_reduce(
            out=S[:, C_SQSP + k:C_SQSP + k + 1], in_=sq_scr[:], axis=AX, op=AluOpType.add))
        E("vector", lambda e, s=slot, k=kch: e.tensor_reduce(
            out=S[:, C_MXP + k:C_MXP + k + 1], in_=s[:], axis=AX, op=AluOpType.max))
        nv = E("vector", lambda e, s=slot, k=kch: e.tensor_reduce(
            out=S[:, C_MNP + k:C_MNP + k + 1], in_=s[:], axis=AX, op=AluOpType.min), inc="vS")
        pa_done.append(nv)
        pa_gs.append(nv)
    DR = prog.drain
    E("vector", lambda e: e.tensor_scalar(
        out=S[:, C_MEANP:C_MEANP + 32], in0=S[:, C_SUMP:C_SUMP + 32],
        scalar1=1.0 / D, scalar2=None, op0=AluOpType.mult))
    E("vector", lambda e: e.tensor_scalar(
        out=S[:, C_MSQP:C_MSQP + 32], in0=S[:, C_SQSP:C_SQSP + 32],
        scalar1=1.0 / D, scalar2=None, op0=AluOpType.mult))
    DR("vector")
    E("vector", lambda e: e.tensor_tensor(
        out=S[:, C_VARP:C_VARP + 32], in0=S[:, C_MEANP:C_MEANP + 32],
        in1=S[:, C_MEANP:C_MEANP + 32], op=AluOpType.mult))
    DR("vector")
    E("vector", lambda e: e.tensor_tensor(
        out=S[:, C_VARP:C_VARP + 32], in0=S[:, C_MSQP:C_MSQP + 32],
        in1=S[:, C_VARP:C_VARP + 32], op=AluOpType.subtract))
    DR("vector")
    vv = DR("vector", inc="vS")
    E("vector", lambda e: e.tensor_scalar(
        out=S[:, C_VARP:C_VARP + 32], in0=S[:, C_VARP:C_VARP + 32],
        scalar1=1e-5, scalar2=None, op0=AluOpType.add))
    vv = DR("vector", inc="vS")
    W("scalar", "vS", vv)
    E("scalar", lambda e: e.activation(
        out=S[:, C_VARP:C_VARP + 32], in_=S[:, C_VARP:C_VARP + 32], func=ACT.Sqrt))
    sv = DR("scalar", inc="aS")
    W("vector", "aS", sv)
    E("vector", lambda e: e.reciprocal(
        out=S[:, C_RSTDP:C_RSTDP + 32], in_=S[:, C_VARP:C_VARP + 32]))
    DR("vector")
    E("vector", lambda e: e.tensor_tensor(
        out=S[:, C_MXP:C_MXP + 32], in0=S[:, C_MXP:C_MXP + 32],
        in1=S[:, C_MEANP:C_MEANP + 32], op=AluOpType.subtract))
    E("vector", lambda e: e.tensor_tensor(
        out=S[:, C_MNP:C_MNP + 32], in0=S[:, C_MEANP:C_MEANP + 32],
        in1=S[:, C_MNP:C_MNP + 32], op=AluOpType.subtract))
    DR("vector")
    E("vector", lambda e: e.tensor_tensor(
        out=S[:, C_AMP:C_AMP + 32], in0=S[:, C_MXP:C_MXP + 32],
        in1=S[:, C_MNP:C_MNP + 32], op=AluOpType.max))
    DR("vector")
    E("vector", lambda e: e.tensor_tensor(
        out=S[:, C_AMP:C_AMP + 32], in0=S[:, C_AMP:C_AMP + 32],
        in1=S[:, C_RSTDP:C_RSTDP + 32], op=AluOpType.mult))
    DR("vector")
    E("vector", lambda e: e.tensor_reduce(
        out=Sc(C_GMAX1), in_=S[:, C_AMP:C_AMP + 32], axis=AX, op=AluOpType.max))
    stats_done = DR("vector", inc="vS")

    W("sync", "vS", stats_done)
    pb_done = []
    tp_done = []
    ev_done = 0
    for kch in range(NCHUNK):
        slot = xbuf[kch % 2]
        nslot = xn_tm[kch % 2]
        if kch >= 2:
            W("sync", "vS", pb_done[kch - 2])
        cd = prog.dma_handoff("sync", slot[:], x[kch * 128:(kch + 1) * 128, :])
        prog.hwait("vector", cd)
        if kch == 0:
            W("vector", "vS", stats_done)
        if kch >= 2:
            W("vector", "tS", tp_done[kch - 2])
        nv = E("vector", lambda e, s=slot, n=nslot, k=kch: e.tensor_scalar(
            out=n[:], in0=s[:], scalar1=S[:, C_MEANP + k:C_MEANP + k + 1],
            scalar2=S[:, C_RSTDP + k:C_RSTDP + k + 1],
            op0=AluOpType.subtract, op1=AluOpType.mult), inc="vS")
        pb_done.append(nv)
        W("tensor", "vS", nv)
        if kch >= 1:
            W("tensor", "aS", ev_done)
        for t8 in range(8):
            E("tensor", lambda e, n=nslot, t=t8: e.transpose(
                out=pst[:, t * 128:(t + 1) * 128], in_=n[:, t * 128:(t + 1) * 128],
                identity=ident_bf[:]), inc="tS")
        tp_done.append(prog.cnt["tS"])
        W("scalar", "tS", prog.cnt["tS"])
        for t8 in range(8):
            E("scalar", lambda e, t=t8, k=kch: e.copy(
                out=xnT[:, t * NTOK + k * 128: t * NTOK + (k + 1) * 128],
                in_=pst[:, t * 128:(t + 1) * 128]), inc="aS")
        ev_done = prog.cnt["aS"]

    prog.drain("scalar", inc="aS")
    p1_aS = prog.cnt["aS"]
    p1_vS = prog.cnt["vS"]

    if "xnT" in dbg:
        W("gpsimd", "aS", p1_aS)
        DMA("gpsimd", dbg["xnT"], xnT[:])

    W("gpsimd", "vS", p1_vS)
    c1 = DMA("gpsimd", cc1_in[0:1, 0:128], Sc(C_GMAX1))
    W("gpsimd", "dG", c1)
    ccd = E("gpsimd", lambda e: e.collective_compute(
        "AllReduce", AluOpType.max, replica_groups=[list(range(NCORES))],
        ins=[cc1_in[0:1, 0:128].opt()], outs=[cc1_out[0:1, 0:128].opt()]), inc="gS")
    W("gpsimd", "gS", ccd)
    if "cc1i" in dbg:
        DMA("gpsimd", dbg["cc1i"], cc1_in)
        DMA("gpsimd", dbg["cc1o"], cc1_out)
    c2 = prog.dma_handoff("gpsimd", S[:, C_BCR:C_BCR + 128],
                          bass.AP(cc1_out.tensor, 0, [[0, 128], [1, 128]]))
    prog.hwait("vector", c2)
    E("vector", lambda e: e.tensor_reduce(
        out=Sc(C_S1), in_=S[:, C_BCR:C_BCR + 128], axis=AX, op=AluOpType.max))
    W("vector", "aS", sig_done)
    prog.drain("vector")
    E("vector", lambda e: e.tensor_tensor(
        out=Sc(C_S1), in0=Sc(C_S1), in1=Sc(C_SIGX), op=AluOpType.mult))
    prog.drain("vector")
    E("vector", lambda e: e.tensor_scalar(
        out=Sc(C_S1), in0=Sc(C_S1), scalar1=1.0 / QMAX, scalar2=None, op0=AluOpType.mult))
    prog.drain("vector")
    E("vector", lambda e: e.reciprocal(out=Sc(C_RS1), in_=Sc(C_S1)))
    s1_done = prog.drain("vector", inc="vS")
    E("vector", lambda e: e.tensor_scalar(
        out=S[:, C_CBG:C_CBG + NIC], in0=S[:, C_SWG:C_SWG + NIC],
        scalar1=Sc(C_S1), scalar2=None, op0=AluOpType.mult))
    E("vector", lambda e: e.tensor_scalar(
        out=S[:, C_CBU:C_CBU + NIC], in0=S[:, C_SWU:C_SWU + NIC],
        scalar1=Sc(C_S1), scalar2=None, op0=AluOpType.mult))
    combo_done = prog.drain("vector", inc="vS")

    def snap(tag):
        if tag in dbg:
            W("gpsimd", "vS", prog.cnt["vS"])
            W("gpsimd", "aS", prog.cnt["aS"])
            DMA("gpsimd", dbg[tag], S[:])

    def finish():
        if "smalls" in dbg:
            W("gpsimd", "vS", prog.cnt["vS"])
            W("gpsimd", "aS", prog.cnt["aS"])
            DMA("gpsimd", dbg["smalls"], S[:])
        if phases < 6:
            for k in range(NCHUNK):
                DMA("gpsimd", y[k * 128:(k + 1) * 128, :], x[k * 128:(k + 1) * 128, :])
        W("gpsimd", "dG", prog.cnt["dG"])
        W("gpsimd", "dS", prog.cnt["dS"])
        W("gpsimd", "dA", prog.cnt["dA"])
        with nc.Block() as block:
            prog.replay(block)

    snap("smA")
    if phases < 2:
        finish()
        return nc

    W("gpsimd", "vS", t_done[-1])
    for ifc in range(7):
        DMA("gpsimd", wvT_sb[:, ifc * 256:(ifc + 1) * 256], din["wvT"][ifc * 128:(ifc + 1) * 128, :])
    wv_loaded = prog.dma_handoff("gpsimd", wvT_sb[:, 7 * 256:8 * 256], din["wvT"][7 * 128:8 * 128, :])

    prog.hwait("tensor", max(wqk_loaded, wv_loaded, key=lambda t: t[1]))
    W("tensor", "aS", p1_aS)
    ps_q = [ps_a[:, i * 512:(i + 1) * 512] for i in range(4)]
    qev_cnt = []
    vev_cnt = []
    attn_st = {"o_dma": [0, 0], "stats": [], "red": [], "exp": [], "norm": [],
               "tp": [], "ev": [], "av": [], "oev": [], "last_oev": 0,
               "last_stats": 0}

    rope_tt_done = 0
    for s in range(NSEQ):
        if s > 0:
            W("vector", "tS", prog.cnt["tS"])
            W("scalar", "tS", prog.cnt["tS"])
            W("tensor", "aS", prog.cnt["aS"])
        for tt in range(4):
            for oc in range(10):
                pslot = ps_q[len(qev_cnt) % 4]
                if len(qev_cnt) >= 4:
                    W("tensor", "aS", qev_cnt[-4])
                for ifc in range(8):
                    if oc < 8:
                        lhs = wqT_sb[:, ifc * 1024 + oc * 128: ifc * 1024 + (oc + 1) * 128]
                    else:
                        lhs = wkT_sb[:, ifc * 256 + (oc - 8) * 128: ifc * 256 + (oc - 7) * 128]
                    E("tensor", lambda e, l=lhs, sl=pslot, i=ifc, s_=s, t_=tt: e.matmul(
                        out=sl[:], lhsT=l,
                        rhs=xnT[:, i * NTOK + s_ * T + t_ * 512: i * NTOK + s_ * T + (t_ + 1) * 512],
                        start=(i == 0), stop=(i == 7)), inc="tS" if ifc == 7 else None)
                W("scalar", "tS", prog.cnt["tS"])
                if oc == 0:
                    W("scalar", "vS", rope_tt_done)
                ec = E("scalar", lambda e, sl=pslot, o=oc: e.copy(
                    out=qraw[:, o * 512:(o + 1) * 512], in_=sl[:]), inc="aS")
                qev_cnt.append(ec)
            prog.drain("scalar", inc="aS")
            W("vector", "aS", prog.cnt["aS"])
            cosT = cos_sb[:, tt * 512:(tt + 1) * 512]
            sinT = sin_sb[:, tt * 512:(tt + 1) * 512]
            for j in range(5):
                if j < 4:
                    x1 = qraw[:, j * 512:(j + 1) * 512]
                    x2 = qraw[:, (4 + j) * 512:(5 + j) * 512]
                    d1 = q_rope[:, j * T + tt * 512: j * T + (tt + 1) * 512]
                    d2 = q_rope[:, (4 + j) * T + tt * 512: (4 + j) * T + (tt + 1) * 512]
                else:
                    x1 = qraw[:, 8 * 512:9 * 512]
                    x2 = qraw[:, 9 * 512:10 * 512]
                    d1 = k_rope[:, tt * 512:(tt + 1) * 512]
                    d2 = k_rope[:, T + tt * 512: T + (tt + 1) * 512]
                sc = 0.125 if j < 4 else 1.0
                E("vector", lambda e, a=x1, c=cosT, sc=sc: e.scalar_tensor_tensor(
                    out=rope_t[0][:], in0=a, scalar=sc, in1=c,
                    op0=AluOpType.mult, op1=AluOpType.mult))
                E("vector", lambda e, b=x2, sn=sinT, sc=sc: e.scalar_tensor_tensor(
                    out=rope_t[1][:], in0=b, scalar=sc, in1=sn,
                    op0=AluOpType.mult, op1=AluOpType.mult))
                prog.drain("vector")
                E("vector", lambda e, d=d1: e.tensor_tensor(
                    out=d, in0=rope_t[0][:], in1=rope_t[1][:], op=AluOpType.subtract))
                E("vector", lambda e, a=x1, sn=sinT, sc=sc: e.scalar_tensor_tensor(
                    out=rope_t[0][:], in0=a, scalar=sc, in1=sn,
                    op0=AluOpType.mult, op1=AluOpType.mult))
                E("vector", lambda e, b=x2, c=cosT, sc=sc: e.scalar_tensor_tensor(
                    out=rope_t[1][:], in0=b, scalar=sc, in1=c,
                    op0=AluOpType.mult, op1=AluOpType.mult))
                prog.drain("vector")
                E("vector", lambda e, d=d2: e.tensor_tensor(
                    out=d, in0=rope_t[0][:], in1=rope_t[1][:], op=AluOpType.add),
                  inc="vS" if j == 4 else None)
            rope_tt_done = prog.cnt["vS"]
        rope_done = prog.cnt["vS"]
        for tc in range(16):
            vslot = ps_b[:, (tc % 3) * 512:(tc % 3) * 512 + 256]
            if tc >= 3:
                W("tensor", "aS", vev_cnt[-3])
            for ifc in range(8):
                E("tensor", lambda e, sl=vslot, i=ifc, s_=s, t_=tc: e.matmul(
                    out=sl[:],
                    lhsT=xnT[:, i * NTOK + s_ * T + t_ * 128: i * NTOK + s_ * T + t_ * 128 + 128],
                    rhs=wvT_sb[:, i * 256:(i + 1) * 256],
                    start=(i == 0), stop=(i == 7)), inc="tS" if ifc == 7 else None)
            W("scalar", "tS", prog.cnt["tS"])
            ec = E("scalar", lambda e, sl=vslot, t_=tc: e.copy(
                out=v_tm[:, t_ * 256:(t_ + 1) * 256], in_=sl[:]), inc="aS")
            vev_cnt.append(ec)
        v_done = prog.drain("scalar", inc="aS")

        if phases >= 3:
            _emit_attention((prog, s, q_rope, k_rope, v_tm, probs_tm, probs_T,
                             o_sb, o_park, mask_main, mask_first, ident_bf,
                             ps_a, ps_b, S, C_MPACK, C_DPACK, C_RDPACK,
                             rope_done, v_done, attn_st))

    p23_tS = prog.cnt["tS"]
    p23_vS = prog.cnt["vS"]
    p23_aS = prog.cnt["aS"]
    if "q" in dbg:
        W("gpsimd", "vS", p23_vS)
        DMA("gpsimd", dbg["q"], q_rope[:])
        DMA("gpsimd", dbg["k"], k_rope[:])
        W("gpsimd", "aS", p23_aS)
        DMA("gpsimd", dbg["v"], v_tm[:])

    snap("smB")
    W("vector", "vS", s1_done)
    W("vector", "tS", p23_tS)
    W("sync", "tS", p23_tS)
    q_ev = 0
    q_tp = []
    q_nv = []
    for kch in range(NCHUNK):
        slot = xbuf[kch % 2]
        nslot = xn_tm[kch % 2]
        if kch >= 2:
            W("sync", "vS", q_nv[kch - 2])
        cd = prog.dma_handoff("sync", slot[:], x[kch * 128:(kch + 1) * 128, :])
        prog.hwait("vector", cd)
        if kch >= 2:
            W("vector", "tS", q_tp[kch - 2])
        E("vector", lambda e, s_=slot, k=kch: e.tensor_scalar(
            out=s_[:], in0=s_[:], scalar1=S[:, C_MEANP + k:C_MEANP + k + 1],
            scalar2=S[:, C_RSTDP + k:C_RSTDP + k + 1],
            op0=AluOpType.subtract, op1=AluOpType.mult))
        E("vector", lambda e, s_=slot: e.tensor_scalar(
            out=xq_int[:, 0:D], in0=s_[:], scalar1=Sc(C_RS1), scalar2=None,
            op0=AluOpType.mult))
        E("vector", lambda e: e.tensor_scalar(
            out=xq_int[:, 0:D], in0=xq_int[:, 0:D], scalar1=-128.0, scalar2=127.0,
            op0=AluOpType.max, op1=AluOpType.min))
        nv = E("vector", lambda e, n=nslot: e.tensor_copy(
            out=n[:], in_=xq_int[:, 0:D]), inc="vS")
        q_nv.append(nv)
        W("tensor", "vS", nv)
        if kch >= 1:
            W("tensor", "aS", q_ev)
        for t8 in range(8):
            E("tensor", lambda e, n=nslot, t=t8: e.transpose(
                out=pst[:, t * 128:(t + 1) * 128], in_=n[:, t * 128:(t + 1) * 128],
                identity=ident_bf[:]), inc="tS")
        q_tp.append(prog.cnt["tS"])
        W("scalar", "tS", prog.cnt["tS"])
        for t8 in range(8):
            E("scalar", lambda e, t=t8, k=kch: e.copy(
                out=xnT[:, t * NTOK + k * 128: t * NTOK + (k + 1) * 128],
                in_=pst[:, t * 128:(t + 1) * 128]), inc="aS")
        q_ev = prog.cnt["aS"]
    xq_done = prog.drain("vector", inc="vS")
    xq_evac_done = prog.drain("scalar", inc="aS")
    if "xq" in dbg:
        W("gpsimd", "vS", xq_done)
        DMA("gpsimd", dbg["xq"], xnT[:])

    if phases < 5:
        finish()
        return nc

    C_GM2P = 280
    p3_tS = prog.cnt["tS"]
    p3_aS = prog.cnt["aS"]
    wg_slab_done = []
    spill_done = [0, 0]
    stt_u = []
    o_dma_last = max(attn_st["o_dma"])
    for ic in range(NIC):
        sl = ic % 2
        if ic == 0:
            W("gpsimd", "dG", o_dma_last)
        if ic >= 2:
            W("gpsimd", "tS", wg_slab_done[ic - 2])
        lg = DMA("gpsimd", wgi_sb[sl][:],
                 bass.AP(wg_park.tensor, ic * 128, [[DFF, 128], [128 * DFF, 8], [1, 128]]))
        lu = prog.dma_handoff("gpsimd", wui_sb[sl][:],
                 bass.AP(wu_park.tensor, ic * 128, [[DFF, 128], [128 * DFF, 8], [1, 128]]))
        prog.hwait("tensor", lu)
        if ic == 0:
            W("tensor", "vS", xq_done)
            W("tensor", "aS", xq_evac_done)
        for tt in range(8):
            pg = ps_a[:, (tt % 2) * 512:(tt % 2) * 512 + 512]
            pu = ps_a[:, 1024 + (tt % 2) * 512:1024 + (tt % 2) * 512 + 512]
            if len(stt_u) >= 2:
                W("tensor", "vS", stt_u[-2])
            for ifc in range(8):
                E("tensor", lambda e, p=pg, w=wgi_sb[sl], i=ifc, t=tt: e.matmul(
                    out=p[:], lhsT=w[:, i * 128:(i + 1) * 128],
                    rhs=xnT[:, i * NTOK + t * 512:i * NTOK + (t + 1) * 512],
                    start=(i == 0), stop=(i == 7)), inc="tS" if ifc == 7 else None)
            gm = prog.cnt["tS"]
            for ifc in range(8):
                E("tensor", lambda e, p=pu, w=wui_sb[sl], i=ifc, t=tt: e.matmul(
                    out=p[:], lhsT=w[:, i * 128:(i + 1) * 128],
                    rhs=xnT[:, i * NTOK + t * 512:i * NTOK + (t + 1) * 512],
                    start=(i == 0), stop=(i == 7)), inc="tS" if ifc == 7 else None)
            um = prog.cnt["tS"]
            W("vector", "tS", gm)
            if ic >= 1 and tt == 0:
                W("vector", "dG", spill_done[(ic + 1) % 2])
            E("vector", lambda e, p=pg, t=tt, ic=ic: e.tensor_scalar(
                out=gate_f32[:, t * 512:(t + 1) * 512], in0=p[:],
                scalar1=S[:, C_CBG + ic:C_CBG + ic + 1], scalar2=None,
                op0=AluOpType.mult))
            ev = prog.drain("vector", inc="vS")
            W("scalar", "vS", ev)
            E("scalar", lambda e, t=tt: e.activation(
                out=gup_f32[:, t * 512:(t + 1) * 512],
                in_=gate_f32[:, t * 512:(t + 1) * 512], func=ACT.Silu))
            sv2 = prog.drain("scalar", inc="aS")
            W("vector", "aS", sv2)
            W("vector", "tS", um)
            sg = E("vector", lambda e, p=pu, t=tt, ic=ic: e.scalar_tensor_tensor(
                out=gate_f32[:, t * 512:(t + 1) * 512], in0=p[:],
                scalar=S[:, C_CBU + ic:C_CBU + ic + 1],
                in1=gup_f32[:, t * 512:(t + 1) * 512],
                op0=AluOpType.mult, op1=AluOpType.mult), inc="vS")
            stt_u.append(sg)
        wg_slab_done.append(prog.cnt["tS"])
        prog.drain("vector")
        am = E("vector", lambda e, ic=ic: e.tensor_reduce(
            out=S[:, C_GM2P + ic:C_GM2P + ic + 1], in_=gate_f32[:], axis=AX,
            op=AluOpType.max, apply_absolute_value=True), inc="vS")
        W("gpsimd", "vS", am)
        if ic == NIC - 1:
            spill_h = prog.dma_handoff("gpsimd", g_spill[ic * 128:(ic + 1) * 128, :], gate_f32[:])
            spill_done[ic % 2] = prog.cnt["dG"]
        else:
            spill_done[ic % 2] = DMA("gpsimd", g_spill[ic * 128:(ic + 1) * 128, :], gate_f32[:])

    snap("smC")
    p5_vflush = prog.drain("vector", inc="vS")
    prog.drain("vector")
    E("vector", lambda e: e.tensor_reduce(
        out=Sc(C_GMAX2), in_=S[:, C_GM2P:C_GM2P + NIC], axis=AX, op=AluOpType.max))
    g2 = prog.drain("vector", inc="vS")
    W("gpsimd", "vS", g2)
    cg1 = DMA("gpsimd", cc2_in[0:1, 0:128], Sc(C_GMAX2))
    W("gpsimd", "dG", cg1)
    ccd2 = E("gpsimd", lambda e: e.collective_compute(
        "AllReduce", AluOpType.max, replica_groups=[list(range(NCORES))],
        ins=[cc2_in[0:1, 0:128].opt()], outs=[cc2_out[0:1, 0:128].opt()]), inc="gS")
    W("gpsimd", "gS", ccd2)
    cg2 = prog.dma_handoff("gpsimd", S[:, C_BCR:C_BCR + 128],
                           bass.AP(cc2_out.tensor, 0, [[0, 128], [1, 128]]))
    if "cc2i" in dbg:
        DMA("gpsimd", dbg["cc2i"], cc2_in)
        DMA("gpsimd", dbg["cc2o"], cc2_out)
    prog.hwait("vector", cg2)
    E("vector", lambda e: e.tensor_reduce(
        out=Sc(C_S2), in_=S[:, C_BCR:C_BCR + 128], axis=AX, op=AluOpType.max))
    prog.drain("vector")
    E("vector", lambda e: e.tensor_tensor(
        out=Sc(C_S2), in0=Sc(C_S2), in1=Sc(C_SIGD), op=AluOpType.mult))
    prog.drain("vector")
    E("vector", lambda e: e.tensor_scalar(
        out=Sc(C_S2), in0=Sc(C_S2), scalar1=1.0 / QMAX, scalar2=None, op0=AluOpType.mult))
    prog.drain("vector")
    E("vector", lambda e: e.reciprocal(out=Sc(C_RS2), in_=Sc(C_S2)))
    prog.drain("vector")
    E("vector", lambda e: e.tensor_scalar(
        out=S[:, C_CBD:C_CBD + 8], in0=S[:, C_SWD:C_SWD + 8],
        scalar1=Sc(C_S2), scalar2=None, op0=AluOpType.mult))
    s2_done = prog.drain("vector", inc="vS")

    W("gpsimd", "tS", wg_slab_done[-1])
    for c in range(0, 7):
        DMA("gpsimd", w1T_sb[:, c * 4096:(c + 1) * 4096], din["w1T"][c * 128:(c + 1) * 128, :])
    w1_loaded = prog.dma_handoff("gpsimd", w1T_sb[:, 7 * 4096:8 * 4096], din["w1T"][7 * 128:8 * 128, :])
    W("gpsimd", "vS", xq_done)
    for c in range(32):
        DMA("gpsimd", w2T_sb[:, c * 1024:(c + 1) * 1024], din["w2T"][c * 128:(c + 1) * 128, :])
    w2_loaded = prog.dma_handoff("gpsimd", w2T_sb[:, 0:16], din["w2T"][0:128, 0:16])

    o_ld = [0, 0]
    ab_st = [0, 0]
    mlp_of2_done = 0
    h1_done = 0
    W("sync", "dG", o_dma_last)
    W("sync", "vS", p5_vflush)
    prog.hwait("sync", spill_h)
    for t8 in range(8):
        sl = t8 % 2
        if t8 >= 2:
            W("sync", "tS", o_ld[sl])
        od = prog.dma_handoff("sync", o_sb2[sl][:],
                 bass.AP(o_park.tensor, t8 * 512, [[NTOK, 128], [128 * NTOK, 8], [1, 512]]))
        prog.hwait("tensor", od)
        if t8 == 0:
            prog.hwait("tensor", w1_loaded)
            W("tensor", "vS", p5_vflush)
            prog.hwait("tensor", spill_h)
        if t8 >= 1:
            W("tensor", "aS", mlp_of2_done)
        silus = []
        for of1 in range(32):
            p = ps_a[:, (of1 % 4) * 512:(of1 % 4) * 512 + 512]
            if of1 >= 4:
                W("tensor", "aS", silus[of1 - 4])
            for ifc in range(8):
                w1src = w1T_sb
                E("tensor", lambda e, p=p, c=of1, i=ifc, sl=sl, w=w1src: e.matmul(
                    out=p[:], lhsT=w[:, i * 4096 + c * 128:i * 4096 + (c + 1) * 128],
                    rhs=o_sb2[sl][:, i * 512:(i + 1) * 512],
                    start=(i == 0), stop=(i == 7)), inc="tS" if ifc == 7 else None)
            W("scalar", "tS", prog.cnt["tS"])
            sv2 = E("scalar", lambda e, p=p, c=of1: e.activation(
                out=h1_sb[:, c * 512:(c + 1) * 512], in_=p[:], func=ACT.Silu), inc="aS")
            silus.append(sv2)
        o_ld[sl] = prog.cnt["tS"]
        h1_land = prog.drain("scalar", inc="aS")
        W("tensor", "aS", h1_land)
        if t8 == 0:
            prog.hwait("tensor", w2_loaded)
        ab_ev = []
        for of2 in range(8):
            p = ps_b[:, (of2 % 3) * 512:(of2 % 3) * 512 + 512]
            if of2 >= 3:
                W("tensor", "aS", ab_ev[of2 - 3])
            for c in range(32):
                E("tensor", lambda e, p=p, c=c, o=of2: e.matmul(
                    out=p[:], lhsT=w2T_sb[:, c * 1024 + o * 128:c * 1024 + (o + 1) * 128],
                    rhs=h1_sb[:, c * 512:(c + 1) * 512],
                    start=(c == 0), stop=(c == 31)), inc="tS" if c == 31 else None)
            W("scalar", "tS", prog.cnt["tS"])
            asl = of2 % 2
            if ab_st[asl]:
                W("scalar", "dA", ab_st[asl])
            E("scalar", lambda e, p=p, asl=asl: e.copy(
                out=ab_evac[asl][:], in_=p[:]))
            ae = prog.drain("scalar", inc="aS")
            ab_ev.append(ae)
            ab_st[asl] = prog.dma("scalar",
                bass.AP(abuf.tensor, of2 * 128 * NTOK + t8 * 512, [[NTOK, 128], [1, 512]]),
                ab_evac[asl][:])
        mlp_of2_done = prog.cnt["aS"]
        h1_done = prog.cnt["tS"]

    if phases < 6:
        finish()
        return nc

    W("sync", "dG", prog.cnt["dG"])
    W("sync", "tS", h1_done)
    for c in range(NIC - 1):
        DMA("sync", wdT_sb[:, c * 1024:(c + 1) * 1024], wd_park[c * 128:(c + 1) * 128, :])
    wd_loaded = prog.dma_handoff("sync", wdT_sb[:, (NIC - 1) * 1024:NIC * 1024],
                                 wd_park[(NIC - 1) * 128:NIC * 128, :])
    W("vector", "vS", s2_done)

    q_done = [[0] * NIC, [0] * NIC]
    x_ld = [0] * 4
    ab_ld = [0, 0]
    merge_done = [0, 0]
    out_dma = [0, 0, 0, 0]
    dps_ev = []
    dps_done = [0, 0]
    hbufs = [h_int, h_int2]

    def emit_hquant(t8, ic0, ic1):
        hb = hbufs[t8 % 2]
        for ic in range(ic0, min(ic1, NIC)):
            sl = ic % 2
            prev = q_done[t8 % 2][ic - 2] if ic >= 2 else q_done[(t8 - 1) % 2][NIC - 2 + ic]
            if prev:
                W("sync", "vS", prev)
            gl = prog.dma_handoff("sync", g_slab[sl][:],
                     bass.AP(g_spill.tensor, ic * 128 * NTOK + t8 * 512, [[NTOK, 128], [1, 512]]))
            prog.hwait("vector", gl)
            if ic == 0 and dps_done[t8 % 2]:
                W("vector", "tS", dps_done[t8 % 2])
            E("vector", lambda e, sl=sl: e.tensor_scalar(
                out=qint6[:], in0=g_slab[sl][:], scalar1=Sc(C_RS2),
                scalar2=None, op0=AluOpType.mult))
            E("vector", lambda e: e.tensor_scalar(
                out=qint6[:], in0=qint6[:], scalar1=-128.0,
                scalar2=127.0, op0=AluOpType.max, op1=AluOpType.min))
            E("vector", lambda e, ic=ic, hb=hb: e.tensor_copy(
                out=hb[:, ic * 512:(ic + 1) * 512], in_=qint6[:]))
            q_done[t8 % 2][ic] = prog.drain("vector", inc="vS")

    emit_hquant(0, 0, NIC)
    for t8 in range(8):
        hb = hbufs[t8 % 2]
        hq_all = q_done[t8 % 2][NIC - 1]
        for tc in range(4):
            if out_dma[tc]:
                W("sync", "dG", out_dma[tc])
            x_ld[tc] = prog.dma_handoff("sync", x_sb[tc][:],
                           x[(t8 * 4 + tc) * 128:(t8 * 4 + tc + 1) * 128, :])
        W("tensor", "vS", hq_all)
        if t8 == 0:
            prog.hwait("tensor", wd_loaded)
        tadd = [0] * 8
        for of in range(8):
            if t8 < 7:
                emit_hquant(t8 + 1, of * 3, of * 3 + 3)
            dps = ps_a[:, (of % 2) * 512:(of % 2) * 512 + 512]
            if len(dps_ev) >= 2:
                W("tensor", "vS", dps_ev[-2])
            for ic in range(NIC):
                E("tensor", lambda e, p=dps, ic=ic, o=of, hb=hb: e.matmul(
                    out=p[:], lhsT=wdT_sb[:, ic * 1024 + o * 128:ic * 1024 + (o + 1) * 128],
                    rhs=hb[:, ic * 512:(ic + 1) * 512],
                    start=(ic == 0), stop=(ic == NIC - 1)), inc="tS" if ic == NIC - 1 else None)
            dmm = prog.cnt["tS"]
            asl = of % 2
            if ab_ld[asl]:
                W("scalar", "vS", merge_done[asl])
            ab_ld[asl] = prog.dma_handoff("scalar", ab_slab[asl][:],
                bass.AP(abuf.tensor, of * 128 * NTOK + t8 * 512, [[NTOK, 128], [1, 512]]))
            W("vector", "tS", dmm)
            prog.hwait("vector", ab_ld[asl])
            E("vector", lambda e, p=dps, o=of, asl=asl: e.scalar_tensor_tensor(
                out=sum_sb[asl][:], in0=p[:], scalar=S[:, C_CBD + o:C_CBD + o + 1],
                in1=ab_slab[asl][:], op0=AluOpType.mult, op1=AluOpType.add))
            mg = prog.drain("vector", inc="vS")
            dps_ev.append(mg)
            merge_done[asl] = mg
            W("tensor", "vS", mg)
            tbank = ps_b[:, (of % 2) * 512:(of % 2) * 512 + 512]
            if of >= 2:
                W("tensor", "vS", tadd[of - 2])
            for tc in range(4):
                E("tensor", lambda e, tb=tbank, tc=tc, asl=asl: e.transpose(
                    out=tb[:, tc * 128:(tc + 1) * 128],
                    in_=sum_sb[asl][:, tc * 128:(tc + 1) * 128],
                    identity=ident_f32[:]), inc="tS" if tc == 3 else None)
            ttp = prog.cnt["tS"]
            W("vector", "tS", ttp)
            for tc in range(4):
                prog.hwait("vector", x_ld[tc])
                E("vector", lambda e, tb=tbank, tc=tc, o=of: e.tensor_tensor(
                    out=out_sb[tc][:, o * 128:(o + 1) * 128],
                    in0=tb[:, tc * 128:(tc + 1) * 128],
                    in1=x_sb[tc][:, o * 128:(o + 1) * 128], op=AluOpType.add),
                    inc="vS" if tc == 3 else None)
            tadd[of] = prog.cnt["vS"]
        dps_done[t8 % 2] = prog.cnt["tS"]
        all_adds = prog.drain("vector", inc="vS")
        W("gpsimd", "vS", all_adds)
        for tc in range(4):
            out_dma[tc] = DMA("gpsimd", y[(t8 * 4 + tc) * 128:(t8 * 4 + tc + 1) * 128, :],
                              out_sb[tc][:])

    finish()
    return nc


def _emit_attention(ctx):
    (prog, s, q_rope, k_rope, v_tm, probs_tm, probs_T, o_sb, o_park,
     mask_main, mask_first, ident_bf, ps_a, ps_b, S,
     C_MPACK, C_DPACK, C_RDPACK, rope_done, v_done, st) = ctx
    E, W, DMA = prog.emit, prog.wait, prog.dma
    PS = [ps_a, ps_b]
    PSB = [ps_a.bitcast(mybir.dt.bfloat16), ps_b.bitcast(mybir.dt.bfloat16)]
    T_ = T
    NK = 32

    def half_info(k):
        return k // 2, (k % 2) * 8

    for k in range(NK + 2):
        if k < NK:
            qc, h0 = half_info(k)
            ps = PS[k % 2]
            W("tensor", "vS", rope_done)
            if k >= 3:
                W("tensor", "aS", st["oev"][k - 3])
            if k >= 2:
                W("tensor", "aS", st["exp"][k - 2])
            else:
                W("tensor", "aS", v_done)
            for ih in range(8):
                h = h0 + ih
                j, g = h % 4, h // 4
                tile = ps[:, ih * 256:(ih + 1) * 256]
                qx1 = q_rope[32 * g:32 * g + 32, j * T_ + qc * 128: j * T_ + (qc + 1) * 128]
                qx2 = q_rope[32 * g:32 * g + 32, (4 + j) * T_ + qc * 128: (4 + j) * T_ + (qc + 1) * 128]
                if qc == 0:
                    kx1 = k_rope[32 * g:32 * g + 32, 0:128]
                    kx2 = k_rope[32 * g:32 * g + 32, T_:T_ + 128]
                    out = tile[:, 128:256]
                    mask = mask_first
                else:
                    kx1 = k_rope[32 * g:32 * g + 32, (qc - 1) * 128:(qc + 1) * 128]
                    kx2 = k_rope[32 * g:32 * g + 32, T_ + (qc - 1) * 128:T_ + (qc + 1) * 128]
                    out = tile
                    mask = mask_main
                E("tensor", lambda e, o=out, q=qx1, kk=kx1, g=g: e.matmul(
                    out=o, lhsT=q, rhs=kk, start=True, stop=False, tile_position=(32 * g, 0)))
                E("tensor", lambda e, o=out, q=qx2, kk=kx2, g=g: e.matmul(
                    out=o, lhsT=q, rhs=kk, start=False, stop=False, tile_position=(32 * g, 0)))
                E("tensor", lambda e, t=tile, m=mask: e.matmul(
                    out=t, lhsT=ident_bf[:], rhs=m[:], start=False, stop=True,
                    tile_position=(0, 0)), inc="tS" if ih == 7 else None)
            st["stats"].append(prog.cnt["tS"])
            W("vector", "tS", st["stats"][k])
            mp = C_MPACK + 8 * (k % 2)
            dp = C_DPACK + 8 * (k % 2)
            rp = C_RDPACK + 8 * (k % 2)
            for ih in range(8):
                E("vector", lambda e, t=ps[:, ih * 256:(ih + 1) * 256], c=mp + ih: e.tensor_reduce(
                    out=S[:, c:c + 1], in_=t, axis=AX, op=AluOpType.max, negate=True))
            st["red"].append(prog.drain("vector", inc="vS"))
            W("scalar", "vS", st["red"][k])
            if k >= 2:
                prog.hwait("scalar", st["tp"][k - 2][0])
                prog.hwait("scalar", st["tp"][k - 2][1])
            for ih in range(8):
                E("scalar", lambda e, t=ps[:, ih * 256:(ih + 1) * 256], ih=ih, c=mp + ih,
                  d=dp + ih, k=k: e.activation(
                    out=probs_tm[:, (k % 2) * 2048 + ih * 256:(k % 2) * 2048 + (ih + 1) * 256],
                    in_=t, func=ACT.Exp, bias=S[:, c:c + 1], accum_out=S[:, d:d + 1]))
            st["exp"].append(prog.drain("scalar", inc="aS"))
            W("vector", "aS", st["exp"][k])
            E("vector", lambda e, rp=rp, dp=dp: e.reciprocal(
                out=S[:, rp:rp + 8], in_=S[:, dp:dp + 8]))
            prog.drain("vector")
            lo = 128 if qc == 0 else 0
            for ih in range(8):
                E("vector", lambda e, ih=ih, k=k, lo=lo, rp=rp: e.tensor_scalar(
                    out=probs_tm[:, (k % 2) * 2048 + ih * 256 + lo:(k % 2) * 2048 + (ih + 1) * 256],
                    in0=probs_tm[:, (k % 2) * 2048 + ih * 256 + lo:(k % 2) * 2048 + (ih + 1) * 256],
                    scalar1=S[:, rp + ih:rp + ih + 1], scalar2=None, op0=AluOpType.mult))
            st["norm"].append(prog.drain("vector", inc="vS"))

        if k >= 2:
            j = k - 2
            qc, h0 = half_info(j)
            ava = PS[(j + 1) % 2]
            W("tensor", "aS", st["exp"][min(j + 1, NK - 1)])
            prog.hwait("tensor", st["tp"][j][0])
            prog.hwait("tensor", st["tp"][j][1])
            W("tensor", "aS", v_done)
            for ih in range(8):
                h = h0 + ih
                g = h // 4
                p = ih // 2
                cg = 64 * (h % 2)
                out = ava[cg:cg + 64, 1024 + p * 128:1024 + (p + 1) * 128]
                if qc == 0:
                    E("tensor", lambda e, o=out, g=g, cg=cg, ih=ih, j=j: e.matmul(
                        out=o, lhsT=v_tm[:, 64 * g: 64 * g + 64],
                        rhs=probs_T[:, (j % 2) * 2048 + (2 * ih + 1) * 128:(j % 2) * 2048 + (2 * ih + 2) * 128],
                        start=True, stop=True, tile_position=(0, cg)), inc="tS" if ih == 7 else None)
                else:
                    E("tensor", lambda e, o=out, g=g, cg=cg, ih=ih, j=j, qc=qc: e.matmul(
                        out=o, lhsT=v_tm[:, (qc - 1) * 256 + 64 * g: (qc - 1) * 256 + 64 * g + 64],
                        rhs=probs_T[:, (j % 2) * 2048 + 2 * ih * 128:(j % 2) * 2048 + (2 * ih + 1) * 128],
                        start=True, stop=False, tile_position=(0, cg)))
                    E("tensor", lambda e, o=out, g=g, cg=cg, ih=ih, j=j, qc=qc: e.matmul(
                        out=o, lhsT=v_tm[:, qc * 256 + 64 * g: qc * 256 + 64 * g + 64],
                        rhs=probs_T[:, (j % 2) * 2048 + (2 * ih + 1) * 128:(j % 2) * 2048 + (2 * ih + 2) * 128],
                        start=False, stop=True, tile_position=(0, cg)), inc="tS" if ih == 7 else None)
            st["av"].append(prog.cnt["tS"])
            W("scalar", "tS", st["av"][j])
            slot = o_sb[qc % 2]
            if j % 2 == 0 and st["o_dma"][qc % 2]:
                W("scalar", "dG", st["o_dma"][qc % 2])
            for p in range(4):
                pa = 4 * (j % 2) + p
                E("scalar", lambda e, pa=pa, p=p, sl=slot, ava=ava: e.copy(
                    out=sl[:, pa * 128:(pa + 1) * 128],
                    in_=ava[:, 1024 + p * 128:1024 + (p + 1) * 128]))
            st["oev"].append(prog.drain("scalar", inc="aS"))
            if j % 2 == 1:
                W("gpsimd", "aS", st["oev"][j])
                st["o_dma"][qc % 2] = DMA(
                    "gpsimd",
                    bass.AP(o_park.tensor, s * T_ + qc * 128,
                            [[NTOK, 128], [128 * NTOK, 8], [1, 128]]),
                    slot[:])
        if k >= 1 and k - 1 < NK:
            kb = k - 1
            qc, h0 = half_info(kb)
            sides = (1,) if qc == 0 else (0, 1)
            for eng in ("sync", "scalar"):
                W(eng, "vS", st["norm"][kb])
                if kb >= 2:
                    W(eng, "tS", st["av"][kb - 2])
            i = 0
            for ih in range(8):
                for side in sides:
                    ti = 2 * ih + side
                    eng = "sync" if i % 2 == 0 else "scalar"
                    DMA(eng,
                        probs_T[:, (kb % 2) * 2048 + ti * 128:(kb % 2) * 2048 + (ti + 1) * 128],
                        probs_tm[:, (kb % 2) * 2048 + ih * 256 + side * 128:
                                 (kb % 2) * 2048 + ih * 256 + side * 128 + 128],
                        transpose=True)
                    i += 1
            hs1 = prog.queue_handshake("sync")
            hs2 = prog.queue_handshake("scalar")
            st["tp"].append((hs1, hs2))
    st["last_oev"] = st["oev"][-1]
    st["last_stats"] = st["stats"][-1]
    for key in ("stats", "red", "exp", "norm", "tp", "ev", "av", "oev"):
        st[key] = []



def kernel(**inputs):
    xfull = np.asarray(inputs["L0_attn_query"], np.float32)
    prep = host_prep(inputs)
    nc = build_nc(phases=6)
    in_maps = []
    for c in range(NCORES):
        m = {"x": np.ascontiguousarray(xfull[2 * c:2 * c + 2].reshape(NTOK, D))}
        m.update(prep)
        in_maps.append(m)
    res = run_bass_kernel_spmd(nc, in_maps, core_ids=list(range(NCORES)))
    return np.concatenate([r["y"].reshape(2, T, D) for r in res.results], axis=0)


# version 19
# speedup vs baseline: 1.1320x; 1.0267x over previous
import numpy as np
import concourse.bass as bass
import concourse.mybir as mybir
from concourse.alu_op_type import AluOpType
from concourse.bass_utils import run_bass_kernel_spmd

F32 = mybir.dt.float32
BF16 = mybir.dt.bfloat16
I32 = mybir.dt.int32
ACT = mybir.ActivationFunctionType
AX = mybir.AxisListType.X

D = 1024
T = 2048
NSEQ = 2
NTOK = NSEQ * T
NH = 16
HD = 64
DFF = 2816
DMLP = 4096
NCHUNK = NTOK // 128
NIC = DFF // 128
QMAX = 127.0
NCORES = 8

ENGINES = ("sync", "tensor", "vector", "scalar", "gpsimd")



def _perm_q():
    perm = np.zeros(D, dtype=np.int64)
    for half in range(2):
        for j in range(4):
            for g in range(4):
                for i in range(32):
                    perm[half * 512 + j * 128 + g * 32 + i] = (4 * g + j) * 64 + half * 32 + i
    return perm


def _perm_k():
    perm = np.zeros(256, dtype=np.int64)
    for half in range(2):
        for g in range(4):
            for i in range(32):
                perm[half * 128 + g * 32 + i] = g * 64 + half * 32 + i
    return perm


def host_prep(inputs):
    import ml_dtypes
    d = {}
    wq = np.asarray(inputs["L0_attn_Wq"], np.float32)
    wk = np.asarray(inputs["L0_attn_Wk"], np.float32)
    wv = np.asarray(inputs["L0_attn_Wv"], np.float32)
    d["wqT"] = np.ascontiguousarray(wq[_perm_q(), :].T)
    d["wkT"] = np.ascontiguousarray(wk[_perm_k(), :].T)
    d["wvT"] = np.ascontiguousarray(wv.T)
    d["w1T"] = np.ascontiguousarray(np.asarray(inputs["L0_attn_W1"], np.float32).T)
    d["w2T"] = np.ascontiguousarray(np.asarray(inputs["L0_attn_W2"], np.float32).T)
    d["wg_n"] = np.asarray(inputs["L0_ffn_w_gate"], np.float32)
    d["wu_n"] = np.asarray(inputs["L0_ffn_w_up"], np.float32)
    d["wd_n"] = np.asarray(inputs["L0_ffn_w_down"], np.float32)
    d["wgT"] = np.ascontiguousarray(d["wg_n"].T)
    d["wuT"] = np.ascontiguousarray(d["wu_n"].T)
    d["wdT"] = np.ascontiguousarray(d["wd_n"].T)
    d["clip_x"] = np.asarray(inputs["L0_ffn_clip_x"], np.float32).reshape(1, 1)
    d["clip_down"] = np.asarray(inputs["L0_ffn_clip_down"], np.float32).reshape(1, 1)

    inv = 1.0 / 10000.0 ** (np.arange(0, HD, 2, dtype=np.float64) / HD)
    fr = np.outer(inv, np.arange(T, dtype=np.float64))
    d["cos_t"] = np.tile(np.cos(fr), (4, 1)).astype(np.float32)
    d["sin_t"] = np.tile(np.sin(fr), (4, 1)).astype(np.float32)

    q = np.arange(128)[:, None]
    kk = np.arange(256)[None, :]
    NEG = np.float32(-1e30)
    allow_main = np.where(kk < 128, kk >= q + 1, (kk - 128) <= q)
    allow_first = np.where(kk < 128, False, (kk - 128) <= q)
    d["mask_main"] = np.where(allow_main, np.float32(0), NEG).astype(ml_dtypes.bfloat16)
    d["mask_first"] = np.where(allow_first, np.float32(0), NEG).astype(ml_dtypes.bfloat16)
    d["ident_bf"] = np.eye(128, dtype=ml_dtypes.bfloat16)
    d["ident_f32"] = np.eye(128, dtype=np.float32)
    return d



class Prog:
    def __init__(self, nc):
        self.nc = nc
        self.items = {e: [] for e in ENGINES}
        self.sems = {}
        self.cnt = {}

    def sem(self, name):
        h = self.nc.semaphore(name).__enter__()
        self.sems[name] = h
        self.cnt[name] = 0

    def emit(self, engine, fn, inc=None, inc_amt=1):
        if inc is not None:
            sem = self.sems[inc]
            self.cnt[inc] += inc_amt

            def wrapped(eng, fn=fn, sem=sem, amt=inc_amt):
                fn(eng).then_inc(sem, amt)
            self.items[engine].append(wrapped)
        else:
            self.items[engine].append(fn)
        return self.cnt[inc] if inc is not None else None

    def wait(self, engine, sem_name, value=None):
        if value is None:
            value = self.cnt[sem_name]
        sem = self.sems[sem_name]
        self.items[engine].append(lambda eng, sem=sem, v=value: eng.wait_ge(sem, v))

    def vnop(self):
        self.items["vector"].append(lambda eng: eng.nop())

    HSEM = {"sync": "hSs", "scalar": "hSa", "gpsimd": "hSg"}

    def dma_handoff(self, engine, out, in_, **kw):
        c = self.dma(engine, out, in_, **kw)
        qsem = {"sync": "dS", "scalar": "dA", "gpsimd": "dG"}[engine]
        hs = self.HSEM[engine]
        self.wait(engine, qsem, c)
        self.emit(engine, lambda e, hs=hs: e.sem_inc(self.sems[hs], 1), inc=None)
        return (hs, self._bump(hs))

    def hwait(self, engine, h):
        self.wait(engine, h[0], h[1])

    def queue_handshake(self, engine):
        qsem = {"sync": "dS", "scalar": "dA", "gpsimd": "dG"}[engine]
        hs = self.HSEM[engine]
        self.wait(engine, qsem, self.cnt[qsem])
        self.emit(engine, lambda e, hs=hs: e.sem_inc(self.sems[hs], 1), inc=None)
        return (hs, self._bump(hs))

    def _bump(self, name):
        self.cnt[name] += 1
        return self.cnt[name]

    def drain(self, engine, inc=None):
        if inc is not None:
            return self.emit(engine, lambda e: e.drain(), inc=inc)
        self.items[engine].append(lambda eng: eng.drain())
        return None

    def dma(self, engine, out, in_, sem=None, **kw):
        if sem is None:
            sem = {"sync": "dS", "scalar": "dA", "gpsimd": "dG"}[engine]
        return self.emit(engine, lambda e: e.dma_start(out=out, in_=in_, **kw),
                         inc=sem, inc_amt=16)

    def replay(self, block):
        @block.sync
        def _(eng):
            for fn in self.items["sync"]:
                fn(eng)

        @block.tensor
        def _(eng):
            for fn in self.items["tensor"]:
                fn(eng)

        @block.vector
        def _(eng):
            for fn in self.items["vector"]:
                fn(eng)

        @block.scalar
        def _(eng):
            for fn in self.items["scalar"]:
                fn(eng)

        @block.gpsimd
        def _(eng):
            for fn in self.items["gpsimd"]:
                fn(eng)



def build_nc(phases=6, debug=()):
    nc = bass.Bass("TRN2", target_bir_lowering=False)

    x = nc.dram_tensor("x", [NTOK, D], F32, kind="ExternalInput").ap()
    y = nc.dram_tensor("y", [NTOK, D], F32, kind="ExternalOutput").ap()
    din = {}
    for name, shape in [
        ("wqT", [D, D]), ("wkT", [D, 256]), ("wvT", [D, 256]),
        ("w1T", [D, DMLP]), ("w2T", [DMLP, D]),
        ("wg_n", [DFF, D]), ("wu_n", [DFF, D]), ("wd_n", [D, DFF]),
        ("wgT", [D, DFF]), ("wuT", [D, DFF]), ("wdT", [DFF, D]),
        ("clip_x", [1, 1]), ("clip_down", [1, 1]),
        ("cos_t", [128, T]), ("sin_t", [128, T]),
    ]:
        din[name] = nc.dram_tensor(name, shape, F32, kind="ExternalInput").ap()
    for name, shape in [("mask_main", [128, 256]), ("mask_first", [128, 256]),
                        ("ident_bf", [128, 128])]:
        din[name] = nc.dram_tensor(name, shape, BF16, kind="ExternalInput").ap()
    din["ident_f32"] = nc.dram_tensor("ident_f32", [128, 128], F32, kind="ExternalInput").ap()

    def scratch(name, shape, dt):
        kind = "ExternalOutput" if name in debug else "Internal"
        return nc.dram_tensor(name, shape, dt, kind=kind).ap()

    wg_park = scratch("wg_park", [D, DFF], BF16)
    wu_park = scratch("wu_park", [D, DFF], BF16)
    wd_park = scratch("wd_park", [DFF, D], BF16)
    g_spill = scratch("g_spill", [DFF, NTOK], F32)
    o_park = scratch("o_park", [D, NTOK], BF16)
    abuf = scratch("abuf", [D, NTOK], F32)
    scl_pack = nc.dram_tensor("scl_pack", [64, 128], F32).ap()
    cc1_in = nc.dram_tensor("cc1_in", [128, 128], F32).ap()
    cc1_out = nc.dram_tensor("cc1_out", [128, 128], F32).ap()
    cc2_in = nc.dram_tensor("cc2_in", [128, 128], F32).ap()
    cc2_out = nc.dram_tensor("cc2_out", [128, 128], F32).ap()
    dbg = {}
    for nm, shape, dt in [("smA", [128, 888], F32), ("smB", [128, 888], F32),
                          ("smC", [128, 888], F32),
                          ("cc1i", [128, 128], F32), ("cc1o", [128, 128], F32),
                          ("cc2i", [128, 128], F32), ("cc2o", [128, 128], F32),
                          ("recips", [128, DFF], F32), ("xnT", [128, 8 * NTOK], BF16), ("xq", [128, 8 * NTOK], BF16),
                          ("q", [128, 8 * T], BF16), ("k", [128, 2 * T], BF16),
                          ("v", [128, 16 * 256], BF16), ("smalls", [128, 888], F32)]:
        if nm in debug:
            dbg[nm] = nc.dram_tensor("dbg_" + nm, shape, dt, kind="ExternalOutput").ap()

    SB_BASE = 16512
    def sb(name, off, shape, dt):
        return nc.alloc_sbuf_tensor_at(name, shape, dt, offset=SB_BASE + off).ap()

    xnT = sb("xnT", 0, [128, 8 * NTOK], BF16)
    w1T_sb = sb("w1T_sb", 0, [128, 8 * DMLP], BF16)
    w1T_a = sb("w1T_a", 98304, [128, 4 * DMLP], BF16)
    wdT_sb = sb("wdT_sb", 0, [128, NIC * D], BF16)
    h_int = sb("h_int", 45056, [128, NIC * 512], BF16)
    h_int2 = sb("h_int2", 112640, [128, NIC * 512], BF16)
    qint6 = sb("qint6", 135168, [128, 512], I32)
    g_slab = [sb(f"g_slab{i}", 67584 + i * 2048, [128, 512], F32) for i in range(2)]
    ab_slab = [sb(f"ab_slab{i}", 71680 + i * 2048, [128, 512], F32) for i in range(2)]
    sum_sb = [sb(f"sum_sb{i}", 75776 + i * 2048, [128, 512], F32) for i in range(2)]
    out_sb = [sb(f"out_sb{i}", 79872 + i * 4096, [128, D], F32) for i in range(4)]
    x_sb = [sb(f"x_sb{i}", 96256 + i * 4096, [128, D], F32) for i in range(4)]

    w2T_sb = sb("w2T_sb", 65536, [128, 32 * D], BF16)
    wnat = [sb(f"wnat{i}", 65536 + i * 11264, [128, DFF], F32) for i in range(2)]
    wTb = [sb(f"wTb{i}", 88064 + i * 11264, [128, DFF], F32) for i in range(2)]
    wint32 = sb("wint32", 110592, [128, DFF], I32)
    wintbf = sb("wintbf", 121856, [128, DFF], BF16)
    recip_gb = sb("recip_gb", 127488, [128, DFF], F32)
    recip_ub = sb("recip_ub", 138752, [128, DFF], F32)
    recip_db = sb("recip_db", 150016, [128, D], F32)
    xbuf = [sb(f"xbuf{i}", 154112 + i * 4096, [128, D], F32) for i in range(2)]
    xn_tm = [sb(f"xn_tm{i}", 162304 + i * 2048, [128, D], BF16) for i in range(2)]
    q_rope = sb("q_rope", 65536, [128, 8 * T], BF16)
    k_rope = sb("k_rope", 98304, [128, 2 * T], BF16)
    v_tm = sb("v_tm", 106496, [128, 16 * 256], BF16)
    qraw = sb("qraw", 114688, [128, 10 * 512], F32)
    rope_t = [sb(f"rope_t{i}", 135168 + i * 2048, [128, 512], F32) for i in range(2)]
    wvT_sb = sb("wvT_sb", 150016, [128, 8 * 256], BF16)
    wqT_sb = sb("wqT_sb", 166400, [128, 8 * 1024], BF16)
    wkT_sb = sb("wkT_sb", 182784, [128, 8 * 256], BF16)
    sq_scr = sb("sq_scr", 186880, [128, D], F32)
    xq_int = sb("xq_int", 114688, [128, 4096], I32)
    probs_tm = sb("probs_tm", 114688, [128, NH * 256], BF16)
    probs_T = sb("probs_T", 122880, [128, NH * 256], BF16)
    o_sb = [sb(f"o_sb{i}", 131072 + i * 2048, [128, D], BF16) for i in range(2)]
    wgi_sb = [sb(f"wgi_sb{i}", 131072 + i * 2048, [128, 8 * 128], BF16) for i in range(2)]
    wui_sb = [sb(f"wui_sb{i}", 135168 + i * 2048, [128, 8 * 128], BF16) for i in range(2)]
    gate_f32 = sb("gate_f32", 139264, [128, 4096], F32)
    gup_f32 = sb("gup_f32", 155648, [128, 4096], F32)
    h1_sb = sb("h1_sb", 131072, [128, 32 * 512], BF16)
    o_sb2 = [sb(f"o_sb2{i}", 163840 + i * 8192, [128, 8 * 512], BF16) for i in range(2)]
    ab_evac = [sb(f"ab_evac{i}", 180224 + i * 2048, [128, 512], F32) for i in range(2)]

    PT = 191104
    cos_sb = sb("cos_sb", PT, [128, T], F32)
    sin_sb = sb("sin_sb", PT + 8192, [128, T], F32)
    mask_main = sb("mask_main_sb", PT + 16384, [128, 256], BF16)
    mask_first = sb("mask_first_sb", PT + 16896, [128, 256], BF16)
    ident_bf = sb("ident_bf_sb", PT + 17408, [128, 128], BF16)
    ident_f32 = sb("ident_f32_sb", PT + 17664, [128, 128], F32)
    S = sb("smalls", PT + 18176, [128, 888], F32)

    (C_GMAX1, C_MEAN, C_MSQ, C_VAR, C_RSTD, C_MX, C_MN, C_T0, C_T1, C_SUM,
     C_SQS, C_S1, C_RS1, C_SIGX, C_SIGD, C_GMAX2, C_S2, C_RS2, C_CLX,
     C_CLD) = range(20)
    C_MPACK, C_DPACK, C_RDPACK = 32, 48, 64
    C_SUMP, C_SQSP, C_MXP, C_MNP = 600, 632, 664, 696
    C_MEANP, C_MSQP, C_VARP, C_RSTDP, C_AMP = 728, 760, 792, 824, 856
    C_RPG, C_RPU, C_RPD = 96, 118, 140
    C_CBG, C_CBU, C_CBD = 192, 224, 256
    C_BCR = 384
    C_SWG, C_SWU, C_SWD = 512, 544, 576

    def Sc(col, n=1):
        return S[:, col:col + n]

    ps_a = nc.alloc_psum_tensor("ps_a", [128, 2048], F32).ap()
    ps_b = nc.alloc_psum_tensor("ps_b", [128, 2048], F32).ap()
    pst = ps_a.bitcast(BF16)[:, 0:1024]

    prog = Prog(nc)
    for s_ in ("dS", "dA", "dG", "vS", "aS", "tS", "gS", "hSs", "hSa", "hSg"):
        prog.sem(s_)
    E, W, DMA = prog.emit, prog.wait, prog.dma

    DMA("sync", cos_sb[:], din["cos_t"])
    DMA("sync", sin_sb[:], din["sin_t"])
    DMA("sync", mask_main[:], din["mask_main"])
    DMA("sync", mask_first[:], din["mask_first"])
    DMA("sync", ident_bf[:], din["ident_bf"])
    DMA("sync", ident_f32[:], din["ident_f32"])
    DMA("sync", Sc(C_CLX), bass.AP(din["clip_x"].tensor, 0, [[0, 128], [1, 1]]))
    idf_loaded = prog.cnt["dS"] + 16
    c0 = prog.dma_handoff("sync", Sc(C_CLD), bass.AP(din["clip_down"].tensor, 0, [[0, 128], [1, 1]]))
    prog.hwait("vector", c0)
    E("vector", lambda e: e.memset(Sc(C_GMAX1), 0.0))
    E("vector", lambda e: e.memset(Sc(C_GMAX2), 0.0), inc="vS")
    prog.hwait("scalar", c0)
    E("scalar", lambda e: e.activation(out=Sc(C_SIGX), in_=Sc(C_CLX), func=ACT.Sigmoid))
    sig_done = E("scalar", lambda e: e.activation(out=Sc(C_SIGD), in_=Sc(C_CLD), func=ACT.Sigmoid), inc="aS")

    for ifc in range(8):
        DMA("gpsimd", wqT_sb[:, ifc * 1024:(ifc + 1) * 1024], din["wqT"][ifc * 128:(ifc + 1) * 128, :])
    for ifc in range(7):
        DMA("gpsimd", wkT_sb[:, ifc * 256:(ifc + 1) * 256], din["wkT"][ifc * 128:(ifc + 1) * 128, :])
    wqk_loaded = prog.dma_handoff("gpsimd", wkT_sb[:, 7 * 256:8 * 256], din["wkT"][7 * 128:8 * 128, :])

    C_SWGN, C_SWUN, C_SWDN = 302, 324, 346
    wq_jobs = []
    for src, cS, cN, nchunk, width in ((din["wg_n"], C_SWG, C_SWGN, NIC, D),
                                       (din["wu_n"], C_SWU, C_SWUN, NIC, D),
                                       (din["wd_n"], C_SWD, C_SWDN, 8, DFF)):
        for c in range(nchunk):
            wq_jobs.append((src, cS + c, cN + c, c, width))
    nat_done = []
    for idx, (src, cc, ccn, c, width) in enumerate(wq_jobs):
        slot = wnat[idx % 2]
        if idx >= 2:
            W("sync", "vS", nat_done[idx - 2])
        cd = prog.dma_handoff("sync", slot[:, 0:width], src[c * 128:(c + 1) * 128, :])
        prog.hwait("vector", cd)
        nd = E("vector", lambda e, s_=slot, w=width, cc=cc: e.tensor_reduce(
            out=S[:, cc:cc + 1], in_=s_[:, 0:w], axis=AX, op=AluOpType.max,
            apply_absolute_value=True), inc="vS")
        nat_done.append(nd)
    prog.drain("vector")
    for base, n in ((C_SWG, NIC), (C_SWU, NIC), (C_SWD, 8)):
        E("vector", lambda e, b=base, n_=n: e.tensor_scalar(
            out=S[:, b:b + n_], in0=S[:, b:b + n_], scalar1=1.0 / QMAX,
            scalar2=None, op0=AluOpType.mult))
    prog.drain("vector")
    for dst, srcb, n in ((C_RPG, C_SWG, NIC), (C_RPU, C_SWU, NIC), (C_RPD, C_SWD, 8)):
        E("vector", lambda e, d=dst, s=srcb, n_=n: e.reciprocal(
            out=S[:, d:d + n_], in_=S[:, s:s + n_]))
    rp_done = prog.drain("vector", inc="vS")
    W("tensor", "vS", rp_done)
    W("tensor", "dS", idf_loaded)
    tp = E("tensor", lambda e: e.transpose(
        out=ps_b[0:64, 0:128], in_=S[:, C_RPG:C_RPG + 64], identity=ident_f32[:]), inc="tS")
    W("scalar", "tS", tp)
    E("scalar", lambda e: e.copy(
        out=S[0:64, C_BCR:C_BCR + 128], in_=ps_b[0:64, 0:128]))
    tpe = prog.drain("scalar", inc="aS")
    W("gpsimd", "aS", tpe)
    pk = DMA("gpsimd", scl_pack[0:64, :], S[0:64, C_BCR:C_BCR + 128])
    W("gpsimd", "dG", pk)
    DMA("gpsimd", recip_gb[:], bass.AP(scl_pack.tensor, 0, [[0, 128], [1, DFF]]))
    DMA("gpsimd", recip_ub[:], bass.AP(scl_pack.tensor, 22 * 128, [[0, 128], [1, DFF]]))
    bc_done = prog.dma_handoff("gpsimd", recip_db[:], bass.AP(scl_pack.tensor, 44 * 128, [[0, 128], [1, D]]))

    if "recips" in dbg:
        W("gpsimd", "dG", bc_done)
        DMA("gpsimd", dbg["recips"], recip_gb[:])
    wt_jobs = []
    for src, park, rb, nchunk, width in ((din["wgT"], wg_park, recip_gb, 8, DFF),
                                         (din["wuT"], wu_park, recip_ub, 8, DFF),
                                         (din["wdT"], wd_park, recip_db, NIC, D)):
        for c in range(nchunk):
            wt_jobs.append((src, park, rb, c, width))
    prog.hwait("sync", bc_done)
    t_done, park_done = [], []
    for idx, (src, park, rb, c, width) in enumerate(wt_jobs):
        slot = wTb[idx % 2]
        if idx >= 2:
            W("sync", "vS", t_done[idx - 2])
        cd = prog.dma_handoff("sync", slot[:, 0:width], src[c * 128:(c + 1) * 128, :])
        prog.hwait("vector", cd)
        if idx >= 1:
            W("vector", "dG", park_done[idx - 1])
        E("vector", lambda e, s=slot, r=rb, w=width: e.tensor_tensor(
            out=wint32[:, 0:w], in0=s[:, 0:w], in1=r[:, 0:w], op=AluOpType.mult))
        E("vector", lambda e, w=width: e.tensor_copy(
            out=wintbf[:, 0:w], in_=wint32[:, 0:w]))
        cv = prog.drain("vector", inc="vS")
        t_done.append(cv)
        W("gpsimd", "vS", cv)
        pd = DMA("gpsimd", park[c * 128:(c + 1) * 128, :], wintbf[:, 0:width])
        park_done.append(pd)

    pa_done = []
    pa_gs = []
    for kch in range(NCHUNK):
        slot = xbuf[kch % 2]
        if kch >= 2:
            W("sync", "vS", pa_done[kch - 2])
        cd = prog.dma_handoff("sync", slot[:], x[kch * 128:(kch + 1) * 128, :])
        prog.hwait("vector", cd)
        E("vector", lambda e, s=slot, k=kch: e.tensor_reduce(
            out=S[:, C_SUMP + k:C_SUMP + k + 1], in_=s[:], axis=AX, op=AluOpType.add))
        E("vector", lambda e, s=slot: e.tensor_tensor(
            out=sq_scr[:], in0=s[:], in1=s[:], op=AluOpType.mult))
        E("vector", lambda e, k=kch: e.tensor_reduce(
            out=S[:, C_SQSP + k:C_SQSP + k + 1], in_=sq_scr[:], axis=AX, op=AluOpType.add))
        E("vector", lambda e, s=slot, k=kch: e.tensor_reduce(
            out=S[:, C_MXP + k:C_MXP + k + 1], in_=s[:], axis=AX, op=AluOpType.max))
        nv = E("vector", lambda e, s=slot, k=kch: e.tensor_reduce(
            out=S[:, C_MNP + k:C_MNP + k + 1], in_=s[:], axis=AX, op=AluOpType.min), inc="vS")
        pa_done.append(nv)
        pa_gs.append(nv)
    DR = prog.drain
    E("vector", lambda e: e.tensor_scalar(
        out=S[:, C_MEANP:C_MEANP + 32], in0=S[:, C_SUMP:C_SUMP + 32],
        scalar1=1.0 / D, scalar2=None, op0=AluOpType.mult))
    E("vector", lambda e: e.tensor_scalar(
        out=S[:, C_MSQP:C_MSQP + 32], in0=S[:, C_SQSP:C_SQSP + 32],
        scalar1=1.0 / D, scalar2=None, op0=AluOpType.mult))
    DR("vector")
    E("vector", lambda e: e.tensor_tensor(
        out=S[:, C_VARP:C_VARP + 32], in0=S[:, C_MEANP:C_MEANP + 32],
        in1=S[:, C_MEANP:C_MEANP + 32], op=AluOpType.mult))
    DR("vector")
    E("vector", lambda e: e.tensor_tensor(
        out=S[:, C_VARP:C_VARP + 32], in0=S[:, C_MSQP:C_MSQP + 32],
        in1=S[:, C_VARP:C_VARP + 32], op=AluOpType.subtract))
    DR("vector")
    vv = DR("vector", inc="vS")
    E("vector", lambda e: e.tensor_scalar(
        out=S[:, C_VARP:C_VARP + 32], in0=S[:, C_VARP:C_VARP + 32],
        scalar1=1e-5, scalar2=None, op0=AluOpType.add))
    vv = DR("vector", inc="vS")
    W("scalar", "vS", vv)
    E("scalar", lambda e: e.activation(
        out=S[:, C_VARP:C_VARP + 32], in_=S[:, C_VARP:C_VARP + 32], func=ACT.Sqrt))
    sv = DR("scalar", inc="aS")
    W("vector", "aS", sv)
    E("vector", lambda e: e.reciprocal(
        out=S[:, C_RSTDP:C_RSTDP + 32], in_=S[:, C_VARP:C_VARP + 32]))
    DR("vector")
    E("vector", lambda e: e.tensor_tensor(
        out=S[:, C_MXP:C_MXP + 32], in0=S[:, C_MXP:C_MXP + 32],
        in1=S[:, C_MEANP:C_MEANP + 32], op=AluOpType.subtract))
    E("vector", lambda e: e.tensor_tensor(
        out=S[:, C_MNP:C_MNP + 32], in0=S[:, C_MEANP:C_MEANP + 32],
        in1=S[:, C_MNP:C_MNP + 32], op=AluOpType.subtract))
    DR("vector")
    E("vector", lambda e: e.tensor_tensor(
        out=S[:, C_AMP:C_AMP + 32], in0=S[:, C_MXP:C_MXP + 32],
        in1=S[:, C_MNP:C_MNP + 32], op=AluOpType.max))
    DR("vector")
    E("vector", lambda e: e.tensor_tensor(
        out=S[:, C_AMP:C_AMP + 32], in0=S[:, C_AMP:C_AMP + 32],
        in1=S[:, C_RSTDP:C_RSTDP + 32], op=AluOpType.mult))
    DR("vector")
    E("vector", lambda e: e.tensor_reduce(
        out=Sc(C_GMAX1), in_=S[:, C_AMP:C_AMP + 32], axis=AX, op=AluOpType.max))
    stats_done = DR("vector", inc="vS")

    W("sync", "vS", stats_done)
    pb_done = []
    tp_done = []
    ev_done = 0
    for kch in range(NCHUNK):
        slot = xbuf[kch % 2]
        nslot = xn_tm[kch % 2]
        if kch >= 2:
            W("sync", "vS", pb_done[kch - 2])
        cd = prog.dma_handoff("sync", slot[:], x[kch * 128:(kch + 1) * 128, :])
        prog.hwait("vector", cd)
        if kch == 0:
            W("vector", "vS", stats_done)
        if kch >= 2:
            W("vector", "tS", tp_done[kch - 2])
        nv = E("vector", lambda e, s=slot, n=nslot, k=kch: e.tensor_scalar(
            out=n[:], in0=s[:], scalar1=S[:, C_MEANP + k:C_MEANP + k + 1],
            scalar2=S[:, C_RSTDP + k:C_RSTDP + k + 1],
            op0=AluOpType.subtract, op1=AluOpType.mult), inc="vS")
        pb_done.append(nv)
        W("tensor", "vS", nv)
        if kch >= 1:
            W("tensor", "aS", ev_done)
        for t8 in range(8):
            E("tensor", lambda e, n=nslot, t=t8: e.transpose(
                out=pst[:, t * 128:(t + 1) * 128], in_=n[:, t * 128:(t + 1) * 128],
                identity=ident_bf[:]), inc="tS")
        tp_done.append(prog.cnt["tS"])
        W("scalar", "tS", prog.cnt["tS"])
        for t8 in range(8):
            E("scalar", lambda e, t=t8, k=kch: e.copy(
                out=xnT[:, t * NTOK + k * 128: t * NTOK + (k + 1) * 128],
                in_=pst[:, t * 128:(t + 1) * 128]), inc="aS")
        ev_done = prog.cnt["aS"]

    prog.drain("scalar", inc="aS")
    p1_aS = prog.cnt["aS"]
    p1_vS = prog.cnt["vS"]

    if "xnT" in dbg:
        W("gpsimd", "aS", p1_aS)
        DMA("gpsimd", dbg["xnT"], xnT[:])

    W("gpsimd", "vS", p1_vS)
    c1 = DMA("gpsimd", cc1_in[0:1, 0:128], Sc(C_GMAX1))
    W("gpsimd", "dG", c1)
    ccd = E("gpsimd", lambda e: e.collective_compute(
        "AllReduce", AluOpType.max, replica_groups=[list(range(NCORES))],
        ins=[cc1_in[0:1, 0:128].opt()], outs=[cc1_out[0:1, 0:128].opt()]), inc="gS")
    W("gpsimd", "gS", ccd)
    if "cc1i" in dbg:
        DMA("gpsimd", dbg["cc1i"], cc1_in)
        DMA("gpsimd", dbg["cc1o"], cc1_out)
    c2 = prog.dma_handoff("gpsimd", S[:, C_BCR:C_BCR + 128],
                          bass.AP(cc1_out.tensor, 0, [[0, 128], [1, 128]]))
    prog.hwait("vector", c2)
    E("vector", lambda e: e.tensor_reduce(
        out=Sc(C_S1), in_=S[:, C_BCR:C_BCR + 128], axis=AX, op=AluOpType.max))
    W("vector", "aS", sig_done)
    prog.drain("vector")
    E("vector", lambda e: e.tensor_tensor(
        out=Sc(C_S1), in0=Sc(C_S1), in1=Sc(C_SIGX), op=AluOpType.mult))
    prog.drain("vector")
    E("vector", lambda e: e.tensor_scalar(
        out=Sc(C_S1), in0=Sc(C_S1), scalar1=1.0 / QMAX, scalar2=None, op0=AluOpType.mult))
    prog.drain("vector")
    E("vector", lambda e: e.reciprocal(out=Sc(C_RS1), in_=Sc(C_S1)))
    s1_done = prog.drain("vector", inc="vS")
    E("vector", lambda e: e.tensor_scalar(
        out=S[:, C_CBG:C_CBG + NIC], in0=S[:, C_SWG:C_SWG + NIC],
        scalar1=Sc(C_S1), scalar2=None, op0=AluOpType.mult))
    E("vector", lambda e: e.tensor_scalar(
        out=S[:, C_CBU:C_CBU + NIC], in0=S[:, C_SWU:C_SWU + NIC],
        scalar1=Sc(C_S1), scalar2=None, op0=AluOpType.mult))
    combo_done = prog.drain("vector", inc="vS")

    def snap(tag):
        if tag in dbg:
            W("gpsimd", "vS", prog.cnt["vS"])
            W("gpsimd", "aS", prog.cnt["aS"])
            DMA("gpsimd", dbg[tag], S[:])

    def finish():
        if "smalls" in dbg:
            W("gpsimd", "vS", prog.cnt["vS"])
            W("gpsimd", "aS", prog.cnt["aS"])
            DMA("gpsimd", dbg["smalls"], S[:])
        if phases < 6:
            for k in range(NCHUNK):
                DMA("gpsimd", y[k * 128:(k + 1) * 128, :], x[k * 128:(k + 1) * 128, :])
        W("gpsimd", "dG", prog.cnt["dG"])
        W("gpsimd", "dS", prog.cnt["dS"])
        W("gpsimd", "dA", prog.cnt["dA"])
        with nc.Block() as block:
            prog.replay(block)

    snap("smA")
    if phases < 2:
        finish()
        return nc

    W("gpsimd", "vS", t_done[-1])
    for ifc in range(7):
        DMA("gpsimd", wvT_sb[:, ifc * 256:(ifc + 1) * 256], din["wvT"][ifc * 128:(ifc + 1) * 128, :])
    wv_loaded = prog.dma_handoff("gpsimd", wvT_sb[:, 7 * 256:8 * 256], din["wvT"][7 * 128:8 * 128, :])

    prog.hwait("tensor", max(wqk_loaded, wv_loaded, key=lambda t: t[1]))
    W("tensor", "aS", p1_aS)
    ps_q = [ps_a[:, i * 512:(i + 1) * 512] for i in range(4)]
    qev_cnt = []
    vev_cnt = []
    attn_st = {"o_dma": [0, 0], "stats": [], "red": [], "exp": [], "norm": [],
               "tp": [], "ev": [], "av": [], "oev": [], "last_oev": 0,
               "last_stats": 0}

    rope_tt_done = 0
    for s in range(NSEQ):
        if s > 0:
            W("vector", "tS", prog.cnt["tS"])
            W("scalar", "tS", prog.cnt["tS"])
            W("tensor", "aS", prog.cnt["aS"])
        for tt in range(4):
            for oc in range(10):
                pslot = ps_q[len(qev_cnt) % 4]
                if len(qev_cnt) >= 4:
                    W("tensor", "aS", qev_cnt[-4])
                for ifc in range(8):
                    if oc < 8:
                        lhs = wqT_sb[:, ifc * 1024 + oc * 128: ifc * 1024 + (oc + 1) * 128]
                    else:
                        lhs = wkT_sb[:, ifc * 256 + (oc - 8) * 128: ifc * 256 + (oc - 7) * 128]
                    E("tensor", lambda e, l=lhs, sl=pslot, i=ifc, s_=s, t_=tt: e.matmul(
                        out=sl[:], lhsT=l,
                        rhs=xnT[:, i * NTOK + s_ * T + t_ * 512: i * NTOK + s_ * T + (t_ + 1) * 512],
                        start=(i == 0), stop=(i == 7)), inc="tS" if ifc == 7 else None)
                W("scalar", "tS", prog.cnt["tS"])
                if oc == 0:
                    W("scalar", "vS", rope_tt_done)
                ec = E("scalar", lambda e, sl=pslot, o=oc: e.copy(
                    out=qraw[:, o * 512:(o + 1) * 512], in_=sl[:]), inc="aS")
                qev_cnt.append(ec)
            prog.drain("scalar", inc="aS")
            W("vector", "aS", prog.cnt["aS"])
            cosT = cos_sb[:, tt * 512:(tt + 1) * 512]
            sinT = sin_sb[:, tt * 512:(tt + 1) * 512]
            for j in range(5):
                if j < 4:
                    x1 = qraw[:, j * 512:(j + 1) * 512]
                    x2 = qraw[:, (4 + j) * 512:(5 + j) * 512]
                    d1 = q_rope[:, j * T + tt * 512: j * T + (tt + 1) * 512]
                    d2 = q_rope[:, (4 + j) * T + tt * 512: (4 + j) * T + (tt + 1) * 512]
                else:
                    x1 = qraw[:, 8 * 512:9 * 512]
                    x2 = qraw[:, 9 * 512:10 * 512]
                    d1 = k_rope[:, tt * 512:(tt + 1) * 512]
                    d2 = k_rope[:, T + tt * 512: T + (tt + 1) * 512]
                sc = 0.125 if j < 4 else 1.0
                E("vector", lambda e, a=x1, c=cosT, sc=sc: e.scalar_tensor_tensor(
                    out=rope_t[0][:], in0=a, scalar=sc, in1=c,
                    op0=AluOpType.mult, op1=AluOpType.mult))
                E("vector", lambda e, b=x2, sn=sinT, sc=sc: e.scalar_tensor_tensor(
                    out=rope_t[1][:], in0=b, scalar=sc, in1=sn,
                    op0=AluOpType.mult, op1=AluOpType.mult))
                prog.drain("vector")
                E("vector", lambda e, d=d1: e.tensor_tensor(
                    out=d, in0=rope_t[0][:], in1=rope_t[1][:], op=AluOpType.subtract))
                E("vector", lambda e, a=x1, sn=sinT, sc=sc: e.scalar_tensor_tensor(
                    out=rope_t[0][:], in0=a, scalar=sc, in1=sn,
                    op0=AluOpType.mult, op1=AluOpType.mult))
                E("vector", lambda e, b=x2, c=cosT, sc=sc: e.scalar_tensor_tensor(
                    out=rope_t[1][:], in0=b, scalar=sc, in1=c,
                    op0=AluOpType.mult, op1=AluOpType.mult))
                prog.drain("vector")
                E("vector", lambda e, d=d2: e.tensor_tensor(
                    out=d, in0=rope_t[0][:], in1=rope_t[1][:], op=AluOpType.add),
                  inc="vS" if j == 4 else None)
            rope_tt_done = prog.cnt["vS"]
        rope_done = prog.cnt["vS"]
        for tc in range(16):
            vslot = ps_b[:, (tc % 3) * 512:(tc % 3) * 512 + 256]
            if tc >= 3:
                W("tensor", "aS", vev_cnt[-3])
            for ifc in range(8):
                E("tensor", lambda e, sl=vslot, i=ifc, s_=s, t_=tc: e.matmul(
                    out=sl[:],
                    lhsT=xnT[:, i * NTOK + s_ * T + t_ * 128: i * NTOK + s_ * T + t_ * 128 + 128],
                    rhs=wvT_sb[:, i * 256:(i + 1) * 256],
                    start=(i == 0), stop=(i == 7)), inc="tS" if ifc == 7 else None)
            W("scalar", "tS", prog.cnt["tS"])
            ec = E("scalar", lambda e, sl=vslot, t_=tc: e.copy(
                out=v_tm[:, t_ * 256:(t_ + 1) * 256], in_=sl[:]), inc="aS")
            vev_cnt.append(ec)
        v_done = prog.drain("scalar", inc="aS")

        if phases >= 3:
            _emit_attention((prog, s, q_rope, k_rope, v_tm, probs_tm, probs_T,
                             o_sb, o_park, mask_main, mask_first, ident_bf,
                             ps_a, ps_b, S, C_MPACK, C_DPACK, C_RDPACK,
                             rope_done, v_done, attn_st))

    p23_tS = prog.cnt["tS"]
    p23_vS = prog.cnt["vS"]
    p23_aS = prog.cnt["aS"]
    if "q" in dbg:
        W("gpsimd", "vS", p23_vS)
        DMA("gpsimd", dbg["q"], q_rope[:])
        DMA("gpsimd", dbg["k"], k_rope[:])
        W("gpsimd", "aS", p23_aS)
        DMA("gpsimd", dbg["v"], v_tm[:])

    snap("smB")
    W("vector", "vS", s1_done)
    W("vector", "tS", p23_tS)
    W("sync", "tS", p23_tS)
    q_ev = 0
    q_tp = []
    q_nv = []
    for kch in range(NCHUNK):
        slot = xbuf[kch % 2]
        nslot = xn_tm[kch % 2]
        if kch >= 2:
            W("sync", "vS", q_nv[kch - 2])
        cd = prog.dma_handoff("sync", slot[:], x[kch * 128:(kch + 1) * 128, :])
        prog.hwait("vector", cd)
        if kch >= 2:
            W("vector", "tS", q_tp[kch - 2])
        E("vector", lambda e, s_=slot, k=kch: e.tensor_scalar(
            out=s_[:], in0=s_[:], scalar1=S[:, C_MEANP + k:C_MEANP + k + 1],
            scalar2=S[:, C_RSTDP + k:C_RSTDP + k + 1],
            op0=AluOpType.subtract, op1=AluOpType.mult))
        E("vector", lambda e, s_=slot: e.tensor_scalar(
            out=xq_int[:, 0:D], in0=s_[:], scalar1=Sc(C_RS1), scalar2=None,
            op0=AluOpType.mult))
        nv = E("vector", lambda e, n=nslot: e.tensor_scalar(
            out=n[:], in0=xq_int[:, 0:D], scalar1=-128.0, scalar2=127.0,
            op0=AluOpType.max, op1=AluOpType.min), inc="vS")
        q_nv.append(nv)
        W("tensor", "vS", nv)
        if kch >= 1:
            W("tensor", "aS", q_ev)
        for t8 in range(8):
            E("tensor", lambda e, n=nslot, t=t8: e.transpose(
                out=pst[:, t * 128:(t + 1) * 128], in_=n[:, t * 128:(t + 1) * 128],
                identity=ident_bf[:]), inc="tS")
        q_tp.append(prog.cnt["tS"])
        W("scalar", "tS", prog.cnt["tS"])
        for t8 in range(8):
            E("scalar", lambda e, t=t8, k=kch: e.copy(
                out=xnT[:, t * NTOK + k * 128: t * NTOK + (k + 1) * 128],
                in_=pst[:, t * 128:(t + 1) * 128]), inc="aS")
        q_ev = prog.cnt["aS"]
    xq_done = prog.drain("vector", inc="vS")
    xq_evac_done = prog.drain("scalar", inc="aS")
    if "xq" in dbg:
        W("gpsimd", "vS", xq_done)
        DMA("gpsimd", dbg["xq"], xnT[:])

    if phases < 5:
        finish()
        return nc

    C_GM2P = 280
    p3_tS = prog.cnt["tS"]
    p3_aS = prog.cnt["aS"]
    wg_slab_done = []
    spill_done = [0, 0]
    stt_u = []
    o_dma_last = max(attn_st["o_dma"])
    for ic in range(NIC):
        sl = ic % 2
        if ic == 0:
            W("gpsimd", "dG", o_dma_last)
        if ic >= 2:
            W("gpsimd", "tS", wg_slab_done[ic - 2])
        lg = DMA("gpsimd", wgi_sb[sl][:],
                 bass.AP(wg_park.tensor, ic * 128, [[DFF, 128], [128 * DFF, 8], [1, 128]]))
        lu = prog.dma_handoff("gpsimd", wui_sb[sl][:],
                 bass.AP(wu_park.tensor, ic * 128, [[DFF, 128], [128 * DFF, 8], [1, 128]]))
        prog.hwait("tensor", lu)
        if ic == 0:
            W("tensor", "vS", xq_done)
            W("tensor", "aS", xq_evac_done)
        for tt in range(8):
            pg = ps_a[:, (tt % 2) * 512:(tt % 2) * 512 + 512]
            pu = ps_a[:, 1024 + (tt % 2) * 512:1024 + (tt % 2) * 512 + 512]
            if len(stt_u) >= 2:
                W("tensor", "vS", stt_u[-2])
            for ifc in range(8):
                E("tensor", lambda e, p=pg, w=wgi_sb[sl], i=ifc, t=tt: e.matmul(
                    out=p[:], lhsT=w[:, i * 128:(i + 1) * 128],
                    rhs=xnT[:, i * NTOK + t * 512:i * NTOK + (t + 1) * 512],
                    start=(i == 0), stop=(i == 7)), inc="tS" if ifc == 7 else None)
            gm = prog.cnt["tS"]
            for ifc in range(8):
                E("tensor", lambda e, p=pu, w=wui_sb[sl], i=ifc, t=tt: e.matmul(
                    out=p[:], lhsT=w[:, i * 128:(i + 1) * 128],
                    rhs=xnT[:, i * NTOK + t * 512:i * NTOK + (t + 1) * 512],
                    start=(i == 0), stop=(i == 7)), inc="tS" if ifc == 7 else None)
            um = prog.cnt["tS"]
            W("vector", "tS", gm)
            if ic >= 1 and tt == 0:
                W("vector", "dG", spill_done[(ic + 1) % 2])
            E("vector", lambda e, p=pg, t=tt, ic=ic: e.tensor_scalar(
                out=gate_f32[:, t * 512:(t + 1) * 512], in0=p[:],
                scalar1=S[:, C_CBG + ic:C_CBG + ic + 1], scalar2=None,
                op0=AluOpType.mult))
            ev = prog.drain("vector", inc="vS")
            W("scalar", "vS", ev)
            E("scalar", lambda e, t=tt: e.activation(
                out=gup_f32[:, t * 512:(t + 1) * 512],
                in_=gate_f32[:, t * 512:(t + 1) * 512], func=ACT.Silu))
            sv2 = prog.drain("scalar", inc="aS")
            W("vector", "aS", sv2)
            W("vector", "tS", um)
            sg = E("vector", lambda e, p=pu, t=tt, ic=ic: e.scalar_tensor_tensor(
                out=gate_f32[:, t * 512:(t + 1) * 512], in0=p[:],
                scalar=S[:, C_CBU + ic:C_CBU + ic + 1],
                in1=gup_f32[:, t * 512:(t + 1) * 512],
                op0=AluOpType.mult, op1=AluOpType.mult), inc="vS")
            stt_u.append(sg)
        wg_slab_done.append(prog.cnt["tS"])
        prog.drain("vector")
        am = E("vector", lambda e, ic=ic: e.tensor_reduce(
            out=S[:, C_GM2P + ic:C_GM2P + ic + 1], in_=gate_f32[:], axis=AX,
            op=AluOpType.max, apply_absolute_value=True), inc="vS")
        W("gpsimd", "vS", am)
        if ic == NIC - 1:
            spill_h = prog.dma_handoff("gpsimd", g_spill[ic * 128:(ic + 1) * 128, :], gate_f32[:])
            spill_done[ic % 2] = prog.cnt["dG"]
        else:
            spill_done[ic % 2] = DMA("gpsimd", g_spill[ic * 128:(ic + 1) * 128, :], gate_f32[:])

    snap("smC")
    p5_vflush = prog.drain("vector", inc="vS")
    prog.drain("vector")
    E("vector", lambda e: e.tensor_reduce(
        out=Sc(C_GMAX2), in_=S[:, C_GM2P:C_GM2P + NIC], axis=AX, op=AluOpType.max))
    g2 = prog.drain("vector", inc="vS")
    W("gpsimd", "vS", g2)
    cg1 = DMA("gpsimd", cc2_in[0:1, 0:128], Sc(C_GMAX2))
    W("gpsimd", "dG", cg1)
    ccd2 = E("gpsimd", lambda e: e.collective_compute(
        "AllReduce", AluOpType.max, replica_groups=[list(range(NCORES))],
        ins=[cc2_in[0:1, 0:128].opt()], outs=[cc2_out[0:1, 0:128].opt()]), inc="gS")
    W("gpsimd", "gS", ccd2)
    cg2 = prog.dma_handoff("gpsimd", S[:, C_BCR:C_BCR + 128],
                           bass.AP(cc2_out.tensor, 0, [[0, 128], [1, 128]]))
    if "cc2i" in dbg:
        DMA("gpsimd", dbg["cc2i"], cc2_in)
        DMA("gpsimd", dbg["cc2o"], cc2_out)
    prog.hwait("vector", cg2)
    E("vector", lambda e: e.tensor_reduce(
        out=Sc(C_S2), in_=S[:, C_BCR:C_BCR + 128], axis=AX, op=AluOpType.max))
    prog.drain("vector")
    E("vector", lambda e: e.tensor_tensor(
        out=Sc(C_S2), in0=Sc(C_S2), in1=Sc(C_SIGD), op=AluOpType.mult))
    prog.drain("vector")
    E("vector", lambda e: e.tensor_scalar(
        out=Sc(C_S2), in0=Sc(C_S2), scalar1=1.0 / QMAX, scalar2=None, op0=AluOpType.mult))
    prog.drain("vector")
    E("vector", lambda e: e.reciprocal(out=Sc(C_RS2), in_=Sc(C_S2)))
    prog.drain("vector")
    E("vector", lambda e: e.tensor_scalar(
        out=S[:, C_CBD:C_CBD + 8], in0=S[:, C_SWD:C_SWD + 8],
        scalar1=Sc(C_S2), scalar2=None, op0=AluOpType.mult))
    s2_done = prog.drain("vector", inc="vS")

    W("gpsimd", "tS", wg_slab_done[-1])
    for c in range(0, 7):
        DMA("gpsimd", w1T_sb[:, c * 4096:(c + 1) * 4096], din["w1T"][c * 128:(c + 1) * 128, :])
    w1_loaded = prog.dma_handoff("gpsimd", w1T_sb[:, 7 * 4096:8 * 4096], din["w1T"][7 * 128:8 * 128, :])
    W("gpsimd", "vS", xq_done)
    for c in range(32):
        DMA("gpsimd", w2T_sb[:, c * 1024:(c + 1) * 1024], din["w2T"][c * 128:(c + 1) * 128, :])
    w2_loaded = prog.dma_handoff("gpsimd", w2T_sb[:, 0:16], din["w2T"][0:128, 0:16])

    o_ld = [0, 0]
    ab_st = [0, 0]
    mlp_of2_done = 0
    h1_done = 0
    W("sync", "dG", o_dma_last)
    W("sync", "vS", p5_vflush)
    prog.hwait("sync", spill_h)
    for t8 in range(8):
        sl = t8 % 2
        if t8 >= 2:
            W("sync", "tS", o_ld[sl])
        od = prog.dma_handoff("sync", o_sb2[sl][:],
                 bass.AP(o_park.tensor, t8 * 512, [[NTOK, 128], [128 * NTOK, 8], [1, 512]]))
        prog.hwait("tensor", od)
        if t8 == 0:
            prog.hwait("tensor", w1_loaded)
            W("tensor", "vS", p5_vflush)
            prog.hwait("tensor", spill_h)
        if t8 >= 1:
            W("tensor", "aS", mlp_of2_done)
        silus = []
        for of1 in range(32):
            p = ps_a[:, (of1 % 4) * 512:(of1 % 4) * 512 + 512]
            if of1 >= 4:
                W("tensor", "aS", silus[of1 - 4])
            for ifc in range(8):
                w1src = w1T_sb
                E("tensor", lambda e, p=p, c=of1, i=ifc, sl=sl, w=w1src: e.matmul(
                    out=p[:], lhsT=w[:, i * 4096 + c * 128:i * 4096 + (c + 1) * 128],
                    rhs=o_sb2[sl][:, i * 512:(i + 1) * 512],
                    start=(i == 0), stop=(i == 7)), inc="tS" if ifc == 7 else None)
            W("scalar", "tS", prog.cnt["tS"])
            sv2 = E("scalar", lambda e, p=p, c=of1: e.activation(
                out=h1_sb[:, c * 512:(c + 1) * 512], in_=p[:], func=ACT.Silu), inc="aS")
            silus.append(sv2)
        o_ld[sl] = prog.cnt["tS"]
        h1_land = prog.drain("scalar", inc="aS")
        W("tensor", "aS", h1_land)
        if t8 == 0:
            prog.hwait("tensor", w2_loaded)
        ab_ev = []
        for of2 in range(8):
            p = ps_b[:, (of2 % 3) * 512:(of2 % 3) * 512 + 512]
            if of2 >= 3:
                W("tensor", "aS", ab_ev[of2 - 3])
            for c in range(32):
                E("tensor", lambda e, p=p, c=c, o=of2: e.matmul(
                    out=p[:], lhsT=w2T_sb[:, c * 1024 + o * 128:c * 1024 + (o + 1) * 128],
                    rhs=h1_sb[:, c * 512:(c + 1) * 512],
                    start=(c == 0), stop=(c == 31)), inc="tS" if c == 31 else None)
            W("scalar", "tS", prog.cnt["tS"])
            asl = of2 % 2
            if ab_st[asl]:
                W("scalar", "dA", ab_st[asl])
            E("scalar", lambda e, p=p, asl=asl: e.copy(
                out=ab_evac[asl][:], in_=p[:]))
            ae = prog.drain("scalar", inc="aS")
            ab_ev.append(ae)
            ab_st[asl] = prog.dma("scalar",
                bass.AP(abuf.tensor, of2 * 128 * NTOK + t8 * 512, [[NTOK, 128], [1, 512]]),
                ab_evac[asl][:])
        mlp_of2_done = prog.cnt["aS"]
        h1_done = prog.cnt["tS"]

    if phases < 6:
        finish()
        return nc

    W("sync", "dG", prog.cnt["dG"])
    W("sync", "tS", h1_done)
    for c in range(NIC - 1):
        DMA("sync", wdT_sb[:, c * 1024:(c + 1) * 1024], wd_park[c * 128:(c + 1) * 128, :])
    wd_loaded = prog.dma_handoff("sync", wdT_sb[:, (NIC - 1) * 1024:NIC * 1024],
                                 wd_park[(NIC - 1) * 128:NIC * 128, :])
    W("vector", "vS", s2_done)

    q_done = [[0] * NIC, [0] * NIC]
    x_ld = [0] * 4
    ab_ld = [0, 0]
    merge_done = [0, 0]
    out_dma = [0, 0, 0, 0]
    dps_ev = []
    dps_done = [0, 0]
    hbufs = [h_int, h_int2]

    def emit_hquant(t8, ic0, ic1):
        hb = hbufs[t8 % 2]
        for ic in range(ic0, min(ic1, NIC)):
            sl = ic % 2
            prev = q_done[t8 % 2][ic - 2] if ic >= 2 else q_done[(t8 - 1) % 2][NIC - 2 + ic]
            if prev:
                W("sync", "vS", prev)
            gl = prog.dma_handoff("sync", g_slab[sl][:],
                     bass.AP(g_spill.tensor, ic * 128 * NTOK + t8 * 512, [[NTOK, 128], [1, 512]]))
            prog.hwait("vector", gl)
            if ic == 0 and dps_done[t8 % 2]:
                W("vector", "tS", dps_done[t8 % 2])
            E("vector", lambda e, sl=sl: e.tensor_scalar(
                out=qint6[:], in0=g_slab[sl][:], scalar1=Sc(C_RS2),
                scalar2=None, op0=AluOpType.mult))
            E("vector", lambda e, ic=ic, hb=hb: e.tensor_scalar(
                out=hb[:, ic * 512:(ic + 1) * 512], in0=qint6[:], scalar1=-128.0,
                scalar2=127.0, op0=AluOpType.max, op1=AluOpType.min))
            q_done[t8 % 2][ic] = prog.drain("vector", inc="vS")

    emit_hquant(0, 0, NIC)
    for t8 in range(8):
        hb = hbufs[t8 % 2]
        hq_all = q_done[t8 % 2][NIC - 1]
        for tc in range(4):
            if out_dma[tc]:
                W("sync", "dG", out_dma[tc])
            x_ld[tc] = prog.dma_handoff("sync", x_sb[tc][:],
                           x[(t8 * 4 + tc) * 128:(t8 * 4 + tc + 1) * 128, :])
        W("tensor", "vS", hq_all)
        if t8 == 0:
            prog.hwait("tensor", wd_loaded)
        tadd = [0] * 8
        for of in range(8):
            if t8 < 7:
                emit_hquant(t8 + 1, of * 3, of * 3 + 3)
            dps = ps_a[:, (of % 2) * 512:(of % 2) * 512 + 512]
            if len(dps_ev) >= 2:
                W("tensor", "vS", dps_ev[-2])
            for ic in range(NIC):
                E("tensor", lambda e, p=dps, ic=ic, o=of, hb=hb: e.matmul(
                    out=p[:], lhsT=wdT_sb[:, ic * 1024 + o * 128:ic * 1024 + (o + 1) * 128],
                    rhs=hb[:, ic * 512:(ic + 1) * 512],
                    start=(ic == 0), stop=(ic == NIC - 1)), inc="tS" if ic == NIC - 1 else None)
            dmm = prog.cnt["tS"]
            asl = of % 2
            if ab_ld[asl]:
                W("scalar", "vS", merge_done[asl])
            ab_ld[asl] = prog.dma_handoff("scalar", ab_slab[asl][:],
                bass.AP(abuf.tensor, of * 128 * NTOK + t8 * 512, [[NTOK, 128], [1, 512]]))
            W("vector", "tS", dmm)
            prog.hwait("vector", ab_ld[asl])
            E("vector", lambda e, p=dps, o=of, asl=asl: e.scalar_tensor_tensor(
                out=sum_sb[asl][:], in0=p[:], scalar=S[:, C_CBD + o:C_CBD + o + 1],
                in1=ab_slab[asl][:], op0=AluOpType.mult, op1=AluOpType.add))
            mg = prog.drain("vector", inc="vS")
            dps_ev.append(mg)
            merge_done[asl] = mg
            W("tensor", "vS", mg)
            tbank = ps_b[:, (of % 2) * 512:(of % 2) * 512 + 512]
            if of >= 2:
                W("tensor", "vS", tadd[of - 2])
            for tc in range(4):
                E("tensor", lambda e, tb=tbank, tc=tc, asl=asl: e.transpose(
                    out=tb[:, tc * 128:(tc + 1) * 128],
                    in_=sum_sb[asl][:, tc * 128:(tc + 1) * 128],
                    identity=ident_f32[:]), inc="tS" if tc == 3 else None)
            ttp = prog.cnt["tS"]
            W("vector", "tS", ttp)
            for tc in range(4):
                prog.hwait("vector", x_ld[tc])
                E("vector", lambda e, tb=tbank, tc=tc, o=of: e.tensor_tensor(
                    out=out_sb[tc][:, o * 128:(o + 1) * 128],
                    in0=tb[:, tc * 128:(tc + 1) * 128],
                    in1=x_sb[tc][:, o * 128:(o + 1) * 128], op=AluOpType.add),
                    inc="vS" if tc == 3 else None)
            tadd[of] = prog.cnt["vS"]
        dps_done[t8 % 2] = prog.cnt["tS"]
        all_adds = prog.drain("vector", inc="vS")
        W("gpsimd", "vS", all_adds)
        for tc in range(4):
            out_dma[tc] = DMA("gpsimd", y[(t8 * 4 + tc) * 128:(t8 * 4 + tc + 1) * 128, :],
                              out_sb[tc][:])

    finish()
    return nc


def _emit_attention(ctx):
    (prog, s, q_rope, k_rope, v_tm, probs_tm, probs_T, o_sb, o_park,
     mask_main, mask_first, ident_bf, ps_a, ps_b, S,
     C_MPACK, C_DPACK, C_RDPACK, rope_done, v_done, st) = ctx
    E, W, DMA = prog.emit, prog.wait, prog.dma
    PS = [ps_a, ps_b]
    PSB = [ps_a.bitcast(mybir.dt.bfloat16), ps_b.bitcast(mybir.dt.bfloat16)]
    T_ = T
    NK = 32

    def half_info(k):
        return k // 2, (k % 2) * 8

    for k in range(NK + 2):
        if k < NK:
            qc, h0 = half_info(k)
            ps = PS[k % 2]
            W("tensor", "vS", rope_done)
            if k >= 3:
                W("tensor", "aS", st["oev"][k - 3])
            if k >= 2:
                W("tensor", "aS", st["exp"][k - 2])
            else:
                W("tensor", "aS", v_done)
            for ih in range(8):
                h = h0 + ih
                j, g = h % 4, h // 4
                tile = ps[:, ih * 256:(ih + 1) * 256]
                qx1 = q_rope[32 * g:32 * g + 32, j * T_ + qc * 128: j * T_ + (qc + 1) * 128]
                qx2 = q_rope[32 * g:32 * g + 32, (4 + j) * T_ + qc * 128: (4 + j) * T_ + (qc + 1) * 128]
                if qc == 0:
                    kx1 = k_rope[32 * g:32 * g + 32, 0:128]
                    kx2 = k_rope[32 * g:32 * g + 32, T_:T_ + 128]
                    out = tile[:, 128:256]
                    mask = mask_first
                else:
                    kx1 = k_rope[32 * g:32 * g + 32, (qc - 1) * 128:(qc + 1) * 128]
                    kx2 = k_rope[32 * g:32 * g + 32, T_ + (qc - 1) * 128:T_ + (qc + 1) * 128]
                    out = tile
                    mask = mask_main
                E("tensor", lambda e, o=out, q=qx1, kk=kx1, g=g: e.matmul(
                    out=o, lhsT=q, rhs=kk, start=True, stop=False, tile_position=(32 * g, 0)))
                E("tensor", lambda e, o=out, q=qx2, kk=kx2, g=g: e.matmul(
                    out=o, lhsT=q, rhs=kk, start=False, stop=False, tile_position=(32 * g, 0)))
                E("tensor", lambda e, t=tile, m=mask: e.matmul(
                    out=t, lhsT=ident_bf[:], rhs=m[:], start=False, stop=True,
                    tile_position=(0, 0)), inc="tS" if ih == 7 else None)
            st["stats"].append(prog.cnt["tS"])
            W("vector", "tS", st["stats"][k])
            mp = C_MPACK + 8 * (k % 2)
            dp = C_DPACK + 8 * (k % 2)
            rp = C_RDPACK + 8 * (k % 2)
            for ih in range(8):
                E("vector", lambda e, t=ps[:, ih * 256:(ih + 1) * 256], c=mp + ih: e.tensor_reduce(
                    out=S[:, c:c + 1], in_=t, axis=AX, op=AluOpType.max, negate=True))
            st["red"].append(prog.drain("vector", inc="vS"))
            W("scalar", "vS", st["red"][k])
            if k >= 2:
                prog.hwait("scalar", st["tp"][k - 2][0])
                prog.hwait("scalar", st["tp"][k - 2][1])
            for ih in range(8):
                E("scalar", lambda e, t=ps[:, ih * 256:(ih + 1) * 256], ih=ih, c=mp + ih,
                  d=dp + ih, k=k: e.activation(
                    out=probs_tm[:, (k % 2) * 2048 + ih * 256:(k % 2) * 2048 + (ih + 1) * 256],
                    in_=t, func=ACT.Exp, bias=S[:, c:c + 1], accum_out=S[:, d:d + 1]))
            st["exp"].append(prog.drain("scalar", inc="aS"))
            W("vector", "aS", st["exp"][k])
            E("vector", lambda e, rp=rp, dp=dp: e.reciprocal(
                out=S[:, rp:rp + 8], in_=S[:, dp:dp + 8]))
            prog.drain("vector")
            lo = 128 if qc == 0 else 0
            for ih in range(8):
                E("vector", lambda e, ih=ih, k=k, lo=lo, rp=rp: e.tensor_scalar(
                    out=probs_tm[:, (k % 2) * 2048 + ih * 256 + lo:(k % 2) * 2048 + (ih + 1) * 256],
                    in0=probs_tm[:, (k % 2) * 2048 + ih * 256 + lo:(k % 2) * 2048 + (ih + 1) * 256],
                    scalar1=S[:, rp + ih:rp + ih + 1], scalar2=None, op0=AluOpType.mult))
            st["norm"].append(prog.drain("vector", inc="vS"))

        if k >= 2:
            j = k - 2
            qc, h0 = half_info(j)
            ava = PS[(j + 1) % 2]
            W("tensor", "aS", st["exp"][min(j + 1, NK - 1)])
            prog.hwait("tensor", st["tp"][j][0])
            prog.hwait("tensor", st["tp"][j][1])
            W("tensor", "aS", v_done)
            for ih in range(8):
                h = h0 + ih
                g = h // 4
                p = ih // 2
                cg = 64 * (h % 2)
                out = ava[cg:cg + 64, 1024 + p * 128:1024 + (p + 1) * 128]
                if qc == 0:
                    E("tensor", lambda e, o=out, g=g, cg=cg, ih=ih, j=j: e.matmul(
                        out=o, lhsT=v_tm[:, 64 * g: 64 * g + 64],
                        rhs=probs_T[:, (j % 2) * 2048 + (2 * ih + 1) * 128:(j % 2) * 2048 + (2 * ih + 2) * 128],
                        start=True, stop=True, tile_position=(0, cg)), inc="tS" if ih == 7 else None)
                else:
                    E("tensor", lambda e, o=out, g=g, cg=cg, ih=ih, j=j, qc=qc: e.matmul(
                        out=o, lhsT=v_tm[:, (qc - 1) * 256 + 64 * g: (qc - 1) * 256 + 64 * g + 64],
                        rhs=probs_T[:, (j % 2) * 2048 + 2 * ih * 128:(j % 2) * 2048 + (2 * ih + 1) * 128],
                        start=True, stop=False, tile_position=(0, cg)))
                    E("tensor", lambda e, o=out, g=g, cg=cg, ih=ih, j=j, qc=qc: e.matmul(
                        out=o, lhsT=v_tm[:, qc * 256 + 64 * g: qc * 256 + 64 * g + 64],
                        rhs=probs_T[:, (j % 2) * 2048 + (2 * ih + 1) * 128:(j % 2) * 2048 + (2 * ih + 2) * 128],
                        start=False, stop=True, tile_position=(0, cg)), inc="tS" if ih == 7 else None)
            st["av"].append(prog.cnt["tS"])
            W("scalar", "tS", st["av"][j])
            slot = o_sb[qc % 2]
            if j % 2 == 0 and st["o_dma"][qc % 2]:
                W("scalar", "dG", st["o_dma"][qc % 2])
            for p in range(4):
                pa = 4 * (j % 2) + p
                E("scalar", lambda e, pa=pa, p=p, sl=slot, ava=ava: e.copy(
                    out=sl[:, pa * 128:(pa + 1) * 128],
                    in_=ava[:, 1024 + p * 128:1024 + (p + 1) * 128]))
            st["oev"].append(prog.drain("scalar", inc="aS"))
            if j % 2 == 1:
                W("gpsimd", "aS", st["oev"][j])
                st["o_dma"][qc % 2] = DMA(
                    "gpsimd",
                    bass.AP(o_park.tensor, s * T_ + qc * 128,
                            [[NTOK, 128], [128 * NTOK, 8], [1, 128]]),
                    slot[:])
        if k >= 1 and k - 1 < NK:
            kb = k - 1
            qc, h0 = half_info(kb)
            sides = (1,) if qc == 0 else (0, 1)
            for eng in ("sync", "scalar"):
                W(eng, "vS", st["norm"][kb])
                if kb >= 2:
                    W(eng, "tS", st["av"][kb - 2])
            i = 0
            for ih in range(8):
                for side in sides:
                    ti = 2 * ih + side
                    eng = "sync" if i % 2 == 0 else "scalar"
                    DMA(eng,
                        probs_T[:, (kb % 2) * 2048 + ti * 128:(kb % 2) * 2048 + (ti + 1) * 128],
                        probs_tm[:, (kb % 2) * 2048 + ih * 256 + side * 128:
                                 (kb % 2) * 2048 + ih * 256 + side * 128 + 128],
                        transpose=True)
                    i += 1
            hs1 = prog.queue_handshake("sync")
            hs2 = prog.queue_handshake("scalar")
            st["tp"].append((hs1, hs2))
    st["last_oev"] = st["oev"][-1]
    st["last_stats"] = st["stats"][-1]
    for key in ("stats", "red", "exp", "norm", "tp", "ev", "av", "oev"):
        st[key] = []



def kernel(**inputs):
    xfull = np.asarray(inputs["L0_attn_query"], np.float32)
    prep = host_prep(inputs)
    nc = build_nc(phases=6)
    in_maps = []
    for c in range(NCORES):
        m = {"x": np.ascontiguousarray(xfull[2 * c:2 * c + 2].reshape(NTOK, D))}
        m.update(prep)
        in_maps.append(m)
    res = run_bass_kernel_spmd(nc, in_maps, core_ids=list(range(NCORES)))
    return np.concatenate([r["y"].reshape(2, T, D) for r in res.results], axis=0)


# version 20 (final)
# speedup vs baseline: 1.1351x; 1.0027x over previous
import numpy as np
import concourse.bass as bass
import concourse.mybir as mybir
from concourse.alu_op_type import AluOpType
from concourse.bass_utils import run_bass_kernel_spmd

F32 = mybir.dt.float32
BF16 = mybir.dt.bfloat16
I32 = mybir.dt.int32
ACT = mybir.ActivationFunctionType
AX = mybir.AxisListType.X

D = 1024
T = 2048
NSEQ = 2
NTOK = NSEQ * T
NH = 16
HD = 64
DFF = 2816
DMLP = 4096
NCHUNK = NTOK // 128
NIC = DFF // 128
QMAX = 127.0
NCORES = 8

ENGINES = ("sync", "tensor", "vector", "scalar", "gpsimd")



def _perm_q():
    perm = np.zeros(D, dtype=np.int64)
    for half in range(2):
        for j in range(4):
            for g in range(4):
                for i in range(32):
                    perm[half * 512 + j * 128 + g * 32 + i] = (4 * g + j) * 64 + half * 32 + i
    return perm


def _perm_k():
    perm = np.zeros(256, dtype=np.int64)
    for half in range(2):
        for g in range(4):
            for i in range(32):
                perm[half * 128 + g * 32 + i] = g * 64 + half * 32 + i
    return perm


def host_prep(inputs):
    import ml_dtypes
    d = {}
    wq = np.asarray(inputs["L0_attn_Wq"], np.float32)
    wk = np.asarray(inputs["L0_attn_Wk"], np.float32)
    wv = np.asarray(inputs["L0_attn_Wv"], np.float32)
    d["wqT"] = np.ascontiguousarray(wq[_perm_q(), :].T)
    d["wkT"] = np.ascontiguousarray(wk[_perm_k(), :].T)
    d["wvT"] = np.ascontiguousarray(wv.T)
    d["w1T"] = np.ascontiguousarray(np.asarray(inputs["L0_attn_W1"], np.float32).T)
    d["w2T"] = np.ascontiguousarray(np.asarray(inputs["L0_attn_W2"], np.float32).T)
    d["wg_n"] = np.asarray(inputs["L0_ffn_w_gate"], np.float32)
    d["wu_n"] = np.asarray(inputs["L0_ffn_w_up"], np.float32)
    d["wd_n"] = np.asarray(inputs["L0_ffn_w_down"], np.float32)
    d["wgT"] = np.ascontiguousarray(d["wg_n"].T)
    d["wuT"] = np.ascontiguousarray(d["wu_n"].T)
    d["wdT"] = np.ascontiguousarray(d["wd_n"].T)
    d["clip_x"] = np.asarray(inputs["L0_ffn_clip_x"], np.float32).reshape(1, 1)
    d["clip_down"] = np.asarray(inputs["L0_ffn_clip_down"], np.float32).reshape(1, 1)

    inv = 1.0 / 10000.0 ** (np.arange(0, HD, 2, dtype=np.float64) / HD)
    fr = np.outer(inv, np.arange(T, dtype=np.float64))
    d["cos_t"] = np.tile(np.cos(fr), (4, 1)).astype(np.float32)
    d["sin_t"] = np.tile(np.sin(fr), (4, 1)).astype(np.float32)

    q = np.arange(128)[:, None]
    kk = np.arange(256)[None, :]
    NEG = np.float32(-1e30)
    allow_main = np.where(kk < 128, kk >= q + 1, (kk - 128) <= q)
    allow_first = np.where(kk < 128, False, (kk - 128) <= q)
    d["mask_main"] = np.where(allow_main, np.float32(0), NEG).astype(ml_dtypes.bfloat16)
    d["mask_first"] = np.where(allow_first, np.float32(0), NEG).astype(ml_dtypes.bfloat16)
    d["ident_bf"] = np.eye(128, dtype=ml_dtypes.bfloat16)
    d["ident_f32"] = np.eye(128, dtype=np.float32)
    return d



class Prog:
    def __init__(self, nc):
        self.nc = nc
        self.items = {e: [] for e in ENGINES}
        self.sems = {}
        self.cnt = {}

    def sem(self, name):
        h = self.nc.semaphore(name).__enter__()
        self.sems[name] = h
        self.cnt[name] = 0

    def emit(self, engine, fn, inc=None, inc_amt=1):
        if inc is not None:
            sem = self.sems[inc]
            self.cnt[inc] += inc_amt

            def wrapped(eng, fn=fn, sem=sem, amt=inc_amt):
                fn(eng).then_inc(sem, amt)
            self.items[engine].append(wrapped)
        else:
            self.items[engine].append(fn)
        return self.cnt[inc] if inc is not None else None

    def wait(self, engine, sem_name, value=None):
        if value is None:
            value = self.cnt[sem_name]
        sem = self.sems[sem_name]
        self.items[engine].append(lambda eng, sem=sem, v=value: eng.wait_ge(sem, v))

    def vnop(self):
        self.items["vector"].append(lambda eng: eng.nop())

    HSEM = {"sync": "hSs", "scalar": "hSa", "gpsimd": "hSg"}

    def dma_handoff(self, engine, out, in_, **kw):
        c = self.dma(engine, out, in_, **kw)
        qsem = {"sync": "dS", "scalar": "dA", "gpsimd": "dG"}[engine]
        hs = self.HSEM[engine]
        self.wait(engine, qsem, c)
        self.emit(engine, lambda e, hs=hs: e.sem_inc(self.sems[hs], 1), inc=None)
        return (hs, self._bump(hs))

    def hwait(self, engine, h):
        self.wait(engine, h[0], h[1])

    def queue_handshake(self, engine):
        qsem = {"sync": "dS", "scalar": "dA", "gpsimd": "dG"}[engine]
        hs = self.HSEM[engine]
        self.wait(engine, qsem, self.cnt[qsem])
        self.emit(engine, lambda e, hs=hs: e.sem_inc(self.sems[hs], 1), inc=None)
        return (hs, self._bump(hs))

    def _bump(self, name):
        self.cnt[name] += 1
        return self.cnt[name]

    def drain(self, engine, inc=None):
        if inc is not None:
            return self.emit(engine, lambda e: e.drain(), inc=inc)
        self.items[engine].append(lambda eng: eng.drain())
        return None

    def dma(self, engine, out, in_, sem=None, **kw):
        if sem is None:
            sem = {"sync": "dS", "scalar": "dA", "gpsimd": "dG"}[engine]
        return self.emit(engine, lambda e: e.dma_start(out=out, in_=in_, **kw),
                         inc=sem, inc_amt=16)

    def replay(self, block):
        @block.sync
        def _(eng):
            for fn in self.items["sync"]:
                fn(eng)

        @block.tensor
        def _(eng):
            for fn in self.items["tensor"]:
                fn(eng)

        @block.vector
        def _(eng):
            for fn in self.items["vector"]:
                fn(eng)

        @block.scalar
        def _(eng):
            for fn in self.items["scalar"]:
                fn(eng)

        @block.gpsimd
        def _(eng):
            for fn in self.items["gpsimd"]:
                fn(eng)



def build_nc(phases=6, debug=()):
    nc = bass.Bass("TRN2", target_bir_lowering=False)

    x = nc.dram_tensor("x", [NTOK, D], F32, kind="ExternalInput").ap()
    y = nc.dram_tensor("y", [NTOK, D], F32, kind="ExternalOutput").ap()
    din = {}
    for name, shape in [
        ("wqT", [D, D]), ("wkT", [D, 256]), ("wvT", [D, 256]),
        ("w1T", [D, DMLP]), ("w2T", [DMLP, D]),
        ("wg_n", [DFF, D]), ("wu_n", [DFF, D]), ("wd_n", [D, DFF]),
        ("wgT", [D, DFF]), ("wuT", [D, DFF]), ("wdT", [DFF, D]),
        ("clip_x", [1, 1]), ("clip_down", [1, 1]),
        ("cos_t", [128, T]), ("sin_t", [128, T]),
    ]:
        din[name] = nc.dram_tensor(name, shape, F32, kind="ExternalInput").ap()
    for name, shape in [("mask_main", [128, 256]), ("mask_first", [128, 256]),
                        ("ident_bf", [128, 128])]:
        din[name] = nc.dram_tensor(name, shape, BF16, kind="ExternalInput").ap()
    din["ident_f32"] = nc.dram_tensor("ident_f32", [128, 128], F32, kind="ExternalInput").ap()

    def scratch(name, shape, dt):
        kind = "ExternalOutput" if name in debug else "Internal"
        return nc.dram_tensor(name, shape, dt, kind=kind).ap()

    wg_park = scratch("wg_park", [D, DFF], BF16)
    wu_park = scratch("wu_park", [D, DFF], BF16)
    wd_park = scratch("wd_park", [DFF, D], BF16)
    g_spill = scratch("g_spill", [DFF, NTOK], F32)
    o_park = scratch("o_park", [D, NTOK], BF16)
    abuf = scratch("abuf", [D, NTOK], F32)
    scl_pack = nc.dram_tensor("scl_pack", [64, 128], F32).ap()
    cc1_in = nc.dram_tensor("cc1_in", [128, 128], F32).ap()
    cc1_out = nc.dram_tensor("cc1_out", [128, 128], F32).ap()
    cc2_in = nc.dram_tensor("cc2_in", [128, 128], F32).ap()
    cc2_out = nc.dram_tensor("cc2_out", [128, 128], F32).ap()
    dbg = {}
    for nm, shape, dt in [("smA", [128, 888], F32), ("smB", [128, 888], F32),
                          ("smC", [128, 888], F32),
                          ("cc1i", [128, 128], F32), ("cc1o", [128, 128], F32),
                          ("cc2i", [128, 128], F32), ("cc2o", [128, 128], F32),
                          ("recips", [128, DFF], F32), ("xnT", [128, 8 * NTOK], BF16), ("xq", [128, 8 * NTOK], BF16),
                          ("q", [128, 8 * T], BF16), ("k", [128, 2 * T], BF16),
                          ("v", [128, 16 * 256], BF16), ("smalls", [128, 888], F32)]:
        if nm in debug:
            dbg[nm] = nc.dram_tensor("dbg_" + nm, shape, dt, kind="ExternalOutput").ap()

    SB_BASE = 16512
    def sb(name, off, shape, dt):
        return nc.alloc_sbuf_tensor_at(name, shape, dt, offset=SB_BASE + off).ap()

    xnT = sb("xnT", 0, [128, 8 * NTOK], BF16)
    w1T_sb = sb("w1T_sb", 0, [128, 8 * DMLP], BF16)
    w1T_a = sb("w1T_a", 98304, [128, 4 * DMLP], BF16)
    wdT_sb = sb("wdT_sb", 0, [128, NIC * D], BF16)
    h_int = sb("h_int", 45056, [128, NIC * 512], BF16)
    h_int2 = sb("h_int2", 112640, [128, NIC * 512], BF16)
    qint6 = sb("qint6", 135168, [128, 512], I32)
    g_slab = [sb(f"g_slab{i}", 67584 + i * 2048, [128, 512], F32) for i in range(2)]
    ab_slab = [sb(f"ab_slab{i}", 71680 + i * 2048, [128, 512], F32) for i in range(2)]
    sum_sb = [sb(f"sum_sb{i}", 75776 + i * 2048, [128, 512], F32) for i in range(2)]
    out_sb = [sb(f"out_sb{i}", 79872 + i * 4096, [128, D], F32) for i in range(4)]
    x_sb = [sb(f"x_sb{i}", 96256 + i * 4096, [128, D], F32) for i in range(4)]

    w2T_sb = sb("w2T_sb", 65536, [128, 32 * D], BF16)
    wnat = [sb(f"wnat{i}", 65536 + i * 11264, [128, DFF], F32) for i in range(2)]
    wTb = [sb(f"wTb{i}", 88064 + i * 11264, [128, DFF], F32) for i in range(2)]
    wint32 = sb("wint32", 110592, [128, DFF], I32)
    wintbf = sb("wintbf", 121856, [128, DFF], BF16)
    recip_gb = sb("recip_gb", 127488, [128, DFF], F32)
    recip_ub = sb("recip_ub", 138752, [128, DFF], F32)
    recip_db = sb("recip_db", 150016, [128, D], F32)
    xbuf = [sb(f"xbuf{i}", 154112 + i * 4096, [128, D], F32) for i in range(2)]
    xn_tm = [sb(f"xn_tm{i}", 162304 + i * 2048, [128, D], BF16) for i in range(2)]
    q_rope = sb("q_rope", 65536, [128, 8 * T], BF16)
    k_rope = sb("k_rope", 98304, [128, 2 * T], BF16)
    v_tm = sb("v_tm", 106496, [128, 16 * 256], BF16)
    qraw = sb("qraw", 114688, [128, 10 * 512], F32)
    rope_t = [sb(f"rope_t{i}", 135168 + i * 2048, [128, 512], F32) for i in range(2)]
    wvT_sb = sb("wvT_sb", 150016, [128, 8 * 256], BF16)
    wqT_sb = sb("wqT_sb", 166400, [128, 8 * 1024], BF16)
    wkT_sb = sb("wkT_sb", 182784, [128, 8 * 256], BF16)
    sq_scr = sb("sq_scr", 186880, [128, D], F32)
    xq_int = sb("xq_int", 114688, [128, 4096], I32)
    probs_tm = sb("probs_tm", 114688, [128, NH * 256], BF16)
    probs_T = sb("probs_T", 122880, [128, NH * 256], BF16)
    o_sb = [sb(f"o_sb{i}", 131072 + i * 2048, [128, D], BF16) for i in range(2)]
    wgi_sb = [sb(f"wgi_sb{i}", 131072 + i * 2048, [128, 8 * 128], BF16) for i in range(2)]
    wui_sb = [sb(f"wui_sb{i}", 135168 + i * 2048, [128, 8 * 128], BF16) for i in range(2)]
    gate_f32 = sb("gate_f32", 139264, [128, 4096], F32)
    gup_f32 = sb("gup_f32", 155648, [128, 4096], F32)
    h1_sb = sb("h1_sb", 131072, [128, 32 * 512], BF16)
    o_sb2 = [sb(f"o_sb2{i}", 163840 + i * 8192, [128, 8 * 512], BF16) for i in range(2)]
    ab_evac = [sb(f"ab_evac{i}", 180224 + i * 2048, [128, 512], F32) for i in range(2)]

    PT = 191104
    cos_sb = sb("cos_sb", PT, [128, T], F32)
    sin_sb = sb("sin_sb", PT + 8192, [128, T], F32)
    mask_main = sb("mask_main_sb", PT + 16384, [128, 256], BF16)
    mask_first = sb("mask_first_sb", PT + 16896, [128, 256], BF16)
    ident_bf = sb("ident_bf_sb", PT + 17408, [128, 128], BF16)
    ident_f32 = sb("ident_f32_sb", PT + 17664, [128, 128], F32)
    S = sb("smalls", PT + 18176, [128, 888], F32)

    (C_GMAX1, C_MEAN, C_MSQ, C_VAR, C_RSTD, C_MX, C_MN, C_T0, C_T1, C_SUM,
     C_SQS, C_S1, C_RS1, C_SIGX, C_SIGD, C_GMAX2, C_S2, C_RS2, C_CLX,
     C_CLD) = range(20)
    C_MPACK, C_DPACK, C_RDPACK = 32, 48, 64
    C_SUMP, C_SQSP, C_MXP, C_MNP = 600, 632, 664, 696
    C_MEANP, C_MSQP, C_VARP, C_RSTDP, C_AMP = 728, 760, 792, 824, 856
    C_RPG, C_RPU, C_RPD = 96, 118, 140
    C_CBG, C_CBU, C_CBD = 192, 224, 256
    C_BCR = 384
    C_SWG, C_SWU, C_SWD = 512, 544, 576

    def Sc(col, n=1):
        return S[:, col:col + n]

    ps_a = nc.alloc_psum_tensor("ps_a", [128, 2048], F32).ap()
    ps_b = nc.alloc_psum_tensor("ps_b", [128, 2048], F32).ap()
    pst = ps_a.bitcast(BF16)[:, 0:1024]

    prog = Prog(nc)
    for s_ in ("dS", "dA", "dG", "vS", "aS", "tS", "gS", "hSs", "hSa", "hSg"):
        prog.sem(s_)
    E, W, DMA = prog.emit, prog.wait, prog.dma

    DMA("sync", cos_sb[:], din["cos_t"])
    DMA("sync", sin_sb[:], din["sin_t"])
    DMA("sync", mask_main[:], din["mask_main"])
    DMA("sync", mask_first[:], din["mask_first"])
    DMA("sync", ident_bf[:], din["ident_bf"])
    DMA("sync", ident_f32[:], din["ident_f32"])
    DMA("sync", Sc(C_CLX), bass.AP(din["clip_x"].tensor, 0, [[0, 128], [1, 1]]))
    idf_loaded = prog.cnt["dS"] + 16
    c0 = prog.dma_handoff("sync", Sc(C_CLD), bass.AP(din["clip_down"].tensor, 0, [[0, 128], [1, 1]]))
    prog.hwait("vector", c0)
    E("vector", lambda e: e.memset(Sc(C_GMAX1), 0.0))
    E("vector", lambda e: e.memset(Sc(C_GMAX2), 0.0), inc="vS")
    prog.hwait("scalar", c0)
    E("scalar", lambda e: e.activation(out=Sc(C_SIGX), in_=Sc(C_CLX), func=ACT.Sigmoid))
    sig_done = E("scalar", lambda e: e.activation(out=Sc(C_SIGD), in_=Sc(C_CLD), func=ACT.Sigmoid), inc="aS")

    for ifc in range(8):
        DMA("gpsimd", wqT_sb[:, ifc * 1024:(ifc + 1) * 1024], din["wqT"][ifc * 128:(ifc + 1) * 128, :])
    for ifc in range(7):
        DMA("gpsimd", wkT_sb[:, ifc * 256:(ifc + 1) * 256], din["wkT"][ifc * 128:(ifc + 1) * 128, :])
    wqk_loaded = prog.dma_handoff("gpsimd", wkT_sb[:, 7 * 256:8 * 256], din["wkT"][7 * 128:8 * 128, :])

    C_SWGN, C_SWUN, C_SWDN = 302, 324, 346
    wq_jobs = []
    for src, cS, cN, nchunk, width in ((din["wg_n"], C_SWG, C_SWGN, NIC, D),
                                       (din["wu_n"], C_SWU, C_SWUN, NIC, D),
                                       (din["wd_n"], C_SWD, C_SWDN, 8, DFF)):
        for c in range(nchunk):
            wq_jobs.append((src, cS + c, cN + c, c, width))
    nat_done = []
    for idx, (src, cc, ccn, c, width) in enumerate(wq_jobs):
        slot = wnat[idx % 2]
        if idx >= 2:
            W("sync", "vS", nat_done[idx - 2])
        cd = prog.dma_handoff("sync", slot[:, 0:width], src[c * 128:(c + 1) * 128, :])
        prog.hwait("vector", cd)
        nd = E("vector", lambda e, s_=slot, w=width, cc=cc: e.tensor_reduce(
            out=S[:, cc:cc + 1], in_=s_[:, 0:w], axis=AX, op=AluOpType.max,
            apply_absolute_value=True), inc="vS")
        nat_done.append(nd)
    prog.drain("vector")
    for base, n in ((C_SWG, NIC), (C_SWU, NIC), (C_SWD, 8)):
        E("vector", lambda e, b=base, n_=n: e.tensor_scalar(
            out=S[:, b:b + n_], in0=S[:, b:b + n_], scalar1=1.0 / QMAX,
            scalar2=None, op0=AluOpType.mult))
    prog.drain("vector")
    for dst, srcb, n in ((C_RPG, C_SWG, NIC), (C_RPU, C_SWU, NIC), (C_RPD, C_SWD, 8)):
        E("vector", lambda e, d=dst, s=srcb, n_=n: e.reciprocal(
            out=S[:, d:d + n_], in_=S[:, s:s + n_]))
    rp_done = prog.drain("vector", inc="vS")
    W("tensor", "vS", rp_done)
    W("tensor", "dS", idf_loaded)
    tp = E("tensor", lambda e: e.transpose(
        out=ps_b[0:64, 0:128], in_=S[:, C_RPG:C_RPG + 64], identity=ident_f32[:]), inc="tS")
    W("scalar", "tS", tp)
    E("scalar", lambda e: e.copy(
        out=S[0:64, C_BCR:C_BCR + 128], in_=ps_b[0:64, 0:128]))
    tpe = prog.drain("scalar", inc="aS")
    W("gpsimd", "aS", tpe)
    pk = DMA("gpsimd", scl_pack[0:64, :], S[0:64, C_BCR:C_BCR + 128])
    W("gpsimd", "dG", pk)
    DMA("gpsimd", recip_gb[:], bass.AP(scl_pack.tensor, 0, [[0, 128], [1, DFF]]))
    DMA("gpsimd", recip_ub[:], bass.AP(scl_pack.tensor, 22 * 128, [[0, 128], [1, DFF]]))
    bc_done = prog.dma_handoff("gpsimd", recip_db[:], bass.AP(scl_pack.tensor, 44 * 128, [[0, 128], [1, D]]))

    if "recips" in dbg:
        W("gpsimd", "dG", bc_done)
        DMA("gpsimd", dbg["recips"], recip_gb[:])
    wt_jobs = []
    for src, park, rb, nchunk, width in ((din["wgT"], wg_park, recip_gb, 8, DFF),
                                         (din["wuT"], wu_park, recip_ub, 8, DFF),
                                         (din["wdT"], wd_park, recip_db, NIC, D)):
        for c in range(nchunk):
            wt_jobs.append((src, park, rb, c, width))
    prog.hwait("sync", bc_done)
    t_done, park_done = [], []
    for idx, (src, park, rb, c, width) in enumerate(wt_jobs):
        slot = wTb[idx % 2]
        if idx >= 2:
            W("sync", "vS", t_done[idx - 2])
        cd = prog.dma_handoff("sync", slot[:, 0:width], src[c * 128:(c + 1) * 128, :])
        prog.hwait("vector", cd)
        if idx >= 1:
            W("vector", "dG", park_done[idx - 1])
        E("vector", lambda e, s=slot, r=rb, w=width: e.tensor_tensor(
            out=wint32[:, 0:w], in0=s[:, 0:w], in1=r[:, 0:w], op=AluOpType.mult))
        E("vector", lambda e, w=width: e.tensor_copy(
            out=wintbf[:, 0:w], in_=wint32[:, 0:w]))
        cv = prog.drain("vector", inc="vS")
        t_done.append(cv)
        W("gpsimd", "vS", cv)
        pd = DMA("gpsimd", park[c * 128:(c + 1) * 128, :], wintbf[:, 0:width])
        park_done.append(pd)

    pa_done = []
    pa_gs = []
    for kch in range(NCHUNK):
        slot = xbuf[kch % 2]
        if kch >= 2:
            W("sync", "vS", pa_done[kch - 2])
        cd = prog.dma_handoff("sync", slot[:], x[kch * 128:(kch + 1) * 128, :])
        prog.hwait("vector", cd)
        E("vector", lambda e, s=slot, k=kch: e.tensor_reduce(
            out=S[:, C_SUMP + k:C_SUMP + k + 1], in_=s[:], axis=AX, op=AluOpType.add))
        E("vector", lambda e, s=slot: e.tensor_tensor(
            out=sq_scr[:], in0=s[:], in1=s[:], op=AluOpType.mult))
        E("vector", lambda e, k=kch: e.tensor_reduce(
            out=S[:, C_SQSP + k:C_SQSP + k + 1], in_=sq_scr[:], axis=AX, op=AluOpType.add))
        E("vector", lambda e, s=slot, k=kch: e.tensor_reduce(
            out=S[:, C_MXP + k:C_MXP + k + 1], in_=s[:], axis=AX, op=AluOpType.max))
        nv = E("vector", lambda e, s=slot, k=kch: e.tensor_reduce(
            out=S[:, C_MNP + k:C_MNP + k + 1], in_=s[:], axis=AX, op=AluOpType.min), inc="vS")
        pa_done.append(nv)
        pa_gs.append(nv)
    DR = prog.drain
    E("vector", lambda e: e.tensor_scalar(
        out=S[:, C_MEANP:C_MEANP + 32], in0=S[:, C_SUMP:C_SUMP + 32],
        scalar1=1.0 / D, scalar2=None, op0=AluOpType.mult))
    E("vector", lambda e: e.tensor_scalar(
        out=S[:, C_MSQP:C_MSQP + 32], in0=S[:, C_SQSP:C_SQSP + 32],
        scalar1=1.0 / D, scalar2=None, op0=AluOpType.mult))
    DR("vector")
    E("vector", lambda e: e.tensor_tensor(
        out=S[:, C_VARP:C_VARP + 32], in0=S[:, C_MEANP:C_MEANP + 32],
        in1=S[:, C_MEANP:C_MEANP + 32], op=AluOpType.mult))
    DR("vector")
    E("vector", lambda e: e.tensor_tensor(
        out=S[:, C_VARP:C_VARP + 32], in0=S[:, C_MSQP:C_MSQP + 32],
        in1=S[:, C_VARP:C_VARP + 32], op=AluOpType.subtract))
    DR("vector")
    vv = DR("vector", inc="vS")
    E("vector", lambda e: e.tensor_scalar(
        out=S[:, C_VARP:C_VARP + 32], in0=S[:, C_VARP:C_VARP + 32],
        scalar1=1e-5, scalar2=None, op0=AluOpType.add))
    vv = DR("vector", inc="vS")
    W("scalar", "vS", vv)
    E("scalar", lambda e: e.activation(
        out=S[:, C_VARP:C_VARP + 32], in_=S[:, C_VARP:C_VARP + 32], func=ACT.Sqrt))
    sv = DR("scalar", inc="aS")
    W("vector", "aS", sv)
    E("vector", lambda e: e.reciprocal(
        out=S[:, C_RSTDP:C_RSTDP + 32], in_=S[:, C_VARP:C_VARP + 32]))
    DR("vector")
    E("vector", lambda e: e.tensor_tensor(
        out=S[:, C_MXP:C_MXP + 32], in0=S[:, C_MXP:C_MXP + 32],
        in1=S[:, C_MEANP:C_MEANP + 32], op=AluOpType.subtract))
    E("vector", lambda e: e.tensor_tensor(
        out=S[:, C_MNP:C_MNP + 32], in0=S[:, C_MEANP:C_MEANP + 32],
        in1=S[:, C_MNP:C_MNP + 32], op=AluOpType.subtract))
    DR("vector")
    E("vector", lambda e: e.tensor_tensor(
        out=S[:, C_AMP:C_AMP + 32], in0=S[:, C_MXP:C_MXP + 32],
        in1=S[:, C_MNP:C_MNP + 32], op=AluOpType.max))
    DR("vector")
    E("vector", lambda e: e.tensor_tensor(
        out=S[:, C_AMP:C_AMP + 32], in0=S[:, C_AMP:C_AMP + 32],
        in1=S[:, C_RSTDP:C_RSTDP + 32], op=AluOpType.mult))
    DR("vector")
    E("vector", lambda e: e.tensor_reduce(
        out=Sc(C_GMAX1), in_=S[:, C_AMP:C_AMP + 32], axis=AX, op=AluOpType.max))
    stats_done = DR("vector", inc="vS")

    W("sync", "vS", stats_done)
    pb_done = []
    tp_done = []
    ev_done = 0
    for kch in range(NCHUNK):
        slot = xbuf[kch % 2]
        nslot = xn_tm[kch % 2]
        if kch >= 2:
            W("sync", "vS", pb_done[kch - 2])
        cd = prog.dma_handoff("sync", slot[:], x[kch * 128:(kch + 1) * 128, :])
        prog.hwait("vector", cd)
        if kch == 0:
            W("vector", "vS", stats_done)
        if kch >= 2:
            W("vector", "tS", tp_done[kch - 2])
        nv = E("vector", lambda e, s=slot, n=nslot, k=kch: e.tensor_scalar(
            out=n[:], in0=s[:], scalar1=S[:, C_MEANP + k:C_MEANP + k + 1],
            scalar2=S[:, C_RSTDP + k:C_RSTDP + k + 1],
            op0=AluOpType.subtract, op1=AluOpType.mult), inc="vS")
        pb_done.append(nv)
        W("tensor", "vS", nv)
        if kch >= 1:
            W("tensor", "aS", ev_done)
        for t8 in range(8):
            E("tensor", lambda e, n=nslot, t=t8: e.transpose(
                out=pst[:, t * 128:(t + 1) * 128], in_=n[:, t * 128:(t + 1) * 128],
                identity=ident_bf[:]), inc="tS")
        tp_done.append(prog.cnt["tS"])
        W("scalar", "tS", prog.cnt["tS"])
        for t8 in range(8):
            E("scalar", lambda e, t=t8, k=kch: e.copy(
                out=xnT[:, t * NTOK + k * 128: t * NTOK + (k + 1) * 128],
                in_=pst[:, t * 128:(t + 1) * 128]), inc="aS")
        ev_done = prog.cnt["aS"]

    prog.drain("scalar", inc="aS")
    p1_aS = prog.cnt["aS"]
    p1_vS = prog.cnt["vS"]

    if "xnT" in dbg:
        W("gpsimd", "aS", p1_aS)
        DMA("gpsimd", dbg["xnT"], xnT[:])

    W("gpsimd", "vS", p1_vS)
    c1 = DMA("gpsimd", cc1_in[0:1, 0:128], Sc(C_GMAX1))
    W("gpsimd", "dG", c1)
    ccd = E("gpsimd", lambda e: e.collective_compute(
        "AllReduce", AluOpType.max, replica_groups=[list(range(NCORES))],
        ins=[cc1_in[0:1, 0:128].opt()], outs=[cc1_out[0:1, 0:128].opt()]), inc="gS")
    W("gpsimd", "gS", ccd)
    if "cc1i" in dbg:
        DMA("gpsimd", dbg["cc1i"], cc1_in)
        DMA("gpsimd", dbg["cc1o"], cc1_out)
    c2 = prog.dma_handoff("gpsimd", S[:, C_BCR:C_BCR + 128],
                          bass.AP(cc1_out.tensor, 0, [[0, 128], [1, 128]]))
    prog.hwait("vector", c2)
    E("vector", lambda e: e.tensor_reduce(
        out=Sc(C_S1), in_=S[:, C_BCR:C_BCR + 128], axis=AX, op=AluOpType.max))
    W("vector", "aS", sig_done)
    prog.drain("vector")
    E("vector", lambda e: e.tensor_tensor(
        out=Sc(C_S1), in0=Sc(C_S1), in1=Sc(C_SIGX), op=AluOpType.mult))
    prog.drain("vector")
    E("vector", lambda e: e.tensor_scalar(
        out=Sc(C_S1), in0=Sc(C_S1), scalar1=1.0 / QMAX, scalar2=None, op0=AluOpType.mult))
    prog.drain("vector")
    E("vector", lambda e: e.reciprocal(out=Sc(C_RS1), in_=Sc(C_S1)))
    prog.drain("vector")
    E("vector", lambda e: e.tensor_scalar(
        out=S[:, C_SUMP:C_SUMP + 32], in0=S[:, C_RSTDP:C_RSTDP + 32],
        scalar1=Sc(C_RS1), scalar2=None, op0=AluOpType.mult))
    s1_done = prog.drain("vector", inc="vS")
    E("vector", lambda e: e.tensor_scalar(
        out=S[:, C_CBG:C_CBG + NIC], in0=S[:, C_SWG:C_SWG + NIC],
        scalar1=Sc(C_S1), scalar2=None, op0=AluOpType.mult))
    E("vector", lambda e: e.tensor_scalar(
        out=S[:, C_CBU:C_CBU + NIC], in0=S[:, C_SWU:C_SWU + NIC],
        scalar1=Sc(C_S1), scalar2=None, op0=AluOpType.mult))
    combo_done = prog.drain("vector", inc="vS")

    def snap(tag):
        if tag in dbg:
            W("gpsimd", "vS", prog.cnt["vS"])
            W("gpsimd", "aS", prog.cnt["aS"])
            DMA("gpsimd", dbg[tag], S[:])

    def finish():
        if "smalls" in dbg:
            W("gpsimd", "vS", prog.cnt["vS"])
            W("gpsimd", "aS", prog.cnt["aS"])
            DMA("gpsimd", dbg["smalls"], S[:])
        if phases < 6:
            for k in range(NCHUNK):
                DMA("gpsimd", y[k * 128:(k + 1) * 128, :], x[k * 128:(k + 1) * 128, :])
        W("gpsimd", "dG", prog.cnt["dG"])
        W("gpsimd", "dS", prog.cnt["dS"])
        W("gpsimd", "dA", prog.cnt["dA"])
        with nc.Block() as block:
            prog.replay(block)

    snap("smA")
    if phases < 2:
        finish()
        return nc

    W("gpsimd", "vS", t_done[-1])
    for ifc in range(7):
        DMA("gpsimd", wvT_sb[:, ifc * 256:(ifc + 1) * 256], din["wvT"][ifc * 128:(ifc + 1) * 128, :])
    wv_loaded = prog.dma_handoff("gpsimd", wvT_sb[:, 7 * 256:8 * 256], din["wvT"][7 * 128:8 * 128, :])

    prog.hwait("tensor", max(wqk_loaded, wv_loaded, key=lambda t: t[1]))
    W("tensor", "aS", p1_aS)
    ps_q = [ps_a[:, i * 512:(i + 1) * 512] for i in range(4)]
    qev_cnt = []
    vev_cnt = []
    attn_st = {"o_dma": [0, 0], "stats": [], "red": [], "exp": [], "norm": [],
               "tp": [], "ev": [], "av": [], "oev": [], "last_oev": 0,
               "last_stats": 0}

    rope_tt_done = 0
    for s in range(NSEQ):
        if s > 0:
            W("vector", "tS", prog.cnt["tS"])
            W("scalar", "tS", prog.cnt["tS"])
            W("tensor", "aS", prog.cnt["aS"])
        for tt in range(4):
            for oc in range(10):
                pslot = ps_q[len(qev_cnt) % 4]
                if len(qev_cnt) >= 4:
                    W("tensor", "aS", qev_cnt[-4])
                for ifc in range(8):
                    if oc < 8:
                        lhs = wqT_sb[:, ifc * 1024 + oc * 128: ifc * 1024 + (oc + 1) * 128]
                    else:
                        lhs = wkT_sb[:, ifc * 256 + (oc - 8) * 128: ifc * 256 + (oc - 7) * 128]
                    E("tensor", lambda e, l=lhs, sl=pslot, i=ifc, s_=s, t_=tt: e.matmul(
                        out=sl[:], lhsT=l,
                        rhs=xnT[:, i * NTOK + s_ * T + t_ * 512: i * NTOK + s_ * T + (t_ + 1) * 512],
                        start=(i == 0), stop=(i == 7)), inc="tS" if ifc == 7 else None)
                W("scalar", "tS", prog.cnt["tS"])
                if oc == 0:
                    W("scalar", "vS", rope_tt_done)
                ec = E("scalar", lambda e, sl=pslot, o=oc: e.copy(
                    out=qraw[:, o * 512:(o + 1) * 512], in_=sl[:]), inc="aS")
                qev_cnt.append(ec)
            prog.drain("scalar", inc="aS")
            W("vector", "aS", prog.cnt["aS"])
            cosT = cos_sb[:, tt * 512:(tt + 1) * 512]
            sinT = sin_sb[:, tt * 512:(tt + 1) * 512]
            for j in range(5):
                if j < 4:
                    x1 = qraw[:, j * 512:(j + 1) * 512]
                    x2 = qraw[:, (4 + j) * 512:(5 + j) * 512]
                    d1 = q_rope[:, j * T + tt * 512: j * T + (tt + 1) * 512]
                    d2 = q_rope[:, (4 + j) * T + tt * 512: (4 + j) * T + (tt + 1) * 512]
                else:
                    x1 = qraw[:, 8 * 512:9 * 512]
                    x2 = qraw[:, 9 * 512:10 * 512]
                    d1 = k_rope[:, tt * 512:(tt + 1) * 512]
                    d2 = k_rope[:, T + tt * 512: T + (tt + 1) * 512]
                sc = 0.125 if j < 4 else 1.0
                E("vector", lambda e, a=x1, c=cosT, sc=sc: e.scalar_tensor_tensor(
                    out=rope_t[0][:], in0=a, scalar=sc, in1=c,
                    op0=AluOpType.mult, op1=AluOpType.mult))
                E("vector", lambda e, b=x2, sn=sinT, sc=sc: e.scalar_tensor_tensor(
                    out=rope_t[1][:], in0=b, scalar=sc, in1=sn,
                    op0=AluOpType.mult, op1=AluOpType.mult))
                prog.drain("vector")
                E("vector", lambda e, d=d1: e.tensor_tensor(
                    out=d, in0=rope_t[0][:], in1=rope_t[1][:], op=AluOpType.subtract))
                E("vector", lambda e, a=x1, sn=sinT, sc=sc: e.scalar_tensor_tensor(
                    out=rope_t[0][:], in0=a, scalar=sc, in1=sn,
                    op0=AluOpType.mult, op1=AluOpType.mult))
                E("vector", lambda e, b=x2, c=cosT, sc=sc: e.scalar_tensor_tensor(
                    out=rope_t[1][:], in0=b, scalar=sc, in1=c,
                    op0=AluOpType.mult, op1=AluOpType.mult))
                prog.drain("vector")
                E("vector", lambda e, d=d2: e.tensor_tensor(
                    out=d, in0=rope_t[0][:], in1=rope_t[1][:], op=AluOpType.add),
                  inc="vS" if j == 4 else None)
            rope_tt_done = prog.cnt["vS"]
        rope_done = prog.cnt["vS"]
        for tc in range(16):
            vslot = ps_b[:, (tc % 3) * 512:(tc % 3) * 512 + 256]
            if tc >= 3:
                W("tensor", "aS", vev_cnt[-3])
            for ifc in range(8):
                E("tensor", lambda e, sl=vslot, i=ifc, s_=s, t_=tc: e.matmul(
                    out=sl[:],
                    lhsT=xnT[:, i * NTOK + s_ * T + t_ * 128: i * NTOK + s_ * T + t_ * 128 + 128],
                    rhs=wvT_sb[:, i * 256:(i + 1) * 256],
                    start=(i == 0), stop=(i == 7)), inc="tS" if ifc == 7 else None)
            W("scalar", "tS", prog.cnt["tS"])
            ec = E("scalar", lambda e, sl=vslot, t_=tc: e.copy(
                out=v_tm[:, t_ * 256:(t_ + 1) * 256], in_=sl[:]), inc="aS")
            vev_cnt.append(ec)
        v_done = prog.drain("scalar", inc="aS")

        if phases >= 3:
            _emit_attention((prog, s, q_rope, k_rope, v_tm, probs_tm, probs_T,
                             o_sb, o_park, mask_main, mask_first, ident_bf,
                             ps_a, ps_b, S, C_MPACK, C_DPACK, C_RDPACK,
                             rope_done, v_done, attn_st))

    p23_tS = prog.cnt["tS"]
    p23_vS = prog.cnt["vS"]
    p23_aS = prog.cnt["aS"]
    if "q" in dbg:
        W("gpsimd", "vS", p23_vS)
        DMA("gpsimd", dbg["q"], q_rope[:])
        DMA("gpsimd", dbg["k"], k_rope[:])
        W("gpsimd", "aS", p23_aS)
        DMA("gpsimd", dbg["v"], v_tm[:])

    snap("smB")
    W("vector", "vS", s1_done)
    W("vector", "tS", p23_tS)
    W("sync", "tS", p23_tS)
    q_ev = 0
    q_tp = []
    q_nv = []
    for kch in range(NCHUNK):
        slot = xbuf[kch % 2]
        nslot = xn_tm[kch % 2]
        if kch >= 2:
            W("sync", "vS", q_nv[kch - 2])
        cd = prog.dma_handoff("sync", slot[:], x[kch * 128:(kch + 1) * 128, :])
        prog.hwait("vector", cd)
        if kch >= 2:
            W("vector", "tS", q_tp[kch - 2])
        E("vector", lambda e, s_=slot, k=kch: e.tensor_scalar(
            out=xq_int[:, 0:D], in0=s_[:], scalar1=S[:, C_MEANP + k:C_MEANP + k + 1],
            scalar2=S[:, C_SUMP + k:C_SUMP + k + 1],
            op0=AluOpType.subtract, op1=AluOpType.mult))
        nv = E("vector", lambda e, n=nslot: e.tensor_scalar(
            out=n[:], in0=xq_int[:, 0:D], scalar1=-128.0, scalar2=127.0,
            op0=AluOpType.max, op1=AluOpType.min), inc="vS")
        q_nv.append(nv)
        W("tensor", "vS", nv)
        if kch >= 1:
            W("tensor", "aS", q_ev)
        for t8 in range(8):
            E("tensor", lambda e, n=nslot, t=t8: e.transpose(
                out=pst[:, t * 128:(t + 1) * 128], in_=n[:, t * 128:(t + 1) * 128],
                identity=ident_bf[:]), inc="tS")
        q_tp.append(prog.cnt["tS"])
        W("scalar", "tS", prog.cnt["tS"])
        for t8 in range(8):
            E("scalar", lambda e, t=t8, k=kch: e.copy(
                out=xnT[:, t * NTOK + k * 128: t * NTOK + (k + 1) * 128],
                in_=pst[:, t * 128:(t + 1) * 128]), inc="aS")
        q_ev = prog.cnt["aS"]
    xq_done = prog.drain("vector", inc="vS")
    xq_evac_done = prog.drain("scalar", inc="aS")
    if "xq" in dbg:
        W("gpsimd", "vS", xq_done)
        DMA("gpsimd", dbg["xq"], xnT[:])

    if phases < 5:
        finish()
        return nc

    C_GM2P = 280
    p3_tS = prog.cnt["tS"]
    p3_aS = prog.cnt["aS"]
    wg_slab_done = []
    spill_done = [0, 0]
    stt_u = []
    o_dma_last = max(attn_st["o_dma"])
    for ic in range(NIC):
        sl = ic % 2
        if ic == 0:
            W("gpsimd", "dG", o_dma_last)
        if ic >= 2:
            W("gpsimd", "tS", wg_slab_done[ic - 2])
        lg = DMA("gpsimd", wgi_sb[sl][:],
                 bass.AP(wg_park.tensor, ic * 128, [[DFF, 128], [128 * DFF, 8], [1, 128]]))
        lu = prog.dma_handoff("gpsimd", wui_sb[sl][:],
                 bass.AP(wu_park.tensor, ic * 128, [[DFF, 128], [128 * DFF, 8], [1, 128]]))
        prog.hwait("tensor", lu)
        if ic == 0:
            W("tensor", "vS", xq_done)
            W("tensor", "aS", xq_evac_done)
        for tt in range(8):
            pg = ps_a[:, (tt % 2) * 512:(tt % 2) * 512 + 512]
            pu = ps_a[:, 1024 + (tt % 2) * 512:1024 + (tt % 2) * 512 + 512]
            if len(stt_u) >= 2:
                W("tensor", "vS", stt_u[-2])
            for ifc in range(8):
                E("tensor", lambda e, p=pg, w=wgi_sb[sl], i=ifc, t=tt: e.matmul(
                    out=p[:], lhsT=w[:, i * 128:(i + 1) * 128],
                    rhs=xnT[:, i * NTOK + t * 512:i * NTOK + (t + 1) * 512],
                    start=(i == 0), stop=(i == 7)), inc="tS" if ifc == 7 else None)
            gm = prog.cnt["tS"]
            for ifc in range(8):
                E("tensor", lambda e, p=pu, w=wui_sb[sl], i=ifc, t=tt: e.matmul(
                    out=p[:], lhsT=w[:, i * 128:(i + 1) * 128],
                    rhs=xnT[:, i * NTOK + t * 512:i * NTOK + (t + 1) * 512],
                    start=(i == 0), stop=(i == 7)), inc="tS" if ifc == 7 else None)
            um = prog.cnt["tS"]
            W("vector", "tS", gm)
            if ic >= 1 and tt == 0:
                W("vector", "dG", spill_done[(ic + 1) % 2])
            E("vector", lambda e, p=pg, t=tt, ic=ic: e.tensor_scalar(
                out=gate_f32[:, t * 512:(t + 1) * 512], in0=p[:],
                scalar1=S[:, C_CBG + ic:C_CBG + ic + 1], scalar2=None,
                op0=AluOpType.mult))
            ev = prog.drain("vector", inc="vS")
            W("scalar", "vS", ev)
            E("scalar", lambda e, t=tt: e.activation(
                out=gup_f32[:, t * 512:(t + 1) * 512],
                in_=gate_f32[:, t * 512:(t + 1) * 512], func=ACT.Silu))
            sv2 = prog.drain("scalar", inc="aS")
            W("vector", "aS", sv2)
            W("vector", "tS", um)
            sg = E("vector", lambda e, p=pu, t=tt, ic=ic: e.scalar_tensor_tensor(
                out=gate_f32[:, t * 512:(t + 1) * 512], in0=p[:],
                scalar=S[:, C_CBU + ic:C_CBU + ic + 1],
                in1=gup_f32[:, t * 512:(t + 1) * 512],
                op0=AluOpType.mult, op1=AluOpType.mult), inc="vS")
            stt_u.append(sg)
        wg_slab_done.append(prog.cnt["tS"])
        prog.drain("vector")
        am = E("vector", lambda e, ic=ic: e.tensor_reduce(
            out=S[:, C_GM2P + ic:C_GM2P + ic + 1], in_=gate_f32[:], axis=AX,
            op=AluOpType.max, apply_absolute_value=True), inc="vS")
        W("gpsimd", "vS", am)
        if ic == NIC - 1:
            spill_h = prog.dma_handoff("gpsimd", g_spill[ic * 128:(ic + 1) * 128, :], gate_f32[:])
            spill_done[ic % 2] = prog.cnt["dG"]
        else:
            spill_done[ic % 2] = DMA("gpsimd", g_spill[ic * 128:(ic + 1) * 128, :], gate_f32[:])

    snap("smC")
    p5_vflush = prog.drain("vector", inc="vS")
    prog.drain("vector")
    E("vector", lambda e: e.tensor_reduce(
        out=Sc(C_GMAX2), in_=S[:, C_GM2P:C_GM2P + NIC], axis=AX, op=AluOpType.max))
    g2 = prog.drain("vector", inc="vS")
    W("gpsimd", "vS", g2)
    cg1 = DMA("gpsimd", cc2_in[0:1, 0:128], Sc(C_GMAX2))
    W("gpsimd", "dG", cg1)
    ccd2 = E("gpsimd", lambda e: e.collective_compute(
        "AllReduce", AluOpType.max, replica_groups=[list(range(NCORES))],
        ins=[cc2_in[0:1, 0:128].opt()], outs=[cc2_out[0:1, 0:128].opt()]), inc="gS")
    W("gpsimd", "gS", ccd2)
    cg2 = prog.dma_handoff("gpsimd", S[:, C_BCR:C_BCR + 128],
                           bass.AP(cc2_out.tensor, 0, [[0, 128], [1, 128]]))
    if "cc2i" in dbg:
        DMA("gpsimd", dbg["cc2i"], cc2_in)
        DMA("gpsimd", dbg["cc2o"], cc2_out)
    prog.hwait("vector", cg2)
    E("vector", lambda e: e.tensor_reduce(
        out=Sc(C_S2), in_=S[:, C_BCR:C_BCR + 128], axis=AX, op=AluOpType.max))
    prog.drain("vector")
    E("vector", lambda e: e.tensor_tensor(
        out=Sc(C_S2), in0=Sc(C_S2), in1=Sc(C_SIGD), op=AluOpType.mult))
    prog.drain("vector")
    E("vector", lambda e: e.tensor_scalar(
        out=Sc(C_S2), in0=Sc(C_S2), scalar1=1.0 / QMAX, scalar2=None, op0=AluOpType.mult))
    prog.drain("vector")
    E("vector", lambda e: e.reciprocal(out=Sc(C_RS2), in_=Sc(C_S2)))
    prog.drain("vector")
    E("vector", lambda e: e.tensor_scalar(
        out=S[:, C_CBD:C_CBD + 8], in0=S[:, C_SWD:C_SWD + 8],
        scalar1=Sc(C_S2), scalar2=None, op0=AluOpType.mult))
    s2_done = prog.drain("vector", inc="vS")

    W("gpsimd", "tS", wg_slab_done[-1])
    for c in range(0, 7):
        DMA("gpsimd", w1T_sb[:, c * 4096:(c + 1) * 4096], din["w1T"][c * 128:(c + 1) * 128, :])
    w1_loaded = prog.dma_handoff("gpsimd", w1T_sb[:, 7 * 4096:8 * 4096], din["w1T"][7 * 128:8 * 128, :])
    W("gpsimd", "vS", xq_done)
    for c in range(32):
        DMA("gpsimd", w2T_sb[:, c * 1024:(c + 1) * 1024], din["w2T"][c * 128:(c + 1) * 128, :])
    w2_loaded = prog.dma_handoff("gpsimd", w2T_sb[:, 0:16], din["w2T"][0:128, 0:16])

    o_ld = [0, 0]
    ab_st = [0, 0]
    mlp_of2_done = 0
    h1_done = 0
    W("sync", "dG", o_dma_last)
    W("sync", "vS", p5_vflush)
    prog.hwait("sync", spill_h)
    for t8 in range(8):
        sl = t8 % 2
        if t8 >= 2:
            W("sync", "tS", o_ld[sl])
        od = prog.dma_handoff("sync", o_sb2[sl][:],
                 bass.AP(o_park.tensor, t8 * 512, [[NTOK, 128], [128 * NTOK, 8], [1, 512]]))
        prog.hwait("tensor", od)
        if t8 == 0:
            prog.hwait("tensor", w1_loaded)
            W("tensor", "vS", p5_vflush)
            prog.hwait("tensor", spill_h)
        if t8 >= 1:
            W("tensor", "aS", mlp_of2_done)
        silus = []
        for of1 in range(32):
            p = ps_a[:, (of1 % 4) * 512:(of1 % 4) * 512 + 512]
            if of1 >= 4:
                W("tensor", "aS", silus[of1 - 4])
            for ifc in range(8):
                w1src = w1T_sb
                E("tensor", lambda e, p=p, c=of1, i=ifc, sl=sl, w=w1src: e.matmul(
                    out=p[:], lhsT=w[:, i * 4096 + c * 128:i * 4096 + (c + 1) * 128],
                    rhs=o_sb2[sl][:, i * 512:(i + 1) * 512],
                    start=(i == 0), stop=(i == 7)), inc="tS" if ifc == 7 else None)
            W("scalar", "tS", prog.cnt["tS"])
            sv2 = E("scalar", lambda e, p=p, c=of1: e.activation(
                out=h1_sb[:, c * 512:(c + 1) * 512], in_=p[:], func=ACT.Silu), inc="aS")
            silus.append(sv2)
        o_ld[sl] = prog.cnt["tS"]
        h1_land = prog.drain("scalar", inc="aS")
        W("tensor", "aS", h1_land)
        if t8 == 0:
            prog.hwait("tensor", w2_loaded)
        ab_ev = []
        for of2 in range(8):
            p = ps_b[:, (of2 % 3) * 512:(of2 % 3) * 512 + 512]
            if of2 >= 3:
                W("tensor", "aS", ab_ev[of2 - 3])
            for c in range(32):
                E("tensor", lambda e, p=p, c=c, o=of2: e.matmul(
                    out=p[:], lhsT=w2T_sb[:, c * 1024 + o * 128:c * 1024 + (o + 1) * 128],
                    rhs=h1_sb[:, c * 512:(c + 1) * 512],
                    start=(c == 0), stop=(c == 31)), inc="tS" if c == 31 else None)
            W("scalar", "tS", prog.cnt["tS"])
            asl = of2 % 2
            if ab_st[asl]:
                W("scalar", "dA", ab_st[asl])
            E("scalar", lambda e, p=p, asl=asl: e.copy(
                out=ab_evac[asl][:], in_=p[:]))
            ae = prog.drain("scalar", inc="aS")
            ab_ev.append(ae)
            ab_st[asl] = prog.dma("scalar",
                bass.AP(abuf.tensor, of2 * 128 * NTOK + t8 * 512, [[NTOK, 128], [1, 512]]),
                ab_evac[asl][:])
        mlp_of2_done = prog.cnt["aS"]
        h1_done = prog.cnt["tS"]

    if phases < 6:
        finish()
        return nc

    W("sync", "dG", prog.cnt["dG"])
    W("sync", "tS", h1_done)
    for c in range(NIC - 1):
        DMA("sync", wdT_sb[:, c * 1024:(c + 1) * 1024], wd_park[c * 128:(c + 1) * 128, :])
    wd_loaded = prog.dma_handoff("sync", wdT_sb[:, (NIC - 1) * 1024:NIC * 1024],
                                 wd_park[(NIC - 1) * 128:NIC * 128, :])
    W("vector", "vS", s2_done)

    q_done = [[0] * NIC, [0] * NIC]
    x_ld = [0] * 4
    ab_ld = [0, 0]
    merge_done = [0, 0]
    out_dma = [0, 0, 0, 0]
    dps_ev = []
    dps_done = [0, 0]
    hbufs = [h_int, h_int2]

    def emit_hquant(t8, ic0, ic1):
        hb = hbufs[t8 % 2]
        for ic in range(ic0, min(ic1, NIC)):
            sl = ic % 2
            prev = q_done[t8 % 2][ic - 2] if ic >= 2 else q_done[(t8 - 1) % 2][NIC - 2 + ic]
            if prev:
                W("sync", "vS", prev)
            gl = prog.dma_handoff("sync", g_slab[sl][:],
                     bass.AP(g_spill.tensor, ic * 128 * NTOK + t8 * 512, [[NTOK, 128], [1, 512]]))
            prog.hwait("vector", gl)
            if ic == 0 and dps_done[t8 % 2]:
                W("vector", "tS", dps_done[t8 % 2])
            E("vector", lambda e, sl=sl: e.tensor_scalar(
                out=qint6[:], in0=g_slab[sl][:], scalar1=Sc(C_RS2),
                scalar2=None, op0=AluOpType.mult))
            E("vector", lambda e, ic=ic, hb=hb: e.tensor_scalar(
                out=hb[:, ic * 512:(ic + 1) * 512], in0=qint6[:], scalar1=-128.0,
                scalar2=127.0, op0=AluOpType.max, op1=AluOpType.min))
            q_done[t8 % 2][ic] = prog.drain("vector", inc="vS")

    emit_hquant(0, 0, NIC)
    for t8 in range(8):
        hb = hbufs[t8 % 2]
        hq_all = q_done[t8 % 2][NIC - 1]
        for tc in range(4):
            if out_dma[tc]:
                W("sync", "dG", out_dma[tc])
            x_ld[tc] = prog.dma_handoff("sync", x_sb[tc][:],
                           x[(t8 * 4 + tc) * 128:(t8 * 4 + tc + 1) * 128, :])
        W("tensor", "vS", hq_all)
        if t8 == 0:
            prog.hwait("tensor", wd_loaded)
        tadd = [0] * 8
        for of in range(8):
            if t8 < 7:
                emit_hquant(t8 + 1, of * 3, of * 3 + 3)
            dps = ps_a[:, (of % 2) * 512:(of % 2) * 512 + 512]
            if len(dps_ev) >= 2:
                W("tensor", "vS", dps_ev[-2])
            for ic in range(NIC):
                E("tensor", lambda e, p=dps, ic=ic, o=of, hb=hb: e.matmul(
                    out=p[:], lhsT=wdT_sb[:, ic * 1024 + o * 128:ic * 1024 + (o + 1) * 128],
                    rhs=hb[:, ic * 512:(ic + 1) * 512],
                    start=(ic == 0), stop=(ic == NIC - 1)), inc="tS" if ic == NIC - 1 else None)
            dmm = prog.cnt["tS"]
            asl = of % 2
            if ab_ld[asl]:
                W("scalar", "vS", merge_done[asl])
            ab_ld[asl] = prog.dma_handoff("scalar", ab_slab[asl][:],
                bass.AP(abuf.tensor, of * 128 * NTOK + t8 * 512, [[NTOK, 128], [1, 512]]))
            W("vector", "tS", dmm)
            prog.hwait("vector", ab_ld[asl])
            E("vector", lambda e, p=dps, o=of, asl=asl: e.scalar_tensor_tensor(
                out=sum_sb[asl][:], in0=p[:], scalar=S[:, C_CBD + o:C_CBD + o + 1],
                in1=ab_slab[asl][:], op0=AluOpType.mult, op1=AluOpType.add))
            mg = prog.drain("vector", inc="vS")
            dps_ev.append(mg)
            merge_done[asl] = mg
            W("tensor", "vS", mg)
            tbank = ps_b[:, (of % 2) * 512:(of % 2) * 512 + 512]
            if of >= 2:
                W("tensor", "vS", tadd[of - 2])
            for tc in range(4):
                E("tensor", lambda e, tb=tbank, tc=tc, asl=asl: e.transpose(
                    out=tb[:, tc * 128:(tc + 1) * 128],
                    in_=sum_sb[asl][:, tc * 128:(tc + 1) * 128],
                    identity=ident_f32[:]), inc="tS" if tc == 3 else None)
            ttp = prog.cnt["tS"]
            W("vector", "tS", ttp)
            for tc in range(4):
                prog.hwait("vector", x_ld[tc])
                E("vector", lambda e, tb=tbank, tc=tc, o=of: e.tensor_tensor(
                    out=out_sb[tc][:, o * 128:(o + 1) * 128],
                    in0=tb[:, tc * 128:(tc + 1) * 128],
                    in1=x_sb[tc][:, o * 128:(o + 1) * 128], op=AluOpType.add),
                    inc="vS" if tc == 3 else None)
            tadd[of] = prog.cnt["vS"]
        dps_done[t8 % 2] = prog.cnt["tS"]
        all_adds = prog.drain("vector", inc="vS")
        W("gpsimd", "vS", all_adds)
        for tc in range(4):
            out_dma[tc] = DMA("gpsimd", y[(t8 * 4 + tc) * 128:(t8 * 4 + tc + 1) * 128, :],
                              out_sb[tc][:])

    finish()
    return nc


def _emit_attention(ctx):
    (prog, s, q_rope, k_rope, v_tm, probs_tm, probs_T, o_sb, o_park,
     mask_main, mask_first, ident_bf, ps_a, ps_b, S,
     C_MPACK, C_DPACK, C_RDPACK, rope_done, v_done, st) = ctx
    E, W, DMA = prog.emit, prog.wait, prog.dma
    PS = [ps_a, ps_b]
    PSB = [ps_a.bitcast(mybir.dt.bfloat16), ps_b.bitcast(mybir.dt.bfloat16)]
    T_ = T
    NK = 32

    def half_info(k):
        return k // 2, (k % 2) * 8

    for k in range(NK + 2):
        if k < NK:
            qc, h0 = half_info(k)
            ps = PS[k % 2]
            W("tensor", "vS", rope_done)
            if k >= 3:
                W("tensor", "aS", st["oev"][k - 3])
            if k >= 2:
                W("tensor", "aS", st["exp"][k - 2])
            else:
                W("tensor", "aS", v_done)
            for ih in range(8):
                h = h0 + ih
                j, g = h % 4, h // 4
                tile = ps[:, ih * 256:(ih + 1) * 256]
                qx1 = q_rope[32 * g:32 * g + 32, j * T_ + qc * 128: j * T_ + (qc + 1) * 128]
                qx2 = q_rope[32 * g:32 * g + 32, (4 + j) * T_ + qc * 128: (4 + j) * T_ + (qc + 1) * 128]
                if qc == 0:
                    kx1 = k_rope[32 * g:32 * g + 32, 0:128]
                    kx2 = k_rope[32 * g:32 * g + 32, T_:T_ + 128]
                    out = tile[:, 128:256]
                    mask = mask_first
                else:
                    kx1 = k_rope[32 * g:32 * g + 32, (qc - 1) * 128:(qc + 1) * 128]
                    kx2 = k_rope[32 * g:32 * g + 32, T_ + (qc - 1) * 128:T_ + (qc + 1) * 128]
                    out = tile
                    mask = mask_main
                E("tensor", lambda e, o=out, q=qx1, kk=kx1, g=g: e.matmul(
                    out=o, lhsT=q, rhs=kk, start=True, stop=False, tile_position=(32 * g, 0)))
                E("tensor", lambda e, o=out, q=qx2, kk=kx2, g=g: e.matmul(
                    out=o, lhsT=q, rhs=kk, start=False, stop=False, tile_position=(32 * g, 0)))
                E("tensor", lambda e, t=tile, m=mask: e.matmul(
                    out=t, lhsT=ident_bf[:], rhs=m[:], start=False, stop=True,
                    tile_position=(0, 0)), inc="tS" if ih == 7 else None)
            st["stats"].append(prog.cnt["tS"])
            W("vector", "tS", st["stats"][k])
            mp = C_MPACK + 8 * (k % 2)
            dp = C_DPACK + 8 * (k % 2)
            rp = C_RDPACK + 8 * (k % 2)
            for ih in range(8):
                E("vector", lambda e, t=ps[:, ih * 256:(ih + 1) * 256], c=mp + ih: e.tensor_reduce(
                    out=S[:, c:c + 1], in_=t, axis=AX, op=AluOpType.max, negate=True))
            st["red"].append(prog.drain("vector", inc="vS"))
            W("scalar", "vS", st["red"][k])
            if k >= 2:
                prog.hwait("scalar", st["tp"][k - 2][0])
                prog.hwait("scalar", st["tp"][k - 2][1])
            for ih in range(8):
                E("scalar", lambda e, t=ps[:, ih * 256:(ih + 1) * 256], ih=ih, c=mp + ih,
                  d=dp + ih, k=k: e.activation(
                    out=probs_tm[:, (k % 2) * 2048 + ih * 256:(k % 2) * 2048 + (ih + 1) * 256],
                    in_=t, func=ACT.Exp, bias=S[:, c:c + 1], accum_out=S[:, d:d + 1]))
            st["exp"].append(prog.drain("scalar", inc="aS"))
            W("vector", "aS", st["exp"][k])
            E("vector", lambda e, rp=rp, dp=dp: e.reciprocal(
                out=S[:, rp:rp + 8], in_=S[:, dp:dp + 8]))
            prog.drain("vector")
            lo = 128 if qc == 0 else 0
            for ih in range(8):
                E("vector", lambda e, ih=ih, k=k, lo=lo, rp=rp: e.tensor_scalar(
                    out=probs_tm[:, (k % 2) * 2048 + ih * 256 + lo:(k % 2) * 2048 + (ih + 1) * 256],
                    in0=probs_tm[:, (k % 2) * 2048 + ih * 256 + lo:(k % 2) * 2048 + (ih + 1) * 256],
                    scalar1=S[:, rp + ih:rp + ih + 1], scalar2=None, op0=AluOpType.mult))
            st["norm"].append(prog.drain("vector", inc="vS"))

        if k >= 2:
            j = k - 2
            qc, h0 = half_info(j)
            ava = PS[(j + 1) % 2]
            W("tensor", "aS", st["exp"][min(j + 1, NK - 1)])
            prog.hwait("tensor", st["tp"][j][0])
            prog.hwait("tensor", st["tp"][j][1])
            W("tensor", "aS", v_done)
            for ih in range(8):
                h = h0 + ih
                g = h // 4
                p = ih // 2
                cg = 64 * (h % 2)
                out = ava[cg:cg + 64, 1024 + p * 128:1024 + (p + 1) * 128]
                if qc == 0:
                    E("tensor", lambda e, o=out, g=g, cg=cg, ih=ih, j=j: e.matmul(
                        out=o, lhsT=v_tm[:, 64 * g: 64 * g + 64],
                        rhs=probs_T[:, (j % 2) * 2048 + (2 * ih + 1) * 128:(j % 2) * 2048 + (2 * ih + 2) * 128],
                        start=True, stop=True, tile_position=(0, cg)), inc="tS" if ih == 7 else None)
                else:
                    E("tensor", lambda e, o=out, g=g, cg=cg, ih=ih, j=j, qc=qc: e.matmul(
                        out=o, lhsT=v_tm[:, (qc - 1) * 256 + 64 * g: (qc - 1) * 256 + 64 * g + 64],
                        rhs=probs_T[:, (j % 2) * 2048 + 2 * ih * 128:(j % 2) * 2048 + (2 * ih + 1) * 128],
                        start=True, stop=False, tile_position=(0, cg)))
                    E("tensor", lambda e, o=out, g=g, cg=cg, ih=ih, j=j, qc=qc: e.matmul(
                        out=o, lhsT=v_tm[:, qc * 256 + 64 * g: qc * 256 + 64 * g + 64],
                        rhs=probs_T[:, (j % 2) * 2048 + (2 * ih + 1) * 128:(j % 2) * 2048 + (2 * ih + 2) * 128],
                        start=False, stop=True, tile_position=(0, cg)), inc="tS" if ih == 7 else None)
            st["av"].append(prog.cnt["tS"])
            W("scalar", "tS", st["av"][j])
            slot = o_sb[qc % 2]
            if j % 2 == 0 and st["o_dma"][qc % 2]:
                W("scalar", "dG", st["o_dma"][qc % 2])
            for p in range(4):
                pa = 4 * (j % 2) + p
                E("scalar", lambda e, pa=pa, p=p, sl=slot, ava=ava: e.copy(
                    out=sl[:, pa * 128:(pa + 1) * 128],
                    in_=ava[:, 1024 + p * 128:1024 + (p + 1) * 128]))
            st["oev"].append(prog.drain("scalar", inc="aS"))
            if j % 2 == 1:
                W("gpsimd", "aS", st["oev"][j])
                st["o_dma"][qc % 2] = DMA(
                    "gpsimd",
                    bass.AP(o_park.tensor, s * T_ + qc * 128,
                            [[NTOK, 128], [128 * NTOK, 8], [1, 128]]),
                    slot[:])
        if k >= 1 and k - 1 < NK:
            kb = k - 1
            qc, h0 = half_info(kb)
            sides = (1,) if qc == 0 else (0, 1)
            for eng in ("sync", "scalar"):
                W(eng, "vS", st["norm"][kb])
                if kb >= 2:
                    W(eng, "tS", st["av"][kb - 2])
            i = 0
            for ih in range(8):
                for side in sides:
                    ti = 2 * ih + side
                    eng = "sync" if i % 2 == 0 else "scalar"
                    DMA(eng,
                        probs_T[:, (kb % 2) * 2048 + ti * 128:(kb % 2) * 2048 + (ti + 1) * 128],
                        probs_tm[:, (kb % 2) * 2048 + ih * 256 + side * 128:
                                 (kb % 2) * 2048 + ih * 256 + side * 128 + 128],
                        transpose=True)
                    i += 1
            hs1 = prog.queue_handshake("sync")
            hs2 = prog.queue_handshake("scalar")
            st["tp"].append((hs1, hs2))
    st["last_oev"] = st["oev"][-1]
    st["last_stats"] = st["stats"][-1]
    for key in ("stats", "red", "exp", "norm", "tp", "ev", "av", "oev"):
        st[key] = []



def kernel(**inputs):
    xfull = np.asarray(inputs["L0_attn_query"], np.float32)
    prep = host_prep(inputs)
    nc = build_nc(phases=6)
    in_maps = []
    for c in range(NCORES):
        m = {"x": np.ascontiguousarray(xfull[2 * c:2 * c + 2].reshape(NTOK, D))}
        m.update(prep)
        in_maps.append(m)
    res = run_bass_kernel_spmd(nc, in_maps, core_ids=list(range(NCORES)))
    return np.concatenate([r["y"].reshape(2, T, D) for r in res.results], axis=0)
